# version 11 (final)
# baseline (speedup 1.0000x reference)
import numpy as np
import ml_dtypes
from contextlib import ExitStack

import concourse.bass as bass
import concourse.mybir as mybir
from concourse.bass_utils import run_bass_kernel_spmd

F32 = mybir.dt.float32
BF16 = mybir.dt.bfloat16
AF = mybir.ActivationFunctionType
ALU = mybir.AluOpType
AX = mybir.AxisListType

D = 1024
S = 4096
H = 16
DK = 64
NFT = D // 128
NTT = S // 128

ALL_SUBLAYERS = ["ffn0a", "attn0", "ffn0b", "ffn1a", "attn1", "ffn1b"]

CAST_ONLY = {"L0_attn_Wq", "L0_attn_Wk", "L0_attn_Wv", "L0_attn_Wo"}
TRANSPOSE_2D = {
    "L1_attn_Wq_w": (D, D), "L1_attn_Wk_w": (D, D), "L1_attn_Wv_w": (D, D),
    "L1_attn_out_w": (D, D),
    "L1_attn_qproj_w": (D, 2 * D), "L1_attn_kproj_w": (D, 2 * D),
    "L1_ffn_w1_w": (4 * D, D), "L1_ffn_w2_w": (4 * D, D),
    "L1_ffn_h1_w": (4 * D, 4 * D), "L1_ffn_h2_w": (4 * D, 4 * D),
    "L1_ffn_w3_w": (D, 4 * D),
    "L0_ffn_sg": (512, D), "L0_ffn_su": (512, D), "L0_ffn_sd": (D, 512),
}
TRANSPOSE_3D = {
    "L0_ffn_wg": (8, 512, D), "L0_ffn_wu": (8, 512, D), "L0_ffn_wd": (8, D, 512),
}
SMALL_T = {"L0_ffn_gate_w": (8, D), "L1_attn_tau2_w": (H, 2 * D), "L1_attn_del2_w": (H, 2 * D)}
CONV_W = {"L1_attn_convq_w", "L1_attn_convk_w"}

SUBLAYER_WEIGHTS = {
    "ffn0a": ["L0_ffn_gate_w", "L0_ffn_wg", "L0_ffn_wu", "L0_ffn_wd",
              "L0_ffn_sg", "L0_ffn_su", "L0_ffn_sd"],
    "attn0": ["L0_attn_Wq", "L0_attn_bq", "L0_attn_Wk", "L0_attn_bk",
              "L0_attn_Wv", "L0_attn_bv", "L0_attn_Wo", "L0_attn_bo"],
    "ffn1a": ["L1_ffn_w1_w", "L1_ffn_w1_s", "L1_ffn_w2_w", "L1_ffn_w2_s",
              "L1_ffn_h1_w", "L1_ffn_h1_s", "L1_ffn_h2_w", "L1_ffn_h2_s",
              "L1_ffn_w3_w", "L1_ffn_w3_s"],
    "attn1": ["L1_attn_Wq_w", "L1_attn_Wq_b", "L1_attn_Wk_w", "L1_attn_Wk_b",
              "L1_attn_Wv_w", "L1_attn_Wv_b", "L1_attn_convq_w", "L1_attn_convq_b",
              "L1_attn_convk_w", "L1_attn_convk_b", "L1_attn_qproj_w", "L1_attn_qproj_b",
              "L1_attn_kproj_w", "L1_attn_kproj_b", "L1_attn_tau1_w", "L1_attn_tau1_b",
              "L1_attn_tau2_w", "L1_attn_tau2_b", "L1_attn_del1_w", "L1_attn_del1_b",
              "L1_attn_del2_w", "L1_attn_del2_b", "L1_attn_out_w", "L1_attn_out_b"],
}
SUBLAYER_WEIGHTS["ffn0b"] = SUBLAYER_WEIGHTS["ffn0a"]
SUBLAYER_WEIGHTS["ffn1b"] = SUBLAYER_WEIGHTS["ffn1a"]

TILED = {"L1_ffn_w1_w", "L1_ffn_w2_w", "L1_ffn_h1_w", "L1_ffn_h2_w", "L1_ffn_w3_w",
         "L0_ffn_sg", "L0_ffn_su", "L0_ffn_sd", "L0_ffn_wg", "L0_ffn_wu", "L0_ffn_wd"}

INPUT_SPECS = {
    "L0_attn_Wq": (D, D), "L0_attn_bq": (D,), "L0_attn_Wk": (D, D), "L0_attn_bk": (D,),
    "L0_attn_Wv": (D, D), "L0_attn_bv": (D,), "L0_attn_Wo": (D, D), "L0_attn_bo": (D,),
    "L0_ffn_gate_w": (8, D), "L0_ffn_wg": (8, 512, D), "L0_ffn_wu": (8, 512, D),
    "L0_ffn_wd": (8, D, 512), "L0_ffn_sg": (512, D), "L0_ffn_su": (512, D),
    "L0_ffn_sd": (D, 512),
    "L1_attn_Wq_w": (D, D), "L1_attn_Wq_b": (D,), "L1_attn_Wk_w": (D, D),
    "L1_attn_Wk_b": (D,), "L1_attn_Wv_w": (D, D), "L1_attn_Wv_b": (D,),
    "L1_attn_convq_w": (D, D, 3), "L1_attn_convq_b": (D,),
    "L1_attn_convk_w": (D, D, 3), "L1_attn_convk_b": (D,),
    "L1_attn_qproj_w": (D, 2 * D), "L1_attn_qproj_b": (D,),
    "L1_attn_kproj_w": (D, 2 * D), "L1_attn_kproj_b": (D,),
    "L1_attn_tau1_w": (2 * D, 1, 3), "L1_attn_tau1_b": (2 * D,),
    "L1_attn_tau2_w": (H, 2 * D, 1), "L1_attn_tau2_b": (H,),
    "L1_attn_del1_w": (2 * D, 1, 3), "L1_attn_del1_b": (2 * D,),
    "L1_attn_del2_w": (H, 2 * D, 1), "L1_attn_del2_b": (H,),
    "L1_attn_out_w": (D, D), "L1_attn_out_b": (D,),
    "L1_ffn_w1_w": (4 * D, D), "L1_ffn_w1_s": (4 * D,),
    "L1_ffn_w2_w": (4 * D, D), "L1_ffn_w2_s": (4 * D,),
    "L1_ffn_h1_w": (4 * D, 4 * D), "L1_ffn_h1_s": (4 * D,),
    "L1_ffn_h2_w": (4 * D, 4 * D), "L1_ffn_h2_s": (4 * D,),
    "L1_ffn_w3_w": (D, 4 * D), "L1_ffn_w3_s": (D,),
}

COLVECS = {
    "bq": "L0_attn_bq", "bk": "L0_attn_bk", "bv": "L0_attn_bv", "bo": "L0_attn_bo",
    "w1s": "L1_ffn_w1_s", "w2s": "L1_ffn_w2_s", "h1s": "L1_ffn_h1_s",
    "h2s": "L1_ffn_h2_s", "w3s": "L1_ffn_w3_s",
    "qb": "L1_attn_Wq_b", "kb": "L1_attn_Wk_b", "vb": "L1_attn_Wv_b",
    "cqb": "L1_attn_convq_b", "ckb": "L1_attn_convk_b",
    "qpb": "L1_attn_qproj_b", "kpb": "L1_attn_kproj_b", "ob": "L1_attn_out_b",
}


class Sem:
    __slots__ = ("h", "n")

    def __init__(self, h):
        self.h = h
        self.n = 0

    def inc(self, instr, k=1):
        instr.then_inc(self.h, k)
        self.n += k
        return self.n


class Prog:
    def __init__(self, sublayers=None):
        self.sublayers = list(ALL_SUBLAYERS if sublayers is None else sublayers)
        self.nc = bass.Bass("TRN2", target_bir_lowering=False)
        self.es = ExitStack()
        self._build()

    def sem(self, name):
        return Sem(self.es.enter_context(self.nc.semaphore(name)))

    def sb(self, st, name, shape, dtype):
        return st.enter_context(self.nc.sbuf_tensor(name + getattr(self, "scope_tag", ""), shape, dtype))

    def psum(self, st, name, shape, dtype=F32):
        return st.enter_context(self.nc.psum_tensor(name + getattr(self, "scope_tag", ""), shape, dtype))

    def tn(self, f): self.blk.tensor(f)
    def vc(self, f): self.blk.vector(f)
    def sc(self, f): self.blk.scalar(f)
    def gp(self, f): self.blk.gpsimd(f)
    def sy(self, f): self.blk.sync(f)

    def sem_dma(self, name):
        s = self.sem(name)
        self.dma_sems.append(s)
        return s

    def barrier(self):
        ph = self.ph
        ph.n += 5
        target = ph.n
        dma = [(s, s.n) for s in self.dma_sems if s.n]

        def mk(e):
            e.drain().then_inc(ph.h, 1)
            for s, c in dma:
                e.wait_ge(s.h, c)
            e.wait_ge(ph.h, target)

        def mks(e):
            e.nop().then_inc(ph.h, 1)
            for s, c in dma:
                e.wait_ge(s.h, c)
            e.wait_ge(ph.h, target)
        self.tn(mk); self.vc(mk); self.sc(mk); self.gp(mk); self.sy(mks)

    def _build(self):
        nc = self.nc
        used = set()
        for sl in self.sublayers:
            used.update(SUBLAYER_WEIGHTS[sl])
        self.used = used

        self.x0 = nc.dram_tensor("x0", [S, D], F32, kind="ExternalInput").ap()
        self.y = nc.dram_tensor("y", [S, D], F32, kind="ExternalOutput").ap()
        self.win = {}
        for name, shp in INPUT_SPECS.items():
            if name in used:
                self.win[name] = nc.dram_tensor(name, list(shp), F32, kind="ExternalInput").ap()

        self.xa = nc.dram_tensor("xa", [D, S], F32, kind="Internal").ap()
        self.xb = nc.dram_tensor("xb", [D, S], F32, kind="Internal").ap()
        self.wbf = {}
        self.wT = {}
        for name in sorted(used):
            shp = INPUT_SPECS[name]
            if name in CAST_ONLY:
                self.wbf[name] = nc.dram_tensor(name + "_bf", list(shp), BF16, kind="Internal").ap()
            elif name in TRANSPOSE_2D:
                R, C = TRANSPOSE_2D[name]
                self.wbf[name] = nc.dram_tensor(name + "_bf", [R, C], BF16, kind="Internal").ap()
                if name in TILED:
                    self.wT[name] = nc.dram_tensor(
                        name + "_T", [R // 128, 128, C // 128, 128], BF16, kind="Internal").ap()
                else:
                    self.wT[name] = nc.dram_tensor(name + "_T", [C, R], BF16, kind="Internal").ap()
            elif name in TRANSPOSE_3D:
                E, R, C = TRANSPOSE_3D[name]
                self.wbf[name] = nc.dram_tensor(name + "_bf", [E, R, C], BF16, kind="Internal").ap()
                if name in TILED:
                    self.wT[name] = nc.dram_tensor(
                        name + "_T", [E, R // 128, 128, C // 128, 128], BF16, kind="Internal").ap()
                else:
                    self.wT[name] = nc.dram_tensor(name + "_T", [E, C, R], BF16, kind="Internal").ap()
            elif name in SMALL_T:
                r, c = SMALL_T[name]
                self.wbf[name] = nc.dram_tensor(name + "_bf", [r, c], BF16, kind="Internal").ap()
                self.wT[name] = nc.dram_tensor(name + "_T", [c, r], BF16, kind="Internal").ap()
            elif name in CONV_W:
                self.wbf[name] = nc.dram_tensor(name + "_bf", [D, D * 3], BF16, kind="Internal").ap()
                self.wT[name] = nc.dram_tensor(name + "_T", [3, D, D], BF16, kind="Internal").ap()
        self.attn_sp = nc.dram_tensor("attn_sp", [D, S], BF16, kind="Internal").ap()
        self.q_sp = nc.dram_tensor("q_sp", [D, S], BF16, kind="Internal").ap()
        self.k_sp = nc.dram_tensor("k_sp", [D, S], BF16, kind="Internal").ap()
        self.d_sp = nc.dram_tensor("d_sp", [2, 16, S], BF16, kind="Internal").ap()

        self._inline_consts()

        self.dma_sems = []
        self.ld_g = self.sem_dma("ld_g")
        self.ld_s = self.sem_dma("ld_s")
        self.ld_c = self.sem_dma("ld_c")
        self.st_s = self.sem_dma("st_s")
        self.ld_cast = self.sem_dma("ld_cast")
        self.ld_x = self.sem_dma("ld_x")
        self.ld_w = [self.sem_dma(f"ld_w{i}") for i in range(2)]
        self.ld_t = [self.sem_dma(f"ld_t{i}") for i in range(2)]
        self.st_t = [self.sem_dma(f"st_t{i}") for i in range(2)]
        self.st_o = [self.sem_dma(f"st_o{i}") for i in range(2)]
        self.ld_c2 = [self.sem_dma(f"ld_c2{i}") for i in range(2)]
        self.ld_v2 = [self.sem_dma(f"ld_v2{i}") for i in range(2)]
        self.ld_y2 = [self.sem_dma(f"ld_y2{i}") for i in range(2)]
        self.s_mm = self.sem("s_mm")
        self.s_v = self.sem("s_v")
        self.s_sc = self.sem("s_sc")
        self.s_po = self.sem("s_po")
        self.ph = self.sem("phase")

        self.blk = self.es.enter_context(nc.Block())

        self._persist_alloc()
        self._prepass()
        self.barrier()

        EMIT = {"ffn0a": self.emit_ffn0, "ffn0b": self.emit_ffn0,
                "ffn1a": self.emit_ffn1, "ffn1b": self.emit_ffn1,
                "attn0": self.emit_attn0, "attn1": self.emit_attn1}
        cur, nxt = self.xa, self.xb
        for i, sl in enumerate(self.sublayers):
            self.scope_tag = f"_{i}"
            EMIT[sl](cur, nxt)
            self.scope_tag = ""
            self.barrier()
            cur, nxt = nxt, cur

        self._emit_output(cur)
        stn = self.st_s.n

        def fin(e):
            e.wait_ge(self.st_s.h, stn)
        self.tn(fin); self.vc(fin); self.sc(fin); self.gp(fin); self.sy(fin)

        self.es.close()

    def _inline_consts(self):
        nc = self.nc
        bf = ml_dtypes.bfloat16
        self.c = {}
        self.c["id_bf"] = nc.inline_tensor(np.eye(128, dtype=bf), "c_id_bf").ap()
        self.c["id_f32"] = nc.inline_tensor(np.eye(128, dtype=np.float32), "c_id_f32").ap()
        self.c["ones_col_bf"] = nc.inline_tensor(np.ones((128, 1), dtype=bf), "c_ones_col").ap()
        self.c["ones_row_f32"] = nc.inline_tensor(np.ones((1, 128), dtype=np.float32), "c_ones_row_f").ap()
        b2 = np.zeros((128, 2), dtype=bf); b2[0:64, 0] = 1; b2[64:128, 1] = 1
        self.c["head_ones"] = nc.inline_tensor(b2, "c_head_ones").ap()
        b2t = np.zeros((2, 128), dtype=bf); b2t[0, 0:64] = 1; b2t[1, 64:128] = 1
        self.c["head_bcast"] = nc.inline_tensor(b2t, "c_head_bcast").ap()
        self.c["head_bcast_f32"] = nc.inline_tensor(b2t.astype(np.float32), "c_head_bcast_f").ap()
        self.c["head_bcast_neg"] = nc.inline_tensor(-b2t.astype(np.float32), "c_head_bcast_n").ap()
        sp8 = np.zeros((16, 8 * 128), dtype=np.float32)
        for mt_ in range(8):
            sp8[2 * mt_, mt_ * 128:mt_ * 128 + 64] = 0.125
            sp8[2 * mt_ + 1, mt_ * 128 + 64:(mt_ + 1) * 128] = 0.125
        self.c["sel_pair_8th"] = nc.inline_tensor(sp8.astype(bf), "c_sel_pair_8").ap()
        s16 = np.zeros((16, 16 * 128), dtype=bf)
        for h in range(16):
            s16[h, h * 128:(h + 1) * 128] = 1
        self.c["sel16"] = nc.inline_tensor(s16, "c_sel16").ap()
        self.c["sel16_8th"] = nc.inline_tensor((s16.astype(np.float32) * 0.125).astype(bf),
                                               "c_sel16_8th").ap()
        s8 = np.zeros((8, 8 * 128), dtype=bf)
        for e in range(8):
            s8[e, e * 128:(e + 1) * 128] = 1
        self.c["sel8"] = nc.inline_tensor(s8, "c_sel8").ap()
        self.c["ones_row_8th"] = nc.inline_tensor(
            (np.ones((1, 128)) * 0.125).astype(np.float32), "c_ones_row_8th").ap()
        self.c["ones_64col_bf"] = nc.inline_tensor(
            np.concatenate([np.ones((1, 64), dtype=bf), np.zeros((1, 64), dtype=bf)], axis=1),
            "c_ones64").ap()
        self.FC = [1e-5, 1.0, -1.0, 0.5, -0.5, 1e-6, 1e-8, 1.1, 0.125, 2.0,
                   1e-16, float(np.float32(np.log(4.41)))]
        self.c["fc"] = nc.inline_tensor(
            np.tile(np.array(self.FC, dtype=np.float32), (128, 1)), "c_fc").ap()

    def _persist_alloc(self):
        es = self.es
        self.cols = {}
        for key, name in COLVECS.items():
            if name in self.used:
                ln = INPUT_SPECS[name][0]
                self.cols[key] = self.sb(es, "col_" + key, [128, (ln + 127) // 128], F32)
        if "L1_ffn_w3_s" in self.used:
            self.cols["w3sh"] = self.sb(es, "col_w3sh", [128, 8], F32)
        if "L1_attn_tau2_b" in self.used:
            self.cols["t2b"] = self.sb(es, "col_t2b", [16, 1], F32)
            self.cols["d2b"] = self.sb(es, "col_d2b", [16, 1], F32)
        if "L1_attn_tau1_w" in self.used:
            self.cols["t1w"] = self.sb(es, "col_t1w", [128, 48], F32)
            self.cols["d1w"] = self.sb(es, "col_d1w", [128, 48], F32)
            self.cols["t1b"] = self.sb(es, "col_t1b", [128, 16], F32)
            self.cols["d1b"] = self.sb(es, "col_d1b", [128, 16], F32)
        self.k = {}
        for nm, shape, dt in [("id_bf", [128, 128], BF16), ("id_f32", [128, 128], F32),
                              ("ones_col_bf", [128, 1], BF16), ("ones_row_f32", [1, 128], F32),
                              ("head_ones", [128, 2], BF16), ("head_bcast", [2, 128], BF16),
                              ("head_bcast_f32", [2, 128], F32),
                              ("head_bcast_neg", [2, 128], F32),
                              ("sel_pair_8th", [16, 8 * 128], BF16),
                              ("ones_row_8th", [1, 128], F32), ("ones_64col_bf", [1, 128], BF16),
                              ("fc", [128, len(self.FC)], F32)]:
            self.k[nm] = self.sb(es, "k_" + nm, shape, dt)
        for i, v in enumerate(self.FC):
            self.nc.const_aps.register(F32, v, self.k["fc"][:, i:i + 1])

    def _prepass(self):
        def g_casts(g):
            for name in sorted(self.wbf):
                src = self.win[name]
                if name in CONV_W:
                    src = src.rearrange("o i t -> o (i t)")
                elif name in ("L1_attn_tau2_w", "L1_attn_del2_w"):
                    src = src.rearrange("h c o -> h (c o)")
                dst = self.wbf[name]
                n0 = src.shape[0]
                inner = 1
                for d_ in src.shape[1:]:
                    inner *= d_
                rows_per = max(1, (8 << 20) // (inner * 4))
                r = 0
                while r < n0:
                    r2 = min(n0, r + rows_per)
                    self.ld_cast.inc(g.dma_start(out=dst[r:r2], in_=src[r:r2]), 16)
                    r = r2
        self.gp(g_casts)
        casts_done = self.ld_cast.n

        self._pp_consts()
        self.barrier()
        self._pp_x0()
        self.barrier()

        def wcasts(e):
            e.wait_ge(self.ld_cast.h, casts_done)
        self.sy(wcasts); self.sc(wcasts); self.tn(wcasts)
        self._pp_transposes()
        self.barrier()
        self._pp_pe_transposes()

    def _pp_consts(self):
        with ExitStack() as st:
            def loads(sy):
                for nm in self.k:
                    self.ld_s.inc(sy.dma_start(out=self.k[nm][:], in_=self.c[nm][:]), 16)
            self.sy(loads)
            if "t2b" in self.cols:
                def ld2(sy):
                    self.ld_s.inc(sy.dma_start(
                        out=self.cols["t2b"][:, :],
                        in_=self.win["L1_attn_tau2_b"].rearrange("(p o) -> p o", o=1)), 16)
                    self.ld_s.inc(sy.dma_start(
                        out=self.cols["d2b"][:, :],
                        in_=self.win["L1_attn_del2_b"].rearrange("(p o) -> p o", o=1)), 16)
                self.sy(ld2)
            if "t1w" in self.cols:
                def ldtaps(sy):
                    for key, bkey, wname, bname in (("t1w", "t1b", "L1_attn_tau1_w", "L1_attn_tau1_b"),
                                                    ("d1w", "d1b", "L1_attn_del1_w", "L1_attn_del1_b")):
                        wv = self.win[wname].rearrange("(a p q) o t -> a p (q o t)", p=128, q=2)
                        bv = self.win[bname].rearrange("(a p q) -> a p q", p=128, q=2)
                        for r in range(2):
                            for ct in range(8):
                                for tap in range(3):
                                    col = r * 24 + ct * 3 + tap
                                    self.ld_s.inc(sy.dma_start(
                                        out=self.cols[key][:, col:col + 1],
                                        in_=wv[ct, :, r * 3 + tap:r * 3 + tap + 1],
                                        allow_slow_non_contiguous=True), 16)
                                self.ld_s.inc(sy.dma_start(
                                    out=self.cols[bkey][:, r * 8 + ct:r * 8 + ct + 1],
                                    in_=bv[ct, :, r:r + 1],
                                    allow_slow_non_contiguous=True), 16)
                self.sy(ldtaps)

            vrow = self.sb(st, "pp_vrow", [128, 128], F32)
            pvt = self.psum(st, "pp_vt", [128, 128], F32)

            def vz(v):
                self.s_v.inc(v.memset(vrow[:, :], 0.0), 1)
            self.vc(vz)
            for key, name in COLVECS.items():
                if name not in self.used:
                    continue
                kk = (INPUT_SPECS[name][0] + 127) // 128
                v2 = self.win[name].rearrange("(c p) -> c p", p=128)

                def ld(sy):
                    sy.wait_ge(self.s_mm.h, self.s_mm.n)
                    sy.wait_ge(self.s_v.h, self.s_v.n)
                    self.ld_s.inc(sy.dma_start(out=vrow[0:kk, :], in_=v2), 16)
                self.sy(ld)
                n0 = self.ld_s.n

                def tr(t):
                    t.wait_ge(self.ld_s.h, n0)
                    t.wait_ge(self.s_v.h, self.s_v.n)
                    self.s_mm.inc(t.transpose(pvt[:, :], vrow[:, :], self.k["id_f32"][:]), 1)
                self.tn(tr)
                n1 = self.s_mm.n
                dst = self.cols[key]

                def ev(v):
                    v.wait_ge(self.s_mm.h, n1)
                    self.s_v.inc(v.tensor_copy(dst[:, 0:kk], pvt[:, 0:kk]), 1)
                self.vc(ev)
            if "w3sh" in self.cols:
                nv = self.s_v.n

                def half(v):
                    v.wait_ge(self.s_v.h, nv)
                    self.s_v.inc(v.tensor_scalar(self.cols["w3sh"][:, :], self.cols["w3s"][:, :],
                                                 0.5, None, ALU.mult), 1)
                self.vc(half)

    def _pp_x0(self):
        with ExitStack() as st:
            xt = self.sb(st, "pp_xt", [128, D], F32)
            slab = self.sb(st, "pp_slab", [128, NFT, S], F32)
            pft = self.psum(st, "pp_f1", [128, 512], F32)
            pft2 = self.psum(st, "pp_f2", [128, 512], F32)
            for tt in range(NTT):
                def ldx(sy):
                    sy.wait_ge(self.s_mm.h, self.s_mm.n)
                    self.ld_s.inc(sy.dma_start(out=xt[:, :], in_=self.x0[tt * 128:(tt + 1) * 128, :]), 16)
                self.sy(ldx)
                n0 = self.ld_s.n

                def tr(t):
                    t.wait_ge(self.ld_s.h, n0)
                    t.wait_ge(self.s_v.h, self.s_v.n)
                    ins = None
                    for ft in range(NFT):
                        p = pft if ft < 4 else pft2
                        ins = t.transpose(p[:, (ft % 4) * 128:(ft % 4 + 1) * 128],
                                          xt[:, ft * 128:(ft + 1) * 128], self.k["id_f32"][:])
                    self.s_mm.inc(ins, 1)
                self.tn(tr)
                n1 = self.s_mm.n

                def ev(v):
                    v.wait_ge(self.s_mm.h, n1)
                    for ft in range(NFT):
                        p = pft if ft < 4 else pft2
                        ins = v.tensor_copy(slab[:, ft, tt * 128:(tt + 1) * 128],
                                            p[:, (ft % 4) * 128:(ft % 4 + 1) * 128])
                    self.s_v.inc(ins, 1)
                self.vc(ev)
            nv = self.s_v.n

            def stx(sy):
                sy.wait_ge(self.s_v.h, nv)
                for ft in range(NFT):
                    self.st_s.inc(sy.dma_start(out=self.xa[ft * 128:(ft + 1) * 128, :],
                                               in_=slab[:, ft, :]), 16)
            self.sy(stx)
            stn = self.st_s.n

            def wst(sy):
                sy.wait_ge(self.st_s.h, stn)
            self.sy(wst)

    def _pp_transposes(self):
        jobs = []
        for name in sorted(self.wT):
            if name in TRANSPOSE_2D:
                R, C = TRANSPOSE_2D[name]
                jobs.append((self.wbf[name], self.wT[name], R, C, name in TILED))
            elif name in TRANSPOSE_3D:
                E, R, C = TRANSPOSE_3D[name]
                for e in range(E):
                    jobs.append((self.wbf[name][e], self.wT[name][e], R, C, name in TILED))
            elif name in SMALL_T:
                r, c = SMALL_T[name]
                if r % 16 == 0:
                    jobs.append((self.wbf[name], self.wT[name], r, c, False))
        with ExitStack() as st:
            slabs = [self.sb(st, f"pp_ws{i}", [128, 4096], BF16) for i in range(2)]
            i = 0
            for src, dst, R, C, tiled in jobs:
                for cb in range(C // 128):
                    par = i % 2
                    buf = slabs[par]
                    prev = self.st_t[par].n

                    def tld(sy):
                        if prev:
                            sy.wait_ge(self.st_t[par].h, prev)
                        self.ld_t[par].inc(sy.dma_start(out=buf[:, 0:R],
                                                        in_=src[0:R, cb * 128:(cb + 1) * 128],
                                                        transpose=True), 16)
                    self.sy(tld)
                    n0 = self.ld_t[par].n

                    def tst(sc):
                        sc.wait_ge(self.ld_t[par].h, n0)
                        if tiled:
                            self.st_t[par].inc(sc.dma_start(
                                out=dst[:, :, cb, :].rearrange("mt p m -> p mt m"),
                                in_=buf[:, 0:R].rearrange("p (mt m) -> p mt m", m=128)), 16)
                        else:
                            self.st_t[par].inc(sc.dma_start(out=dst[cb * 128:(cb + 1) * 128, 0:R],
                                                            in_=buf[:, 0:R]), 16)
                    self.sc(tst)
                    i += 1
        if "L0_ffn_gate_w" in self.used:
            with ExitStack() as st:
                buf = self.sb(st, "pp_gw", [128, D], BF16)
                obuf = self.sb(st, "pp_gwo", [128, NFT * 8], BF16)
                pt = self.psum(st, "pp_gpt", [128, 1024], BF16)

                nt0, nt1 = self.st_t[0].n, self.st_t[1].n

                def vz(v):
                    if nt0:
                        v.wait_ge(self.st_t[0].h, nt0)
                    if nt1:
                        v.wait_ge(self.st_t[1].h, nt1)
                    self.s_v.inc(v.memset(buf[:, :], 0.0), 1)
                self.vc(vz)
                nvz = self.s_v.n

                def ld(sy):
                    sy.wait_ge(self.s_v.h, nvz)
                    self.ld_s.inc(sy.dma_start(out=buf[0:8, :], in_=self.wbf["L0_ffn_gate_w"][:, :]), 16)
                self.sy(ld)
                n0 = self.ld_s.n

                def tr(t):
                    t.wait_ge(self.ld_s.h, n0)
                    ins = None
                    for ic in range(8):
                        ins = t.transpose(pt[:, ic * 128:(ic + 1) * 128],
                                          buf[:, ic * 128:(ic + 1) * 128], self.k["id_bf"][:])
                    self.s_mm.inc(ins, 1)
                self.tn(tr)
                n1 = self.s_mm.n

                def ev(v):
                    v.wait_ge(self.s_mm.h, n1)
                    for ic in range(8):
                        ins = v.tensor_copy(obuf[:, ic * 8:(ic + 1) * 8],
                                            pt[:, ic * 128:ic * 128 + 8])
                    self.s_v.inc(ins, 1)
                self.vc(ev)
                nv = self.s_v.n

                def stg(sy):
                    sy.wait_ge(self.s_v.h, nv)
                    gT = self.wT["L0_ffn_gate_w"].rearrange("(a p) e -> a p e", p=128)
                    for ic in range(8):
                        self.st_s.inc(sy.dma_start(out=gT[ic], in_=obuf[:, ic * 8:(ic + 1) * 8]), 16)
                self.sy(stg)
                stn = self.st_s.n

                def wst(sy):
                    sy.wait_ge(self.st_s.h, stn)
                self.sy(wst)

    def _pp_pe_transposes(self):
        names = [n for n in ("L1_attn_convq_w", "L1_attn_convk_w") if n in self.used]
        if not names:
            return
        with ExitStack() as st:
            buf = self.sb(st, "pp_cv", [128, D * 3], BF16)
            obuf = self.sb(st, "pp_co", [128, D], BF16)
            pt = self.psum(st, "pp_cpt", [128, 1024], BF16)
            for name in names:
                for ob in range(8):
                    def ld(sy):
                        sy.wait_ge(self.s_mm.h, self.s_mm.n)
                        self.ld_s.inc(sy.dma_start(out=buf[:, :],
                                                   in_=self.wbf[name][ob * 128:(ob + 1) * 128, :]), 16)
                    self.sy(ld)
                    n0 = self.ld_s.n
                    v3 = buf.rearrange("p (i t) -> p i t", t=3)
                    for tap in range(3):
                        def tr(t):
                            t.wait_ge(self.ld_s.h, n0)
                            t.wait_ge(self.s_v.h, self.s_v.n)
                            ins = None
                            for ic in range(8):
                                ins = t.transpose(pt[:, ic * 128:(ic + 1) * 128],
                                                  v3[:, ic * 128:(ic + 1) * 128, tap], self.k["id_bf"][:])
                            self.s_mm.inc(ins, 1)
                        self.tn(tr)
                        n1 = self.s_mm.n

                        def ev(v):
                            v.wait_ge(self.s_mm.h, n1)
                            v.wait_ge(self.st_s.h, self.st_s.n)
                            self.s_v.inc(v.tensor_copy(obuf[:, :], pt[:, :]), 1)
                        self.vc(ev)
                        nv = self.s_v.n

                        def stc(sy):
                            sy.wait_ge(self.s_v.h, nv)
                            for ic in range(8):
                                self.st_s.inc(sy.dma_start(
                                    out=self.wT[name][tap, ic * 128:(ic + 1) * 128,
                                                      ob * 128:(ob + 1) * 128],
                                    in_=obuf[:, ic * 128:(ic + 1) * 128]), 16)
                        self.sy(stc)
            stn = self.st_s.n

            def wst(sy):
                sy.wait_ge(self.st_s.h, stn)
            self.sy(wst)

    def emit_ln_chunk(self, xs, nbuf, xbf, xsq, rows, tt, p_s1, p_s2, p_bc, T):
        NCH = T // 512
        r_m = rows[:, 0:512]; r_msq = rows[:, 512:1024]; r_var = rows[:, 1024:1536]
        r_a = rows[:, 1536:2048]; r_t = rows[:, 2048:2560]
        r_b = r_msq

        def s_cast(s):
            s.activation(xbf[:, :, :], xs[:, :, :], AF.Copy)
            self.s_sc.inc(s.activation(xsq[:, :, :], xs[:, :, :], AF.Square), 1)
        self.sc(s_cast)
        n_sc = self.s_sc.n

        for nch in range(NCH):
            sl = slice(nch * 512, (nch + 1) * 512)

            def t_stats(t):
                t.wait_ge(self.s_sc.h, n_sc)
                t.wait_ge(self.s_sc.h, self.s_sc.n)
                for kt in range(NFT):
                    t.matmul(p_s1[:, :], self.k["ones_col_bf"][:, :], xbf[:, kt, sl],
                             start=(kt == 0), stop=(kt == NFT - 1))
                ins = None
                for kt in range(NFT):
                    ins = t.matmul(p_s2[:, :], self.k["ones_col_bf"][:, :], xsq[:, kt, sl],
                                   start=(kt == 0), stop=(kt == NFT - 1))
                self.s_mm.inc(ins, 1)
            self.tn(t_stats)
            n_mm = self.s_mm.n

            def s_stats(s):
                s.wait_ge(self.s_mm.h, n_mm)
                s.wait_ge(self.s_v.h, self.s_v.n)
                s.activation(r_t[:, :], p_s1[:, :], AF.Square, scale=1.0 / D)
                s.activation(r_msq[:, :], p_s2[:, :], AF.Copy, scale=1.0 / D)
                self.s_sc.inc(s.activation(r_m[:, :], p_s1[:, :], AF.Copy, scale=1.0 / D), 1)
            self.sc(s_stats)
            n_s1 = self.s_sc.n

            def v_var(v):
                v.wait_ge(self.s_sc.h, n_s1)
                self.s_v.inc(v.tensor_sub(r_var[:, :], r_msq[:, :], r_t[:, :]), 1)
            self.vc(v_var)
            n_v = self.s_v.n

            def s_rstd(s):
                s.wait_ge(self.s_v.h, n_v)
                self.s_sc.inc(s.activation(r_t[:, :], r_var[:, :], AF.Ln, bias=1e-5), 1)
                s.wait_ge(self.s_sc.h, self.s_sc.n)
                self.s_sc.inc(s.activation(r_a[:, :], r_t[:, :], AF.Exp, scale=-0.5), 1)
            self.sc(s_rstd)
            n_s2 = self.s_sc.n

            def v_b(v):
                v.wait_ge(self.s_sc.h, n_s2)
                self.s_v.inc(v.scalar_tensor_tensor(r_b[:, :], r_m[:, :], -1.0, r_a[:, :],
                                                    ALU.mult, ALU.mult), 1)
            self.vc(v_b)
            n_v2 = self.s_v.n

            def t_bc(t):
                t.wait_ge(self.s_v.h, n_v2)
                t.wait_ge(self.s_po.h, self.s_po.n)
                t.wait_ge(self.s_sc.h, self.s_sc.n)
                t.matmul(p_bc[:, 0:512], self.k["ones_row_f32"][:, :], r_a[:, :], start=True, stop=True)
                ins = t.matmul(p_bc[:, 512:1024], self.k["ones_row_f32"][:, :], r_b[:, :], start=True, stop=True)
                self.s_mm.inc(ins, 1)
            self.tn(t_bc)
            n_mm2 = self.s_mm.n

            def s_bevac(s):
                s.wait_ge(self.s_mm.h, n_mm2)
                s.wait_ge(self.s_po.h, self.s_po.n)
                self.s_sc.inc(s.activation(tt[:, 1, :], p_bc[:, 512:1024], AF.Copy), 1)
            self.sc(s_bevac)
            n_s3 = self.s_sc.n

            for ft in range(NFT):
                def v_mul(v):
                    v.wait_ge(self.s_mm.h, n_mm2)
                    if ft >= 1:
                        v.wait_ge(self.s_po.h, self.s_po.n)
                    self.s_v.inc(v.tensor_mul(tt[:, 0, :], xs[:, ft, sl], p_bc[:, 0:512]), 1)
                self.vc(v_mul)
                n_v3 = self.s_v.n

                def p_add(g):
                    g.wait_ge(self.s_v.h, n_v3)
                    g.wait_ge(self.s_sc.h, n_s3)
                    self.s_po.inc(g.tensor_add(nbuf[:, ft, sl], tt[:, 0, :], tt[:, 1, :]), 1)
                self.gp(p_add)

    def emit_ffn1(self, xin, xout):
        T = 1024
        NCH = S // T
        w1T, w2T = self.wT["L1_ffn_w1_w"], self.wT["L1_ffn_w2_w"]
        h1T, h2T = self.wT["L1_ffn_h1_w"], self.wT["L1_ffn_h2_w"]
        w3T = self.wT["L1_ffn_w3_w"]
        cols = self.cols
        with ExitStack() as st:
            xs = self.sb(st, "f1_xs", [128, NFT, T], F32)
            hbuf = self.sb(st, "f1_h", [128, 32, T], BF16)
            gbuf = self.sb(st, "f1_g", [128, 32, T], BF16)
            wcol = [self.sb(st, f"f1_w{i}", [128, 32, 128], BF16) for i in range(2)]
            sstg = self.sb(st, "f1_ss", [128, 2, 512], BF16)
            vstg = self.sb(st, "f1_vs", [128, 2, 512], BF16)
            ostg = self.sb(st, "f1_os", [128, 2, 512], F32)
            rows = self.sb(st, "f1_rows", [1, 5 * 512], F32)
            ttile = self.sb(st, "f1_tt", [128, 2, 512], F32)
            zstg = ttile
            pzp = [self.psum(st, f"f1_pzp{i}", [128, 1024], F32) for i in range(2)]
            pz = [pzp[0][:, 0:512], pzp[0][:, 512:1024], pzp[1][:, 0:512], pzp[1][:, 512:1024]]
            p_s1 = self.psum(st, "f1_ps1", [1, 512], F32)
            p_s2 = self.psum(st, "f1_ps2", [1, 512], F32)
            p_bc = self.psum(st, "f1_pbc", [128, 1024], F32)
            nbuf = gbuf[:, 0:NFT, :]
            xbf = hbuf[:, 0:NFT, :]
            xsq = hbuf[:, NFT:2 * NFT, :]

            wc_cons = {0: None, 1: None}
            z1_cons = {0: 0, 1: 0}
            zfull = [ttile.reshape([128, 1024]), ostg.reshape([128, 1024])]
            zf_cons = {0: 0, 1: 0}

            for c in range(NCH):
                def ldx(sy):
                    sy.wait_ge(self.s_v.h, self.s_v.n)
                    sy.wait_ge(self.s_sc.h, self.s_sc.n)
                    xr = xin.rearrange("(a p) t -> a p t", p=128)
                    for ft in range(NFT):
                        self.ld_x.inc(sy.dma_start(out=xs[:, ft, :], in_=xr[ft, :, c * T:(c + 1) * T]), 16)
                self.sy(ldx)
                nld = self.ld_x.n

                def wx(e):
                    e.wait_ge(self.ld_x.h, nld)
                    e.wait_ge(self.s_mm.h, self.s_mm.n)
                self.sc(wx)
                self.vc(lambda v: v.wait_ge(self.ld_x.h, nld))

                self.emit_ln_chunk(xs, nbuf, xbf, xsq, rows, ttile, p_s1, p_s2, p_bc, T)
                nv_ln = self.s_v.n

                pz_cons = {0: None, 1: None}
                for mt in range(32):
                    wb = wcol[mt % 2]

                    def ldw(g):
                        if wc_cons[mt % 2] is not None:
                            g.wait_ge(self.s_mm.h, wc_cons[mt % 2])
                        self.ld_w[mt % 2].inc(g.dma_start(out=wb[:, 0:4, :], in_=w1T[mt][:, 0:4, :]), 16)
                    self.gp(ldw)

                    def ldwv(v_):
                        if wc_cons[mt % 2] is not None:
                            v_.wait_ge(self.s_mm.h, wc_cons[mt % 2])
                        self.ld_v2[mt % 2].inc(v_.dma_start(out=wb[:, 4:8, :], in_=w1T[mt][:, 4:8, :]), 16)
                    self.sy(ldwv)

                    def ldw2(sc_):
                        if wc_cons[mt % 2] is not None:
                            sc_.wait_ge(self.s_mm.h, wc_cons[mt % 2])
                        self.ld_c2[mt % 2].inc(sc_.dma_start(out=wb[:, 8:12, :], in_=w2T[mt][:, 0:4, :]), 16)
                    self.sc(ldw2)

                    def ldwy(sy):
                        if wc_cons[mt % 2] is not None:
                            sy.wait_ge(self.s_mm.h, wc_cons[mt % 2])
                        self.ld_y2[mt % 2].inc(sy.dma_start(out=wb[:, 12:16, :], in_=w2T[mt][:, 4:8, :]), 16)
                    self.sy(ldwy)
                    nw = self.ld_w[mt % 2].n
                    nw2 = self.ld_c2[mt % 2].n
                    nw3 = self.ld_v2[mt % 2].n
                    nw4 = self.ld_y2[mt % 2].n
                    pzA, pzB = pz[2 * (mt % 2)], pz[2 * (mt % 2) + 1]

                    s1_pend = []
                    for nch in range(2):
                        pcw = pz_cons[mt % 2]

                        def mm(t):
                            t.wait_ge(self.ld_w[mt % 2].h, nw)
                            t.wait_ge(self.ld_c2[mt % 2].h, nw2)
                            t.wait_ge(self.ld_v2[mt % 2].h, nw3)
                            t.wait_ge(self.ld_y2[mt % 2].h, nw4)
                            if mt == 0 and nch == 0:
                                t.wait_ge(self.s_v.h, nv_ln)
                            if pcw is not None:
                                t.wait_ge(self.s_v.h, pcw)
                            for kt in range(NFT):
                                t.matmul(pzA[:, :], wb[:, kt, :], nbuf[:, kt, nch * 512:(nch + 1) * 512],
                                         start=(kt == 0), stop=(kt == NFT - 1))
                            ins = None
                            for kt in range(NFT):
                                ins = t.matmul(pzB[:, :], wb[:, 8 + kt, :], nbuf[:, kt, nch * 512:(nch + 1) * 512],
                                               start=(kt == 0), stop=(kt == NFT - 1))
                            self.s_mm.inc(ins, 1)
                        self.tn(mm)
                        n_mm = self.s_mm.n

                        def ev_z(v, n_mm=n_mm, nch=nch, mt=mt, zw=z1_cons[nch], pog=self.s_po.n):
                            v.wait_ge(self.s_mm.h, n_mm)
                            v.wait_ge(self.s_po.h, pog)
                            if zw:
                                v.wait_ge(self.s_sc.h, zw)
                            self.s_v.inc(v.tensor_scalar(zstg[:, nch, :], pzA[:, :],
                                                         cols["w1s"][:, mt:mt + 1], None, ALU.mult), 1)
                        self.vc(ev_z)
                        n_vz = self.s_v.n

                        self.s_sc.n += 1
                        n_silu = self.s_sc.n
                        z1_cons[nch] = n_silu

                        def ev_s(s, n_vz=n_vz, nch=nch, pog=self.s_po.n):
                            s.wait_ge(self.s_v.h, n_vz)
                            s.wait_ge(self.s_po.h, pog)
                            s.activation(sstg[:, nch, :], zstg[:, nch, :],
                                         AF.Silu).then_inc(self.s_sc.h, 1)
                        s1_pend.append(ev_s)

                        def ev_v(v):
                            v.wait_ge(self.s_mm.h, n_mm)
                            v.wait_ge(self.s_po.h, self.s_po.n)
                            self.s_v.inc(v.tensor_scalar(vstg[:, nch, :], pzB[:, :],
                                                         cols["w2s"][:, mt:mt + 1], None, ALU.mult), 1)
                        self.vc(ev_v)
                        n_sc2, n_v2 = n_silu, self.s_v.n

                        def po(g):
                            g.wait_ge(self.s_sc.h, n_sc2)
                            g.wait_ge(self.s_v.h, n_v2)
                            self.s_po.inc(g.tensor_mul(hbuf[:, mt, nch * 512:(nch + 1) * 512],
                                                       sstg[:, nch, :], vstg[:, nch, :]), 1)
                        self.gp(po)
                        pz_cons[mt % 2] = self.s_v.n
                    self.sc(lambda s, ops=s1_pend: [f(s) for f in ops])
                    wc_cons[mt % 2] = self.s_mm.n

                for stage, (wt_, src_, dst_, scol) in enumerate(
                        [(h1T, hbuf, gbuf, "h1s"), (h2T, gbuf, hbuf, "h2s")]):
                    n_po0, n_sc0 = self.s_po.n, self.s_sc.n

                    def stage_gate(t):
                        t.wait_ge(self.s_po.h, n_po0)
                        t.wait_ge(self.s_sc.h, n_sc0)
                    self.tn(stage_gate)
                    pzp_cons = {0: 0, 1: 0}
                    n_st0 = [self.st_o[0].n, self.st_o[1].n]
                    s2_pend = []
                    for mt in range(32):
                        par = mt % 2
                        wb = wcol[par]

                        def ldw(g):
                            if wc_cons[par] is not None:
                                g.wait_ge(self.s_mm.h, wc_cons[par])
                            self.ld_w[par].inc(g.dma_start(out=wb[:, 0:8, :],
                                                           in_=wt_[mt][:, 0:8, :]), 16)
                        self.gp(ldw)

                        def ldwv(v_):
                            if wc_cons[par] is not None:
                                v_.wait_ge(self.s_mm.h, wc_cons[par])
                            self.ld_v2[par].inc(v_.dma_start(out=wb[:, 8:16, :],
                                                             in_=wt_[mt][:, 8:16, :]), 16)
                        self.sy(ldwv)

                        def ldwb(sc_):
                            if wc_cons[par] is not None:
                                sc_.wait_ge(self.s_mm.h, wc_cons[par])
                            self.ld_c2[par].inc(sc_.dma_start(out=wb[:, 16:24, :],
                                                              in_=wt_[mt][:, 16:24, :]), 16)
                        self.sc(ldwb)

                        def ldwy(sy):
                            if wc_cons[par] is not None:
                                sy.wait_ge(self.s_mm.h, wc_cons[par])
                            self.ld_y2[par].inc(sy.dma_start(out=wb[:, 24:32, :],
                                                             in_=wt_[mt][:, 24:32, :]), 16)
                        self.sy(ldwy)
                        nw = self.ld_w[par].n
                        nw2 = self.ld_c2[par].n
                        nw3 = self.ld_v2[par].n
                        nw4 = self.ld_y2[par].n
                        pcw = pzp_cons[par]

                        def mm(t, par=par, nw=nw, nw2=nw2, nw3=nw3, nw4=nw4, pcw=pcw):
                            t.wait_ge(self.ld_w[par].h, nw)
                            t.wait_ge(self.ld_c2[par].h, nw2)
                            t.wait_ge(self.ld_v2[par].h, nw3)
                            t.wait_ge(self.ld_y2[par].h, nw4)
                            if pcw:
                                t.wait_ge(self.s_v.h, pcw)
                            ins = None
                            for nch in range(2):
                                for kt in range(32):
                                    ins = t.matmul(pzp[par][:, nch * 512:(nch + 1) * 512],
                                                   wb[:, kt, :], src_[:, kt, nch * 512:(nch + 1) * 512],
                                                   start=(kt == 0), stop=(kt == 31))
                            self.s_mm.inc(ins, 1)
                        self.tn(mm)
                        n_mm = self.s_mm.n

                        def ev_z(v, par=par, n_mm=n_mm, mt=mt, zw=zf_cons[par]):
                            v.wait_ge(self.s_mm.h, n_mm)
                            if zw:
                                v.wait_ge(self.s_sc.h, zw)
                            if par == 1:
                                v.wait_ge(self.st_o[0].h, n_st0[0])
                                v.wait_ge(self.st_o[1].h, n_st0[1])
                            self.s_v.inc(v.tensor_scalar(zfull[par][:, :], pzp[par][:, :],
                                                         cols[scol][:, mt:mt + 1], None, ALU.mult), 1)
                        self.vc(ev_z)
                        n_vz = self.s_v.n
                        pzp_cons[par] = n_vz

                        self.s_sc.n += 1
                        zf_cons[par] = self.s_sc.n

                        def ev(s, par=par, n_vz=n_vz, mt=mt):
                            s.wait_ge(self.s_v.h, n_vz)
                            s.activation(dst_[:, mt, :], zfull[par][:, :],
                                         AF.Gelu).then_inc(self.s_sc.h, 1)
                        s2_pend.append(ev)
                        if mt % 2 == 1:
                            self.sc(lambda s, ops=s2_pend: [f(s) for f in ops])
                            s2_pend = []
                        wc_cons[par] = self.s_mm.n

                n_sc0 = self.s_sc.n

                def stage_gate4(t):
                    t.wait_ge(self.s_sc.h, n_sc0)
                self.tn(stage_gate4)
                pz_cons4 = {0: None, 1: None, 2: None, 3: None}
                for dt in range(8):
                    wb = wcol[dt % 2]

                    def ldw(g):
                        if wc_cons[dt % 2] is not None:
                            g.wait_ge(self.s_mm.h, wc_cons[dt % 2])
                        self.ld_w[dt % 2].inc(g.dma_start(out=wb[:, 0:8, :],
                                                          in_=w3T[dt][:, 0:8, :]), 16)
                    self.gp(ldw)

                    def ldwv(v_):
                        if wc_cons[dt % 2] is not None:
                            v_.wait_ge(self.s_mm.h, wc_cons[dt % 2])
                        self.ld_v2[dt % 2].inc(v_.dma_start(out=wb[:, 8:16, :],
                                                            in_=w3T[dt][:, 8:16, :]), 16)
                    self.sy(ldwv)

                    def ldwb(sc_):
                        if wc_cons[dt % 2] is not None:
                            sc_.wait_ge(self.s_mm.h, wc_cons[dt % 2])
                        self.ld_c2[dt % 2].inc(sc_.dma_start(out=wb[:, 16:24, :],
                                                             in_=w3T[dt][:, 16:24, :]), 16)
                    self.sc(ldwb)

                    def ldwy(sy):
                        if wc_cons[dt % 2] is not None:
                            sy.wait_ge(self.s_mm.h, wc_cons[dt % 2])
                        self.ld_y2[dt % 2].inc(sy.dma_start(out=wb[:, 24:32, :],
                                                            in_=w3T[dt][:, 24:32, :]), 16)
                    self.sy(ldwy)
                    nw = self.ld_w[dt % 2].n
                    nw2 = self.ld_c2[dt % 2].n
                    nw3 = self.ld_v2[dt % 2].n
                    nw4 = self.ld_y2[dt % 2].n
                    pzA = pz[2 * (dt % 2)]
                    pzB = pz[2 * (dt % 2) + 1]
                    for nch in range(2):
                        pzX = pzA if nch == 0 else pzB
                        bank = 2 * (dt % 2) + nch
                        pcw = pz_cons4[bank]

                        def mm(t):
                            t.wait_ge(self.ld_w[dt % 2].h, nw)
                            t.wait_ge(self.ld_c2[dt % 2].h, nw2)
                            t.wait_ge(self.ld_v2[dt % 2].h, nw3)
                            t.wait_ge(self.ld_y2[dt % 2].h, nw4)
                            if pcw is not None:
                                t.wait_ge(self.s_v.h, pcw)
                            ins = None
                            for kt in range(32):
                                ins = t.matmul(pzX[:, :], wb[:, kt, :], hbuf[:, kt, nch * 512:(nch + 1) * 512],
                                               start=(kt == 0), stop=(kt == 31))
                            self.s_mm.inc(ins, 1)
                        self.tn(mm)
                        n_mm = self.s_mm.n

                        def ev(v):
                            v.wait_ge(self.s_mm.h, n_mm)
                            v.wait_ge(self.st_o[nch].h, self.st_o[nch].n)
                            self.s_v.inc(v.scalar_tensor_tensor(
                                ostg[:, nch, :], pzX[:, :], cols["w3sh"][:, dt:dt + 1],
                                xs[:, dt, nch * 512:(nch + 1) * 512], ALU.mult, ALU.add), 1)
                        self.vc(ev)
                        n_v3 = self.s_v.n

                        def stv(sy):
                            sy.wait_ge(self.s_v.h, n_v3)
                            self.st_o[nch].inc(sy.dma_start(
                                out=xout[dt * 128:(dt + 1) * 128,
                                         c * T + nch * 512:c * T + (nch + 1) * 512],
                                in_=ostg[:, nch, :]), 16)
                        self.sy(stv)
                        pz_cons4[bank] = self.s_v.n
                    wc_cons[dt % 2] = self.s_mm.n

    def emit_ffn0(self, xin, xout):
        T = 1024
        NCH_T = S // T
        wgT = self.wT["L0_ffn_wg"]; wuT = self.wT["L0_ffn_wu"]; wdT = self.wT["L0_ffn_wd"]
        sgT = self.wT["L0_ffn_sg"]; suT = self.wT["L0_ffn_su"]; sdT = self.wT["L0_ffn_sd"]
        gT = self.wT["L0_ffn_gate_w"]
        with ExitStack() as st:
            xs = self.sb(st, "f0_xs", [128, NFT, T], F32)
            hall = self.sb(st, "f0_h", [128, 36, T], BF16)
            nbuf = self.sb(st, "f0_n", [128, NFT, T], BF16)
            wcol = [self.sb(st, f"f0_w{i}", [128, 36, 128], BF16) for i in range(2)]
            gate_sT = self.sb(st, "f0_gt", [128, 8, 8], BF16)
            sel8 = self.sb(st, "f0_sel8", [8, 8 * 128], BF16)
            gsc = self.sb(st, "f0_gs", [128, 16], F32)
            ebuf = self.sb(st, "f0_eb", [128, 8], F32)
            pbuf = self.sb(st, "f0_pb", [128, 8], F32)
            pbuf2 = self.sb(st, "f0_pb2", [128, 8], F32)
            cwf = self.sb(st, "f0_cwf", [128, 8], F32)
            cwT = self.sb(st, "f0_cwT", [8, T], BF16)
            cwbs = self.sb(st, "f0_cwb", [128, T], BF16)
            sstg = self.sb(st, "f0_ss", [128, 2, 512], BF16)
            vstg = self.sb(st, "f0_vs", [128, 2, 512], F32)
            ostg = self.sb(st, "f0_os", [128, 2, 512], F32)
            rows = self.sb(st, "f0_rows", [1, 5 * 512], F32)
            ttile = self.sb(st, "f0_tt", [128, 2, 512], F32)
            pz = [self.psum(st, f"f0_pz{i}", [128, 512], F32) for i in range(4)]
            p_s1 = self.psum(st, "f0_ps1", [1, 512], F32)
            p_s2 = self.psum(st, "f0_ps2", [1, 512], F32)
            p_bc = self.psum(st, "f0_pbc", [128, 1024], F32)
            xbf = hall[:, 0:NFT, :]
            xsq = hall[:, NFT:2 * NFT, :]

            def ldg(sy):
                self.ld_s.inc(sy.dma_start(out=gate_sT[:, :, :],
                                           in_=gT.rearrange("(a p) e -> p a e", p=128)), 16)
                self.ld_s.inc(sy.dma_start(out=sel8[:, :], in_=self.c["sel8"][:]), 16)
            self.sy(ldg)
            nldg = self.ld_s.n

            wc_cons = {0: None, 1: None}

            for c in range(NCH_T):
                def ldx(sy):
                    sy.wait_ge(self.s_v.h, self.s_v.n)
                    sy.wait_ge(self.s_sc.h, self.s_sc.n)
                    xr = xin.rearrange("(a p) t -> a p t", p=128)
                    for ft in range(NFT):
                        self.ld_x.inc(sy.dma_start(out=xs[:, ft, :], in_=xr[ft, :, c * T:(c + 1) * T]), 16)
                self.sy(ldx)
                nld = self.ld_x.n

                def wx(e):
                    e.wait_ge(self.ld_x.h, nld)
                    e.wait_ge(self.s_mm.h, self.s_mm.n)
                self.sc(wx)
                self.vc(lambda v: v.wait_ge(self.ld_x.h, nld))

                self.emit_ln_chunk(xs, nbuf, xbf, xsq, rows, ttile, p_s1, p_s2, p_bc, T)
                nv_ln = self.s_v.n
                npo_ln = self.s_po.n

                for tt in range(T // 128):
                    def g_mm(t):
                        if tt == 0:
                            t.wait_ge(self.s_po.h, npo_ln)
                            t.wait_ge(self.ld_s.h, nldg)
                            t.wait_ge(self.s_sc.h, self.s_sc.n)
                            t.wait_ge(self.s_v.h, self.s_v.n)
                        else:
                            t.wait_ge(self.s_sc.h, self.s_sc.n)
                        ins = None
                        for kt in range(NFT):
                            ins = t.matmul(pz[0][:, 0:8], nbuf[:, kt, tt * 128:(tt + 1) * 128],
                                           gate_sT[:, kt, :], start=(kt == 0), stop=(kt == NFT - 1))
                        self.s_mm.inc(ins, 1)
                    self.tn(g_mm)
                    n_gmm = self.s_mm.n

                    def g_mx(v):
                        v.wait_ge(self.s_mm.h, n_gmm)
                        self.s_v.inc(v.reduce_max(gsc[:, 0:1], pz[0][:, 0:8], axis=AX.X), 1)
                        v.wait_ge(self.s_v.h, self.s_v.n)
                        self.s_v.inc(v.tensor_scalar(gsc[:, 1:2], gsc[:, 0:1], -1.0, None, ALU.mult), 1)
                    self.vc(g_mx)
                    n_vmx = self.s_v.n

                    def g_exp(s):
                        s.wait_ge(self.s_v.h, n_vmx)
                        self.s_sc.inc(s.activation(ebuf[:, :], pz[0][:, 0:8], AF.Exp,
                                                   bias=gsc[:, 1:2]), 1)
                    self.sc(g_exp)
                    n_sexp = self.s_sc.n

                    def g_top2(v):
                        v.wait_ge(self.s_sc.h, n_sexp)
                        self.s_v.inc(v.reduce_sum(gsc[:, 2:3], ebuf[:, :], axis=AX.X), 1)
                        v.wait_ge(self.s_v.h, self.s_v.n)
                        self.s_v.inc(v.reciprocal(gsc[:, 3:4], gsc[:, 2:3]), 1)
                        v.wait_ge(self.s_v.h, self.s_v.n)
                        self.s_v.inc(v.tensor_scalar(pbuf[:, :], ebuf[:, :], gsc[:, 3:4], None, ALU.mult), 1)
                        v.wait_ge(self.s_v.h, self.s_v.n)
                        self.s_v.inc(v.reduce_max(gsc[:, 4:5], pbuf[:, :], axis=AX.X), 1)
                        v.wait_ge(self.s_v.h, self.s_v.n)
                        self.s_v.inc(v.tensor_scalar(pbuf2[:, :], pbuf[:, :], gsc[:, 4:5], None,
                                                     ALU.is_equal), 1)
                        v.wait_ge(self.s_v.h, self.s_v.n)
                        self.s_v.inc(v.tensor_scalar(pbuf2[:, :], pbuf2[:, :], -1.0, 1.0,
                                                     ALU.mult, ALU.add), 1)
                        v.wait_ge(self.s_v.h, self.s_v.n)
                        self.s_v.inc(v.tensor_mul(pbuf2[:, :], pbuf[:, :], pbuf2[:, :]), 1)
                        v.wait_ge(self.s_v.h, self.s_v.n)
                        self.s_v.inc(v.reduce_max(gsc[:, 5:6], pbuf2[:, :], axis=AX.X), 1)
                        v.wait_ge(self.s_v.h, self.s_v.n)
                        self.s_v.inc(v.tensor_scalar(pbuf2[:, :], pbuf[:, :], gsc[:, 5:6], None,
                                                     ALU.is_ge), 1)
                        v.wait_ge(self.s_v.h, self.s_v.n)
                        self.s_v.inc(v.tensor_mul(cwf[:, :], pbuf[:, :], pbuf2[:, :]), 1)
                    self.vc(g_top2)
                    n_vcw = self.s_v.n

                    def g_tr(t):
                        t.wait_ge(self.s_v.h, n_vcw)
                        if tt == 0:
                            t.wait_ge(self.s_v.h, self.s_v.n)
                        p = pz[1] if tt < 4 else pz[2]
                        self.s_mm.inc(t.transpose(p[0:8, (tt % 4) * 128:(tt % 4 + 1) * 128],
                                                  cwf[:, :], self.k["id_f32"][:]), 1)
                    self.tn(g_tr)
                n_trcw = self.s_mm.n

                def cw_ev(v):
                    v.wait_ge(self.s_mm.h, n_trcw)
                    v.tensor_copy(cwT[:, 0:512], pz[1][0:8, :])
                    self.s_v.inc(v.tensor_copy(cwT[:, 512:1024], pz[2][0:8, :]), 1)
                self.vc(cw_ev)
                n_cwev = self.s_v.n

                pzA_cons = {0: None, 1: None}
                for unit in range(9):
                    is_sh = (unit == 8)
                    if not is_sh:
                        def cw_bc(t):
                            t.wait_ge(self.s_v.h, self.s_v.n)
                            t.matmul(p_bc[:, 0:512], sel8[:, unit * 128:(unit + 1) * 128],
                                     cwT[:, 0:512], start=True, stop=True)
                            ins = t.matmul(p_bc[:, 512:1024], sel8[:, unit * 128:(unit + 1) * 128],
                                           cwT[:, 512:1024], start=True, stop=True)
                            self.s_mm.inc(ins, 1)
                        self.tn(cw_bc)
                        n_bcmm = self.s_mm.n

                        def cw_evb(v):
                            v.wait_ge(self.s_mm.h, n_bcmm)
                            v.wait_ge(self.s_po.h, self.s_po.n)
                            self.s_v.inc(v.tensor_copy(cwbs[:, :], p_bc[:, :]), 1)
                        self.vc(cw_evb)

                    for ft in range(4):
                        par = (unit * 4 + ft) % 2
                        wb = wcol[par]

                        if is_sh:
                            s1, s2 = sgT, suT
                        else:
                            s1, s2 = wgT[unit], wuT[unit]
                        wcc = wc_cons[par]

                        def ldw(g, s1=s1, par=par, wcc=wcc, ft=ft):
                            if wcc is not None:
                                g.wait_ge(self.s_mm.h, wcc)
                            self.ld_w[par].inc(g.dma_start(out=wb[:, 0:4, :], in_=s1[ft][:, 0:4, :]), 16)
                        self.gp(ldw)

                        def ldwv(v_, s1=s1, par=par, wcc=wcc, ft=ft):
                            if wcc is not None:
                                v_.wait_ge(self.s_mm.h, wcc)
                            self.ld_v2[par].inc(v_.dma_start(out=wb[:, 4:8, :], in_=s1[ft][:, 4:8, :]), 16)
                        self.sy(ldwv)

                        def ldw2(sc_, s2=s2, par=par, wcc=wcc, ft=ft):
                            if wcc is not None:
                                sc_.wait_ge(self.s_mm.h, wcc)
                            self.ld_c2[par].inc(sc_.dma_start(out=wb[:, 8:12, :], in_=s2[ft][:, 0:4, :]), 16)
                        self.sc(ldw2)

                        def ldwy(sy, s2=s2, par=par, wcc=wcc, ft=ft):
                            if wcc is not None:
                                sy.wait_ge(self.s_mm.h, wcc)
                            self.ld_y2[par].inc(sy.dma_start(out=wb[:, 12:16, :], in_=s2[ft][:, 4:8, :]), 16)
                        self.sy(ldwy)
                        nw = self.ld_w[par].n
                        nw2 = self.ld_c2[par].n
                        nw3 = self.ld_v2[par].n
                        nw4 = self.ld_y2[par].n
                        pzA, pzB = pz[2 * par], pz[2 * par + 1]
                        dst_u = 32 + ft if is_sh else unit * 4 + ft

                        sA_pend = []
                        for nch in range(2):
                            pcw = pzA_cons[par]

                            def mm(t):
                                t.wait_ge(self.ld_w[par].h, nw)
                                t.wait_ge(self.ld_c2[par].h, nw2)
                                t.wait_ge(self.ld_v2[par].h, nw3)
                                t.wait_ge(self.ld_y2[par].h, nw4)
                                if unit == 0 and ft == 0 and nch == 0:
                                    t.wait_ge(self.s_sc.h, self.s_sc.n)
                                if pcw is not None:
                                    t.wait_ge(self.s_sc.h, pcw[0])
                                    t.wait_ge(self.s_v.h, pcw[1])
                                for kt in range(NFT):
                                    t.matmul(pzA[:, :], wb[:, kt, :], nbuf[:, kt, nch * 512:(nch + 1) * 512],
                                             start=(kt == 0), stop=(kt == NFT - 1))
                                ins = None
                                for kt in range(NFT):
                                    ins = t.matmul(pzB[:, :], wb[:, 8 + kt, :],
                                                   nbuf[:, kt, nch * 512:(nch + 1) * 512],
                                                   start=(kt == 0), stop=(kt == NFT - 1))
                                self.s_mm.inc(ins, 1)
                            self.tn(mm)
                            n_mm = self.s_mm.n

                            self.s_sc.n += 1
                            n_silu = self.s_sc.n

                            def ev_s(s, n_mm=n_mm, nch=nch, pog=self.s_po.n):
                                s.wait_ge(self.s_mm.h, n_mm)
                                s.wait_ge(self.s_po.h, pog)
                                s.activation(sstg[:, nch, :], pzA[:, :],
                                             AF.Silu).then_inc(self.s_sc.h, 1)
                            sA_pend.append(ev_s)

                            def ev_v(v, n_mm=n_mm, n_silu=n_silu, nch=nch):
                                v.wait_ge(self.s_mm.h, n_mm)
                                v.wait_ge(self.s_sc.h, n_silu)
                                v.wait_ge(self.s_po.h, self.s_po.n)
                                self.s_v.inc(v.tensor_mul(vstg[:, nch, :], sstg[:, nch, :], pzB[:, :]), 1)
                            self.vc(ev_v)
                            n_v2 = self.s_v.n

                            def po(g):
                                g.wait_ge(self.s_v.h, n_v2)
                                if is_sh:
                                    self.s_po.inc(g.tensor_copy(hall[:, dst_u, nch * 512:(nch + 1) * 512],
                                                                vstg[:, nch, :]), 1)
                                else:
                                    self.s_po.inc(g.tensor_mul(hall[:, dst_u, nch * 512:(nch + 1) * 512],
                                                               vstg[:, nch, :],
                                                               cwbs[:, nch * 512:(nch + 1) * 512]), 1)
                            self.gp(po)
                            pzA_cons[par] = (n_silu, self.s_v.n)
                        self.sc(lambda s, ops=sA_pend: [f(s) for f in ops])
                        wc_cons[par] = self.s_mm.n

                n_po0 = self.s_po.n

                def gate_b(t):
                    t.wait_ge(self.s_po.h, n_po0)
                self.tn(gate_b)
                pzB_cons = {0: None, 1: None, 2: None, 3: None}
                for dt in range(8):
                    par = dt % 2
                    wb = wcol[par]

                    wcc = wc_cons[par]

                    def ldw(g, par=par, wcc=wcc, dt=dt):
                        if wcc is not None:
                            g.wait_ge(self.s_mm.h, wcc)
                        for e in range(3):
                            self.ld_w[par].inc(g.dma_start(out=wb[:, 4 * e:4 * e + 4, :],
                                                           in_=wdT[e][dt]), 16)
                    self.gp(ldw)

                    def ldwv(v_, par=par, wcc=wcc, dt=dt):
                        if wcc is not None:
                            v_.wait_ge(self.s_mm.h, wcc)
                        for e in range(3, 5):
                            self.ld_v2[par].inc(v_.dma_start(out=wb[:, 4 * e:4 * e + 4, :],
                                                             in_=wdT[e][dt]), 16)
                    self.sy(ldwv)

                    def ldwc(sc_, par=par, wcc=wcc, dt=dt):
                        if wcc is not None:
                            sc_.wait_ge(self.s_mm.h, wcc)
                        for e in range(5, 7):
                            self.ld_c2[par].inc(sc_.dma_start(out=wb[:, 4 * e:4 * e + 4, :],
                                                              in_=wdT[e][dt]), 16)
                    self.sc(ldwc)

                    def ldwy(sy, par=par, wcc=wcc, dt=dt):
                        if wcc is not None:
                            sy.wait_ge(self.s_mm.h, wcc)
                        self.ld_y2[par].inc(sy.dma_start(out=wb[:, 28:32, :], in_=wdT[7][dt]), 16)
                        self.ld_y2[par].inc(sy.dma_start(out=wb[:, 32:36, :], in_=sdT[dt]), 16)
                    self.sy(ldwy)
                    nw = self.ld_w[par].n
                    nw2 = self.ld_c2[par].n
                    nw3 = self.ld_v2[par].n
                    nw4 = self.ld_y2[par].n
                    pzA, pzB = pz[2 * par], pz[2 * par + 1]
                    for nch in range(2):
                        pzX = pzA if nch == 0 else pzB
                        bank = 2 * par + nch
                        pcw = pzB_cons[bank]

                        def mm(t):
                            t.wait_ge(self.ld_w[par].h, nw)
                            t.wait_ge(self.ld_c2[par].h, nw2)
                            t.wait_ge(self.ld_v2[par].h, nw3)
                            t.wait_ge(self.ld_y2[par].h, nw4)
                            if pcw is not None:
                                t.wait_ge(self.s_v.h, pcw)
                            ins = None
                            for kt in range(36):
                                ins = t.matmul(pzX[:, :], wb[:, kt, :], hall[:, kt, nch * 512:(nch + 1) * 512],
                                               start=(kt == 0), stop=(kt == 35))
                            self.s_mm.inc(ins, 1)
                        self.tn(mm)
                        n_mm = self.s_mm.n

                        def ev(v):
                            v.wait_ge(self.s_mm.h, n_mm)
                            v.wait_ge(self.st_o[nch].h, self.st_o[nch].n)
                            self.s_v.inc(v.scalar_tensor_tensor(
                                ostg[:, nch, :], pzX[:, :], 0.5,
                                xs[:, dt, nch * 512:(nch + 1) * 512], ALU.mult, ALU.add), 1)
                        self.vc(ev)
                        n_v3 = self.s_v.n

                        def stv(sy):
                            sy.wait_ge(self.s_v.h, n_v3)
                            self.st_o[nch].inc(sy.dma_start(
                                out=xout[dt * 128:(dt + 1) * 128,
                                         c * T + nch * 512:c * T + (nch + 1) * 512],
                                in_=ostg[:, nch, :]), 16)
                        self.sy(stv)
                        pzB_cons[bank] = self.s_v.n
                    wc_cons[par] = self.s_mm.n

    def emit_attn0(self, xin, xout):
        T = 1024
        wq_bf = self.wbf["L0_attn_Wq"]; wk_bf = self.wbf["L0_attn_Wk"]
        wv_bf = self.wbf["L0_attn_Wv"]; wo_bf = self.wbf["L0_attn_Wo"]
        cols = self.cols
        with ExitStack() as st:
            n1 = self.sb(st, "a0_n1", [128, NFT, S], BF16)
            vaug = self.sb(st, "a0_va", [128, NTT, H * 65], BF16)
            vaug_f = vaug.reshape([128, NTT * H * 65]).bitcast(F32)
            xs = vaug_f[:, 8192:16384].rearrange("p (a t) -> p a t", a=NFT)
            wsb = self.sb(st, "a0_wsb", [128, NFT, D], BF16)
            f2f = self.sb(st, "a0_f2f", [128, S], F32)
            fsq = self.sb(st, "a0_fsq", [128, S], BF16)
            knt = fsq[:, 0:2048]
            qn = self.sb(st, "a0_qn", [128, S], BF16)
            kn = self.sb(st, "a0_kn", [128, S], BF16)
            wblk = self.sb(st, "a0_wblk", [128, 2, NFT, 128], BF16)
            nsr = self.sb(st, "a0_nsr", [2, 512], F32)
            bias_b = qn.bitcast(F32)[:, 0:D]
            brow = kn.bitcast(F32)[0:1, 0:D]
            fsb = f2f.bitcast(BF16)[:, 3072:6144].rearrange("p (a t) -> p a t", a=3)
            osb = self.sb(st, "a0_osb", [64, 512], F32)
            obuf = self.sb(st, "a0_ob", [64, 2, 512], BF16)
            rdb = self.sb(st, "a0_rdb", [1, 512], F32)
            rows = f2f[0:1, 0:2560]
            ttile = f2f[:, 2560:3584].rearrange("p (a t) -> p a t", a=2)
            kn_f = kn.bitcast(F32)
            tstg = kn_f[:, 0:512]
            ostg = kn_f[:, 512:1536].rearrange("p (a t) -> p a t", a=2)
            pzC = self.psum(st, "a0_pzC", [128, 1024], F32)
            p_s1 = pzC[:, 0:512]
            p_s2 = pzC[:, 512:1024]
            p_bc = self.psum(st, "a0_pbc", [128, 1024], F32)
            pzA = self.psum(st, "a0_pzA", [128, 1024], F32)
            pzB = self.psum(st, "a0_pzB", [128, 1024], F32)
            pz = [pzA[:, 0:512], pzA[:, 512:1024], pzB[:, 0:512], pzB[:, 512:1024]]
            p_ns = p_s1[0:2, :]
            p_o = p_bc[0:65, 512:1024]
            wneg = self.sb(st, "a0_wng", [128, 65], BF16)
            ccol = self.sb(st, "a0_cc", [65, 1], F32)
            xbf = vaug.reshape([128, NTT * H * 65])[:, 0:NFT * T]
            xsq = vaug.reshape([128, NTT * H * 65])[:, NFT * T:2 * NFT * T]
            xbf = xbf.rearrange("p (a t) -> p a t", a=NFT)
            xsq = xsq.rearrange("p (a t) -> p a t", a=NFT)

            for c in range(S // T):
                def ldx(sy):
                    sy.wait_ge(self.s_v.h, self.s_v.n)
                    sy.wait_ge(self.s_sc.h, self.s_sc.n)
                    xr = xin.rearrange("(a p) t -> a p t", p=128)
                    for ft in range(NFT):
                        self.ld_x.inc(sy.dma_start(out=xs[:, ft, :], in_=xr[ft, :, c * T:(c + 1) * T]), 16)
                self.sy(ldx)
                nld = self.ld_x.n

                def wx(e):
                    e.wait_ge(self.ld_x.h, nld)
                    e.wait_ge(self.s_mm.h, self.s_mm.n)
                self.sc(wx)
                self.vc(lambda v: v.wait_ge(self.ld_x.h, nld))
                nbufc = n1[:, :, c * T:(c + 1) * T]
                self.emit_ln_chunk(xs, nbufc, xbf, xsq, rows, ttile, p_s1[0:1, :], p_s2[0:1, :], p_bc, T)
            n_po_ln = self.s_po.n

            def ldwv(sy):
                self.ld_s.inc(sy.dma_start(out=wsb[:, :, :],
                                           in_=wv_bf.rearrange("(a p) m -> p a m", p=128)), 16)
                self.ld_s.inc(sy.dma_start(out=brow[:, :],
                                           in_=self.win["L0_attn_bv"].rearrange("(o d) -> o d", o=1)), 16)
            self.sy(ldwv)
            n_ldv = self.ld_s.n

            def bias_mm(t):
                t.wait_ge(self.ld_s.h, n_ldv)
                t.wait_ge(self.s_po.h, n_po_ln)
                t.wait_ge(self.s_sc.h, self.s_sc.n)
                t.wait_ge(self.s_v.h, self.s_v.n)
                t.matmul(p_bc[:, 0:512], self.k["ones_row_f32"][:, :], brow[:, 0:512], start=True, stop=True)
                ins = t.matmul(p_bc[:, 512:1024], self.k["ones_row_f32"][:, :], brow[:, 512:1024],
                               start=True, stop=True)
                self.s_mm.inc(ins, 1)
            self.tn(bias_mm)
            n_bmm = self.s_mm.n

            def bias_ev(v):
                v.wait_ge(self.s_mm.h, n_bmm)
                self.s_v.inc(v.tensor_copy(bias_b[:, :], p_bc[:, :]), 1)
                self.s_v.inc(v.memset(vaug.rearrange("p a (h c) -> p a h c", c=65)[:, :, :, 64:65], 1.0), 1)
            self.vc(bias_ev)
            n_bev = self.s_v.n

            for tt in range(NTT):
                def vmm(t):
                    if tt == 0:
                        t.wait_ge(self.s_v.h, n_bev)
                    t.wait_ge(self.s_v.h, self.s_v.n)
                    ins = None
                    for half in range(2):
                        for kt in range(NFT):
                            ins = t.matmul(pz[half][:, :], n1[:, kt, tt * 128:(tt + 1) * 128],
                                           wsb[:, kt, half * 512:(half + 1) * 512],
                                           start=(kt == 0), stop=(kt == NFT - 1))
                    self.s_mm.inc(ins, 1)
                self.tn(vmm)
                n_vmm = self.s_mm.n

                def vev(v):
                    v.wait_ge(self.s_mm.h, n_vmm)
                    dst = vaug[:, tt, :].rearrange("p (h c) -> p h c", c=65)[:, :, 0:64]
                    src0 = pz[0].rearrange("p (h c) -> p h c", c=64)
                    src1 = pz[1].rearrange("p (h c) -> p h c", c=64)
                    bb0 = bias_b[:, 0:512].rearrange("p (h c) -> p h c", c=64)
                    bb1 = bias_b[:, 512:1024].rearrange("p (h c) -> p h c", c=64)
                    v.tensor_add(dst[:, 0:8, :], src0[:, :, :], bb0[:, :, :])
                    self.s_v.inc(v.tensor_add(dst[:, 8:16, :], src1[:, :, :], bb1[:, :, :]), 1)
                self.vc(vev)

            for hp in range(8):
                for which, (w_bf, bcol, dst, dstsq_needed) in enumerate(
                        [(wq_bf, "bq", qn, True), (wk_bf, "bk", kn, True)]):
                    def ldw(sy):
                        sy.wait_ge(self.s_mm.h, self.s_mm.n)
                        self.ld_s.inc(sy.dma_start(
                            out=wblk[:, which, :, :],
                            in_=w_bf.rearrange("(a p) m -> p a m", p=128)[:, :, hp * 128:(hp + 1) * 128]), 16)
                    self.sy(ldw)
                    n_lw = self.ld_s.n
                    for qc in range(8):
                        def qmm(t):
                            t.wait_ge(self.ld_s.h, n_lw)
                            t.wait_ge(self.s_v.h, self.s_v.n)
                            ins = None
                            for kt in range(NFT):
                                ins = t.matmul(pz[2][:, :], wblk[:, which, kt, :],
                                               n1[:, kt, qc * 512:(qc + 1) * 512],
                                               start=(kt == 0), stop=(kt == NFT - 1))
                            self.s_mm.inc(ins, 1)
                        self.tn(qmm)
                        n_qmm = self.s_mm.n

                        def qev(v):
                            v.wait_ge(self.s_mm.h, n_qmm)
                            v.wait_ge(self.s_po.h, self.s_po.n)
                            v.wait_ge(self.s_v.h, self.s_v.n)
                            self.s_v.inc(v.tensor_scalar(f2f[:, qc * 512:(qc + 1) * 512], pz[2][:, :],
                                                         cols[bcol][:, hp:hp + 1], None, ALU.add), 1)
                        self.vc(qev)
                        n_qb = self.s_v.n

                        def qsq_(g):
                            g.wait_ge(self.s_v.h, n_qb)
                            g.wait_ge(self.s_mm.h, self.s_mm.n)
                            self.s_po.inc(g.tensor_mul(fsq[:, qc * 512:(qc + 1) * 512],
                                                       f2f[:, qc * 512:(qc + 1) * 512],
                                                       f2f[:, qc * 512:(qc + 1) * 512]), 1)
                        self.gp(qsq_)
                        n_qsq = self.s_po.n

                        def nsmm(t):
                            t.wait_ge(self.s_po.h, n_qsq)
                            t.wait_ge(self.s_sc.h, self.s_sc.n)
                            t.wait_ge(self.s_v.h, self.s_v.n)
                            self.s_mm.inc(t.matmul(p_ns[:, :], self.k["head_ones"][:, :],
                                                   fsq[:, qc * 512:(qc + 1) * 512],
                                                   start=True, stop=True), 1)
                        self.tn(nsmm)
                        n_ns = self.s_mm.n

                        def s_rq(s):
                            s.wait_ge(self.s_mm.h, n_ns)
                            s.wait_ge(self.s_v.h, self.s_v.n)
                            self.s_sc.inc(s.activation(nsr[:, :], p_ns[:, :], AF.Ln, bias=1e-16), 1)
                            s.wait_ge(self.s_sc.h, self.s_sc.n)
                            self.s_sc.inc(s.activation(nsr[:, :], nsr[:, :], AF.Exp, scale=-0.5), 1)
                        self.sc(s_rq)
                        n_rq = self.s_sc.n

                        def bcq(t):
                            t.wait_ge(self.s_sc.h, n_rq)
                            t.wait_ge(self.s_v.h, self.s_v.n)
                            bc = self.k["head_bcast_neg"] if which == 1 else self.k["head_bcast_f32"]
                            self.s_mm.inc(t.matmul(pz[3][:, :], bc[:, :], nsr[:, :],
                                                   start=True, stop=True), 1)
                        self.tn(bcq)
                        n_bc = self.s_mm.n

                        def v_mulq(v):
                            v.wait_ge(self.s_mm.h, n_bc)
                            self.s_v.inc(v.tensor_mul(dst[:, qc * 512:(qc + 1) * 512],
                                                      f2f[:, qc * 512:(qc + 1) * 512], pz[3][:, :]), 1)
                        self.vc(v_mulq)

                CDEN = float((-3.1 * S + 1e-6) / 4.41)
                for hh in range(2):
                    h = hp * 2 + hh
                    base = 64 * hh

                    for rnd in range(4):
                        def ktr(t, rnd=rnd):
                            t.wait_ge(self.s_v.h, self.s_v.n)
                            t.wait_ge(self.s_sc.h, self.s_sc.n)
                            ins = None
                            for j in range(8):
                                kc = rnd * 8 + j
                                ins = t.matmul(pzA[:, j * 64:(j + 1) * 64],
                                               kn[base:base + 64, kc * 128:(kc + 1) * 128],
                                               self.k["id_bf"][0:64, 0:64] if base == 0 else
                                               self.k["id_bf"][64:128, 64:128],
                                               start=True, stop=True)
                            self.s_mm.inc(ins, 1)
                        self.tn(ktr)
                        n_ktr = self.s_mm.n

                        def kev(v, rnd=rnd, n_ktr=n_ktr):
                            v.wait_ge(self.s_mm.h, n_ktr)
                            self.s_v.inc(v.tensor_copy(knt[:, rnd * 512:(rnd + 1) * 512], pzA[:, 0:512]), 1)
                        self.vc(kev)
                    n_kev = self.s_v.n

                    def prep_mm(t):
                        t.wait_ge(self.s_v.h, n_kev)
                        t.wait_ge(self.s_sc.h, self.s_sc.n)
                        ins = None
                        for kc in range(NTT):
                            ins = t.matmul(p_s1[0:65, 0:1], vaug[:, kc, h * 65:(h + 1) * 65],
                                           self.k["ones_col_bf"][:, :],
                                           start=(kc == 0), stop=(kc == NTT - 1))
                        self.s_mm.inc(ins, 1)
                        ins = None
                        for kc in range(NTT):
                            ins = t.matmul(p_s2[0:64, 0:65], knt[:, kc * 64:(kc + 1) * 64],
                                           vaug[:, kc, h * 65:(h + 1) * 65],
                                           start=(kc == 0), stop=(kc == NTT - 1))
                        self.s_mm.inc(ins, 1)
                    self.tn(prep_mm)
                    n_prep = self.s_mm.n

                    def prep_ev(v):
                        v.wait_ge(self.s_mm.h, n_prep)
                        self.s_v.inc(v.tensor_scalar(ccol[:, :], p_s1[0:65, 0:1],
                                                     float(-3.1 / 4.41), None, ALU.mult), 1)
                    self.vc(prep_ev)

                    def prep_ev2(s):
                        s.wait_ge(self.s_mm.h, n_prep)
                        self.s_sc.inc(s.activation(wneg[base:base + 64, :], p_s2[0:64, 0:65],
                                                   AF.Copy, scale=float(1.0 / 4.41)), 1)
                    self.sc(prep_ev2)
                    n_wneg = self.s_sc.n

                    NS_SC = 16
                    po_cons_v = self.s_v.n
                    r_done = {0: None, 1: None, 2: None}
                    av_done = {0: None, 1: None, 2: None}
                    PD = [pzA, pzB, pzC]
                    for qc in range(8):
                        t_ops, v_ops, s_ops = [], [], []
                        t_dot = {}

                        def emit_av(kcc):
                            par = kcc % 3

                            def op_av(t, kcc=kcc, par=par, w=r_done[par], h=h):
                                sem_h = self.s_sc.h if w[0] == "s" else self.s_v.h
                                t.wait_ge(sem_h, w[1])
                                for half in range(2):
                                    kc = 2 * kcc + half
                                    ins = t.matmul(p_o[:, :], vaug[:, kc, h * 65:(h + 1) * 65],
                                                   fsb[:, par, half * 512:(half + 1) * 512],
                                                   start=(kc == 0), stop=False)
                                ins.then_inc(self.s_mm.h, 1)
                            self.s_mm.n += 1
                            t_ops.append(op_av)
                            av_done[par] = self.s_mm.n

                        for kcc in range(16):
                            par = kcc % 3
                            pd = PD[par]
                            use_sc = (kcc * NS_SC) // 16 != ((kcc + 1) * NS_SC) // 16

                            def op_dot(t, kcc=kcc, par=par, pd=pd, w=r_done[par],
                                       pocv=po_cons_v, qc=qc):
                                if w is not None:
                                    sem_h = self.s_sc.h if w[0] == "s" else self.s_v.h
                                    t.wait_ge(sem_h, w[1])
                                if kcc == 0:
                                    t.wait_ge(self.s_sc.h, n_wneg)
                                    t.wait_ge(self.s_v.h, pocv)
                                for half in range(2):
                                    kc = 2 * kcc + half
                                    ins = t.matmul(pd[:, half * 512:(half + 1) * 512],
                                                   kn[base:base + 64, kc * 128:(kc + 1) * 128],
                                                   qn[base:base + 64, qc * 512:(qc + 1) * 512],
                                                   start=True, stop=True)
                                ins.then_inc(self.s_mm.h, 1)
                            self.s_mm.n += 1
                            t_ops.append(op_dot)
                            t_dot[kcc] = self.s_mm.n

                            if use_sc:
                                def op_l(s, par=par, pd=pd, w=t_dot[kcc]):
                                    s.wait_ge(self.s_mm.h, w)
                                    ins = s.activation(pd[:, :], pd[:, :], AF.Ln, bias=1.1)
                                    ins.then_inc(self.s_sc.h, 1)
                                self.s_sc.n += 1
                                s_ops.append(op_l)

                                def op_x(s, par=par, pd=pd, wav=av_done[par]):
                                    if wav is not None:
                                        s.wait_ge(self.s_mm.h, wav)
                                    ins = s.activation(fsb[:, par, :], pd[:, :], AF.Exp, scale=-1.0)
                                    ins.then_inc(self.s_sc.h, 1)
                                self.s_sc.n += 1
                                s_ops.append(op_x)
                                r_done[par] = ("s", self.s_sc.n)
                            else:
                                def op_u(v, par=par, pd=pd, w=t_dot[kcc]):
                                    v.wait_ge(self.s_mm.h, w)
                                    ins = v.tensor_scalar(pd[:, :], pd[:, :], 1.1, None, ALU.add)
                                    ins.then_inc(self.s_v.h, 1)
                                self.s_v.n += 1
                                v_ops.append(op_u)

                                def op_r(v, par=par, pd=pd, wav=av_done[par]):
                                    if wav is not None:
                                        v.wait_ge(self.s_mm.h, wav)
                                    with self.nc.allow_low_precision(
                                            reason="r=1/u feeds bf16 matmul operands, like prior bf16 scores"):
                                        ins = v.reciprocal(fsb[:, par, :], pd[:, :])
                                    ins.then_inc(self.s_v.h, 1)
                                self.s_v.n += 1
                                v_ops.append(op_r)
                                r_done[par] = ("v", self.s_v.n)

                            if kcc >= 2:
                                emit_av(kcc - 2)
                        emit_av(14)
                        emit_av(15)

                        def op_corr(t, qc=qc):
                            ins = t.matmul(p_o[:, :], wneg[base:base + 64, :],
                                           qn[base:base + 64, qc * 512:(qc + 1) * 512],
                                           start=False, stop=True)
                            ins.then_inc(self.s_mm.h, 1)
                        self.s_mm.n += 1
                        t_ops.append(op_corr)
                        n_corr = self.s_mm.n

                        self.tn(lambda t, ops=t_ops: [f(t) for f in ops])
                        self.vc(lambda v, ops=v_ops: [f(v) for f in ops])
                        self.sc(lambda s, ops=s_ops: [f(s) for f in ops])

                        def v_dn(v, n_corr=n_corr, qc=qc):
                            v.wait_ge(self.s_mm.h, n_corr)
                            self.s_v.inc(v.tensor_scalar(rdb[:, :], p_o[64:65, :], CDEN, None, ALU.add), 1)
                            v.wait_ge(self.s_v.h, self.s_v.n)
                            self.s_v.inc(v.reciprocal(rdb[:, :], rdb[:, :]), 1)
                            self.s_v.inc(v.tensor_scalar(osb[:, :], p_o[0:64, :], ccol[0:64, :],
                                                         None, ALU.add), 1)
                        self.vc(v_dn)
                        n_dn = self.s_v.n

                        def bcd(t, n_dn=n_dn):
                            t.wait_ge(self.s_v.h, n_dn)
                            self.s_mm.inc(t.matmul(p_bc[:, 0:512], self.k["ones_row_f32"][:, :],
                                                   rdb[:, :], start=True, stop=True), 1)
                        self.tn(bcd)
                        n_bcd = self.s_mm.n

                        def v_onorm(v, n_bcd=n_bcd, qc=qc):
                            v.wait_ge(self.s_mm.h, n_bcd)
                            v.wait_ge(self.st_o[qc % 2].h, self.st_o[qc % 2].n)
                            self.s_v.inc(v.tensor_mul(obuf[:, qc % 2, :], osb[:, :], p_bc[0:64, 0:512]), 1)
                        self.vc(v_onorm)
                        n_on = self.s_v.n
                        po_cons_v = self.s_v.n

                        def stob(sy, n_on=n_on, h=h, qc=qc):
                            sy.wait_ge(self.s_v.h, n_on)
                            self.st_o[qc % 2].inc(sy.dma_start(
                                out=self.attn_sp[h * 64:(h + 1) * 64, qc * 512:(qc + 1) * 512],
                                in_=obuf[:, qc % 2, :]), 16)
                        self.sy(stob)

            def ldwo(sy):
                sy.wait_ge(self.s_mm.h, self.s_mm.n)
                self.ld_s.inc(sy.dma_start(out=wsb[:, :, :],
                                           in_=wo_bf.rearrange("(a p) m -> p a m", p=128)), 16)
            self.sy(ldwo)
            n_lwo = self.ld_s.n
            att = f2f.rearrange("p (a t) -> p a t", a=4)
            attc = fsq.rearrange("p (a t) -> p a t", a=4)
            attc2 = qn.rearrange("p (a t) -> p a t", a=4)
            for c in range(S // T):
                def lda(sy):
                    sy.wait_ge(self.s_v.h, self.s_v.n)
                    sy.wait_ge(self.s_sc.h, self.s_sc.n)
                    sy.wait_ge(self.s_mm.h, self.s_mm.n)
                    ar = self.attn_sp.rearrange("(a p) t -> a p t", p=128)
                    xr = xin.rearrange("(a p) t -> a p t", p=128)
                    for ft in range(NFT):
                        dst = attc[:, ft, :] if ft < 4 else attc2[:, ft - 4, :]
                        self.ld_x.inc(sy.dma_start(out=dst, in_=ar[ft, :, c * T:(c + 1) * T]), 16)
                    for ft in range(NFT):
                        self.ld_x.inc(sy.dma_start(out=xs[:, ft, :], in_=xr[ft, :, c * T:(c + 1) * T]), 16)
                self.sy(lda)
                n_la = self.ld_x.n
                for dt in range(8):
                    for nch in range(2):
                        pzX = pz[2 * (dt % 2) + nch]

                        def omm(t):
                            t.wait_ge(self.ld_x.h, n_la)
                            t.wait_ge(self.ld_s.h, n_lwo)
                            t.wait_ge(self.s_v.h, self.s_v.n)
                            ins = None
                            for kt in range(NFT):
                                rhs = attc[:, kt, nch * 512:(nch + 1) * 512] if kt < 4 else \
                                    attc2[:, kt - 4, nch * 512:(nch + 1) * 512]
                                ins = t.matmul(pzX[:, :], wsb[:, kt, dt * 128:(dt + 1) * 128], rhs,
                                               start=(kt == 0), stop=(kt == NFT - 1))
                            self.s_mm.inc(ins, 1)
                        self.tn(omm)
                        n_omm = self.s_mm.n

                        def oadd(v):
                            v.wait_ge(self.s_mm.h, n_omm)
                            v.wait_ge(self.st_o[nch].h, self.st_o[nch].n)
                            self.s_v.inc(v.scalar_tensor_tensor(
                                ostg[:, nch, :], pzX[:, :], cols["bo"][:, dt:dt + 1],
                                xs[:, dt, nch * 512:(nch + 1) * 512], ALU.add, ALU.add), 1)
                        self.vc(oadd)
                        n_oadd = self.s_v.n

                        def ost(sy):
                            sy.wait_ge(self.s_v.h, n_oadd)
                            self.st_o[nch].inc(sy.dma_start(
                                out=xout[dt * 128:(dt + 1) * 128,
                                         c * T + nch * 512:c * T + (nch + 1) * 512],
                                in_=ostg[:, nch, :]), 16)
                        self.sy(ost)

    def emit_attn1(self, xin, xout):
        T = 1024
        wqT = self.wT["L1_attn_Wq_w"]; wkT = self.wT["L1_attn_Wk_w"]
        wvT = self.wT["L1_attn_Wv_w"]; woT = self.wT["L1_attn_out_w"]
        qpT = self.wT["L1_attn_qproj_w"]; kpT = self.wT["L1_attn_kproj_w"]
        cqT = self.wT["L1_attn_convq_w"]; ckT = self.wT["L1_attn_convk_w"]
        t2T = self.wT["L1_attn_tau2_w"]; d2T = self.wT["L1_attn_del2_w"]
        cols = self.cols
        with ExitStack() as st:
            n1 = self.sb(st, "a1_n1", [128, NFT, S], BF16)
            BIG = self.sb(st, "a1_big", [128, NTT * H * 65], BF16)
            vaug = BIG.rearrange("p (a c) -> p a c", a=NTT)
            q_in = BIG[:, 0:NFT * S].rearrange("p (a t) -> p a t", a=NFT)
            dwg = BIG[:, 0:16 * 1024].rearrange("p (a t) -> p a t", a=16)
            BIG_f = BIG.bitcast(F32)
            xs = BIG_f[:, 8192:16384].rearrange("p (a t) -> p a t", a=NFT)
            xbf = BIG[:, 0:NFT * T].rearrange("p (a t) -> p a t", a=NFT)
            xsq = BIG[:, NFT * T:2 * NFT * T].rearrange("p (a t) -> p a t", a=NFT)
            q3buf = self.sb(st, "a1_q3", [128, 8192], BF16)
            q3 = q3buf.rearrange("p (a t) -> p a t", a=8)[:, :, 0:1024]
            q3v = q3buf.rearrange("p (a t) -> p a t", a=8)
            qha = q3buf[0:66, 0:4096]
            kha = q3buf[0:66, 4096:8192]
            wsb = self.sb(st, "a1_wsb", [128, NFT, D], BF16)
            wblk = [self.sb(st, f"a1_wb{i}", [128, 24, 128], BF16) for i in range(2)]
            t2blk = self.sb(st, "a1_t2b", [128, 16, 16], BF16)
            accb = self.sb(st, "a1_acc", [128, T], F32)
            drows = self.sb(st, "a1_dr", [128, T], F32)
            tau_bf = self.sb(st, "a1_tau", [16, S], BF16)
            dstage = drows[0:16, :]
            taubc_sb = drows.bitcast(BF16)[:, 0:1024].rearrange("p (a t) -> p a t", a=2)
            bias_b = self.sb(st, "a1_bb", [128, D], F32)
            brow = drows[0:1, 0:D]
            wblk1_flat = wblk[1].reshape([128, 24 * 128])
            fsb = wblk1_flat[:, 512:2560].rearrange("p (a t) -> p a t", a=4)
            osb = accb[0:64, 512:1024]
            obuf = bias_b.bitcast(BF16)[0:64, 0:1024].rearrange("p (a t) -> p a t", a=2)
            rdb = accb[0:1, 0:512]
            q3_f = q3buf.bitcast(F32)
            rows = q3_f[0:1, 0:2560]
            ttile = q3_f[:, 2560:3584].rearrange("p (a t) -> p a t", a=2)
            ostg = wblk[0].reshape([128, 24 * 128]).bitcast(F32)[:, 0:1024].rearrange(
                "p (a t) -> p a t", a=2)
            qstg = self.sb(st, "a1_qs", [128, 2, 512], BF16)
            p_s1 = self.psum(st, "a1_ps1", [1, 512], F32)
            p_s2 = self.psum(st, "a1_ps2", [1, 512], F32)
            p_bc = self.psum(st, "a1_pbc", [128, 1024], F32)
            pz = [self.psum(st, f"a1_pz{i}", [128, 512], F32) for i in range(4)]
            p_t = p_bc[0:16, 0:512]
            p_tr = p_bc[0:128, 512:1024]
            p_o = self.psum(st, "a1_po", [65, 512], F32) if False else None
            p_o = pz[3][0:65, :]

            for c in range(S // T):
                def ldx(sy):
                    sy.wait_ge(self.s_v.h, self.s_v.n)
                    sy.wait_ge(self.s_sc.h, self.s_sc.n)
                    xr = xin.rearrange("(a p) t -> a p t", p=128)
                    for ft in range(NFT):
                        self.ld_x.inc(sy.dma_start(out=xs[:, ft, :], in_=xr[ft, :, c * T:(c + 1) * T]), 16)
                self.sy(ldx)
                nld = self.ld_x.n

                def wx(e):
                    e.wait_ge(self.ld_x.h, nld)
                    e.wait_ge(self.s_mm.h, self.s_mm.n)
                self.sc(wx)
                self.vc(lambda v: v.wait_ge(self.ld_x.h, nld))
                nbufc = n1[:, :, c * T:(c + 1) * T]
                self.emit_ln_chunk(xs, nbufc, xbf, xsq, rows, ttile, p_s1, p_s2, p_bc, T)
            n_po_ln = self.s_po.n

            hi_sb = self.sb(st, "a1_dhi", [16, T], BF16)
            lo_sb = self.sb(st, "a1_dlo", [16, T], BF16)

            def ln_done(e):
                e.wait_ge(self.s_po.h, n_po_ln)
            self.tn(ln_done)

            for which, (wcolk, bcolk, w2_, b2c) in enumerate(
                    [("t1w", "t1b", t2T, "t2b"), (("d1w"), ("d1b"), d2T, "d2b")]):
                def ldt2(sy):
                    sy.wait_ge(self.s_mm.h, self.s_mm.n)
                    wv_ = w2_.rearrange("(a p q) h -> a p (q h)", p=128, q=2)
                    for r in range(2):
                        for ct in range(8):
                            self.ld_s.inc(sy.dma_start(out=t2blk[:, r * 8 + ct, :],
                                                       in_=wv_[ct, :, r * 16:(r + 1) * 16]), 16)
                self.sy(ldt2)
                n_lt2 = self.ld_s.n

                for c in range(S // T):
                    g0, g1 = c * T, (c + 1) * T
                    for r in range(2):
                        for ct in range(8):
                            wc = cols[wcolk]
                            col0 = r * 24 + ct * 3

                            def dw(v):
                                v.wait_ge(self.s_sc.h, self.s_sc.n)
                                if which == 0 and c == 0 and r == 0 and ct == 0:
                                    v.wait_ge(self.s_po.h, n_po_ln)
                                self.s_v.inc(v.tensor_scalar(accb[:, :], n1[:, ct, g0:g1],
                                                             wc[:, col0 + 1:col0 + 2], None, ALU.mult), 1)
                                v.wait_ge(self.s_v.h, self.s_v.n)
                                lo = 1 if c == 0 else 0
                                self.s_v.inc(v.scalar_tensor_tensor(
                                    accb[:, lo:T], n1[:, ct, g0 + lo - 1:g1 - 1],
                                    wc[:, col0:col0 + 1], accb[:, lo:T], ALU.mult, ALU.add), 1)
                                v.wait_ge(self.s_v.h, self.s_v.n)
                                hi = T - 1 if c == S // T - 1 else T
                                self.s_v.inc(v.scalar_tensor_tensor(
                                    accb[:, 0:hi], n1[:, ct, g0 + 1:g0 + hi + 1],
                                    wc[:, col0 + 2:col0 + 3], accb[:, 0:hi], ALU.mult, ALU.add), 1)
                            self.vc(dw)
                            n_dw = self.s_v.n

                            def gel(s):
                                s.wait_ge(self.s_v.h, n_dw)
                                s.wait_ge(self.s_mm.h, self.s_mm.n)
                                self.s_sc.inc(s.activation(dwg[:, r * 8 + ct, :], accb[:, :],
                                                           AF.Gelu, bias=cols[bcolk][:, r * 8 + ct:r * 8 + ct + 1]), 1)
                            self.sc(gel)
                    n_gel = self.s_sc.n

                    for nch in range(2):
                        def t2mm(t):
                            t.wait_ge(self.s_sc.h, n_gel)
                            t.wait_ge(self.ld_s.h, n_lt2)
                            t.wait_ge(self.s_sc.h, self.s_sc.n)
                            t.wait_ge(self.s_v.h, self.s_v.n)
                            ins = None
                            for u in range(16):
                                ins = t.matmul(p_t[:, :], t2blk[:, u, :], dwg[:, u, nch * 512:(nch + 1) * 512],
                                               start=(u == 0), stop=(u == 15))
                            self.s_mm.inc(ins, 1)
                        self.tn(t2mm)
                        n_t2 = self.s_mm.n

                        def sig(s):
                            s.wait_ge(self.s_mm.h, n_t2)
                            if which == 0:
                                self.s_sc.inc(s.activation(tau_bf[:, g0 + nch * 512:g0 + (nch + 1) * 512],
                                                           p_t[:, :], AF.Sigmoid,
                                                           bias=cols[b2c][:, 0:1]), 1)
                            else:
                                s.wait_ge(self.s_v.h, self.s_v.n)
                                self.s_sc.inc(s.activation(dstage[:, nch * 512:(nch + 1) * 512],
                                                           p_t[:, :], AF.Sigmoid,
                                                           bias=cols[b2c][:, 0:1]), 1)
                        self.sc(sig)
                    if which == 1:
                        n_sig = self.s_sc.n
                        nst0 = self.st_s.n

                        def d_hi(s):
                            s.wait_ge(self.st_s.h, nst0)
                            self.s_sc.inc(s.activation(hi_sb[:, :], dstage[:, :], AF.Copy), 1)
                        self.sc(d_hi)
                        n_hi = self.s_sc.n

                        def d_lo(v):
                            v.wait_ge(self.s_sc.h, n_hi)
                            self.s_v.inc(v.tensor_sub(lo_sb[:, :], dstage[:, :], hi_sb[:, :]), 1)
                        self.vc(d_lo)
                        n_lo = self.s_v.n

                        def d_st(sy):
                            sy.wait_ge(self.s_v.h, n_lo)
                            self.st_s.inc(sy.dma_start(out=self.d_sp[0][:, g0:g1], in_=hi_sb[:, :]), 16)
                            self.st_s.inc(sy.dma_start(out=self.d_sp[1][:, g0:g1], in_=lo_sb[:, :]), 16)
                        self.sy(d_st)

            wb_cons = {0: None, 1: None}
            pzq_cons = {0: None, 1: None, 2: None, 3: None}
            n_p1_sc = self.s_sc.n
            n_p1_v = self.s_v.n
            tbc_ev = {0: n_p1_sc, 1: n_p1_sc}
            tbc_cons = {0: None, 1: None}
            for which, (wT_, bcol, cT_, cbcol, pT_, pbcol, spill) in enumerate(
                    [(wqT, "qb", cqT, "cqb", qpT, "qpb", self.q_sp),
                     (wkT, "kb", ckT, "ckb", kpT, "kpb", self.k_sp)]):
                for mt in range(8):
                    par = mt % 2
                    wcc = wb_cons[par]

                    def ldw(g, wT_=wT_, par=par, wcc=wcc, mt=mt):
                        if wcc is not None:
                            g.wait_ge(self.s_mm.h, wcc)
                        sv = wT_.rearrange("(a p) m -> p a m", p=128)
                        self.ld_w[par].inc(g.dma_start(out=wblk[par][:, 0:8, :],
                                                       in_=sv[:, :, mt * 128:(mt + 1) * 128]), 16)
                    self.gp(ldw)
                    nw = self.ld_w[par].n
                    for qc in range(8):
                        bank = qc % 4
                        pzX = pz[bank]
                        pcw = pzq_cons[bank]

                        def mmq(t, par=par, nw=nw, pcw=pcw, which=which, mt=mt, qc=qc, pzX=pzX):
                            t.wait_ge(self.ld_w[par].h, nw)
                            if pcw is not None:
                                t.wait_ge(self.s_v.h, pcw)
                            if which == 0 and mt == 0 and qc == 0:
                                t.wait_ge(self.s_sc.h, self.s_sc.n)
                                t.wait_ge(self.s_v.h, self.s_v.n)
                            ins = None
                            for kt in range(NFT):
                                ins = t.matmul(pzX[:, :], wblk[par][:, kt, :],
                                               n1[:, kt, qc * 512:(qc + 1) * 512],
                                               start=(kt == 0), stop=(kt == NFT - 1))
                            self.s_mm.inc(ins, 1)
                        self.tn(mmq)
                        n_mmq = self.s_mm.n

                        def evq(v, n_mmq=n_mmq, mt=mt, qc=qc, pzX=pzX, bcol=bcol):
                            v.wait_ge(self.s_mm.h, n_mmq)
                            self.s_v.inc(v.tensor_scalar(q_in[:, mt, qc * 512:(qc + 1) * 512],
                                                         pzX[:, :], cols[bcol][:, mt:mt + 1],
                                                         None, ALU.add), 1)
                        self.vc(evq)
                        pzq_cons[bank] = self.s_v.n
                    wb_cons[par] = self.s_mm.n

                n_qin_done = self.s_v.n
                for c in range(S // T):
                    g0, g1 = c * T, (c + 1) * T
                    for ot in range(8):
                        par = ot % 2
                        wcc = wb_cons[par]

                        def ldc(g, cT_=cT_, par=par, wcc=wcc, ot=ot):
                            if wcc is not None:
                                g.wait_ge(self.s_mm.h, wcc)
                            for tap in range(2):
                                sv = cT_[tap].rearrange("(a p) m -> p a m", p=128)
                                self.ld_w[par].inc(g.dma_start(out=wblk[par][:, 8 * tap:8 * tap + 8, :],
                                                               in_=sv[:, :, ot * 128:(ot + 1) * 128]), 16)
                        self.gp(ldc)

                        def ldc2(sc_, cT_=cT_, par=par, wcc=wcc, ot=ot):
                            if wcc is not None:
                                sc_.wait_ge(self.s_mm.h, wcc)
                            sv = cT_[2].rearrange("(a p) m -> p a m", p=128)
                            self.ld_c2[par].inc(sc_.dma_start(out=wblk[par][:, 16:24, :],
                                                              in_=sv[:, :, ot * 128:(ot + 1) * 128]), 16)
                        self.sc(ldc2)
                        nwc = self.ld_w[par].n
                        nwc2 = self.ld_c2[par].n
                        for nch in range(2):
                            bank = nch
                            pzX = pz[bank]
                            pcw = pzq_cons[bank]
                            o0 = g0 + nch * 512

                            def mmc(t, par=par, nwc=nwc, nwc2=nwc2, pcw=pcw, o0=o0, pzX=pzX,
                                    c=c, nq=n_qin_done):
                                t.wait_ge(self.ld_w[par].h, nwc)
                                t.wait_ge(self.ld_c2[par].h, nwc2)
                                if pcw is not None:
                                    t.wait_ge(self.s_v.h, pcw)
                                else:
                                    t.wait_ge(self.s_v.h, nq)
                                ins = None
                                n_total = 3 * NFT
                                i_mm = 0
                                for tap in (1, 0, 2):
                                    sh = tap - 1
                                    lo = max(0, -(o0 + sh))
                                    hi = min(512, S - (o0 + sh))
                                    for kt in range(NFT):
                                        ins = t.matmul(pzX[:, lo:hi], wblk[par][:, 8 * tap + kt, :],
                                                       q_in[:, kt, o0 + sh + lo:o0 + sh + hi],
                                                       start=(i_mm == 0), stop=(i_mm == n_total - 1))
                                        i_mm += 1
                                self.s_mm.inc(ins, 1)
                            self.tn(mmc)
                            n_mmc = self.s_mm.n

                            def evc(v, n_mmc=n_mmc, ot=ot, nch=nch, pzX=pzX, cbcol=cbcol):
                                v.wait_ge(self.s_mm.h, n_mmc)
                                self.s_v.inc(v.tensor_scalar(q3[:, ot, nch * 512:(nch + 1) * 512],
                                                             pzX[:, :], cols[cbcol][:, ot:ot + 1],
                                                             None, ALU.add), 1)
                            self.vc(evc)
                            pzq_cons[bank] = self.s_v.n
                        wb_cons[par] = self.s_mm.n
                    n_q3_done = self.s_v.n
                    for mt in range(8):
                        par = mt % 2
                        wcc = wb_cons[par]

                        def ldp(g, pT_=pT_, par=par, wcc=wcc, mt=mt):
                            if wcc is not None:
                                g.wait_ge(self.s_mm.h, wcc)
                            sv = pT_.rearrange("(a p) m -> p a m", p=128)
                            self.ld_w[par].inc(g.dma_start(out=wblk[par][:, 0:16, :],
                                                           in_=sv[:, :, mt * 128:(mt + 1) * 128]), 16)
                        self.gp(ldp)
                        nwp = self.ld_w[par].n
                        for nch in range(2):
                            bank = 2 + nch
                            pzX = pz[bank]
                            pcw = pzq_cons[bank]

                            if which == 0:
                                tprev = tbc_ev[nch]

                                def tbc_mm(t, mt=mt, nch=nch, g0=g0, tprev=tprev):
                                    if tprev is not None:
                                        t.wait_ge(self.s_sc.h, tprev)
                                    ins = t.matmul(p_bc[:, nch * 512:(nch + 1) * 512],
                                                   self.k["sel_pair_8th"][:, mt * 128:(mt + 1) * 128],
                                                   tau_bf[:, g0 + nch * 512:g0 + (nch + 1) * 512],
                                                   start=True, stop=True)
                                    ins.then_inc(self.s_mm.h, 1)
                                self.s_mm.n += 1
                                self.tn(tbc_mm)
                                n_tbc = self.s_mm.n
                                tcons = tbc_cons[nch]

                                def tbc_ev_f(s, nch=nch, n_tbc=n_tbc, tcons=tcons):
                                    s.wait_ge(self.s_mm.h, n_tbc)
                                    s.wait_ge(self.s_v.h, n_p1_v if tcons is None else tcons)
                                    self.s_sc.inc(s.activation(taubc_sb[:, nch, :],
                                                               p_bc[:, nch * 512:(nch + 1) * 512],
                                                               AF.Copy), 1)
                                self.sc(tbc_ev_f)
                                tbc_ev[nch] = self.s_sc.n

                            def mmp(t, par=par, nwp=nwp, pcw=pcw, g0=g0, nch=nch, pzX=pzX,
                                    nq3=n_q3_done):
                                t.wait_ge(self.ld_w[par].h, nwp)
                                t.wait_ge(self.s_v.h, nq3 if pcw is None else max(pcw, nq3))
                                ins = None
                                for kt in range(NFT):
                                    ins = t.matmul(pzX[:, :], wblk[par][:, kt, :],
                                                   q_in[:, kt, g0 + nch * 512:g0 + (nch + 1) * 512],
                                                   start=(kt == 0), stop=False)
                                for kt in range(NFT):
                                    ins = t.matmul(pzX[:, :], wblk[par][:, 8 + kt, :],
                                                   q3[:, kt, nch * 512:(nch + 1) * 512],
                                                   start=False, stop=(kt == NFT - 1))
                                self.s_mm.inc(ins, 1)
                            self.tn(mmp)
                            n_mmp = self.s_mm.n

                            if which == 0:
                                n_tev = tbc_ev[nch]

                                def evp(v, n_mmp=n_mmp, nch=nch, pzX=pzX, pbcol=pbcol, mt=mt,
                                        n_tev=n_tev):
                                    v.wait_ge(self.s_mm.h, n_mmp)
                                    v.wait_ge(self.s_sc.h, n_tev)
                                    v.wait_ge(self.st_o[nch].h, self.st_o[nch].n)
                                    self.s_v.inc(v.scalar_tensor_tensor(
                                        qstg[:, nch, :], pzX[:, :], cols[pbcol][:, mt:mt + 1],
                                        taubc_sb[:, nch, :], ALU.add, ALU.mult), 1)
                                self.vc(evp)
                                tbc_cons[nch] = self.s_v.n
                            else:
                                def evp(v, n_mmp=n_mmp, nch=nch, pzX=pzX, pbcol=pbcol, mt=mt):
                                    v.wait_ge(self.s_mm.h, n_mmp)
                                    v.wait_ge(self.st_o[nch].h, self.st_o[nch].n)
                                    self.s_v.inc(v.tensor_scalar(qstg[:, nch, :],
                                                                 pzX[:, :], cols[pbcol][:, mt:mt + 1],
                                                                 None, ALU.add), 1)
                                self.vc(evp)
                            n_evp = self.s_v.n
                            pzq_cons[bank] = self.s_v.n

                            def stp(sy, n_evp=n_evp, mt=mt, g0=g0, nch=nch, spill=spill):
                                sy.wait_ge(self.s_v.h, n_evp)
                                self.st_o[nch].inc(sy.dma_start(
                                    out=spill[mt * 128:(mt + 1) * 128, g0 + nch * 512:g0 + (nch + 1) * 512],
                                    in_=qstg[:, nch, :]), 16)
                            self.sy(stp)
                        wb_cons[par] = self.s_mm.n

            def ldwv(sy):
                sy.wait_ge(self.s_mm.h, self.s_mm.n)
                sy.wait_ge(self.s_v.h, self.s_v.n)
                self.ld_s.inc(sy.dma_start(out=wsb[:, :, :],
                                           in_=wvT.rearrange("(a p) m -> p a m", p=128)), 16)
                self.ld_s.inc(sy.dma_start(out=brow[:, :],
                                           in_=self.win["L1_attn_Wv_b"].rearrange("(o d) -> o d", o=1)), 16)
            self.sy(ldwv)
            n_ldv = self.ld_s.n

            def bias_mm(t):
                t.wait_ge(self.ld_s.h, n_ldv)
                t.wait_ge(self.s_sc.h, self.s_sc.n)
                t.wait_ge(self.s_v.h, self.s_v.n)
                t.matmul(p_bc[:, 0:512], self.k["ones_row_f32"][:, :], brow[:, 0:512], start=True, stop=True)
                ins = t.matmul(p_bc[:, 512:1024], self.k["ones_row_f32"][:, :], brow[:, 512:1024],
                               start=True, stop=True)
                self.s_mm.inc(ins, 1)
            self.tn(bias_mm)
            n_bmm = self.s_mm.n

            def bias_ev(v):
                v.wait_ge(self.s_mm.h, n_bmm)
                self.s_v.inc(v.tensor_copy(bias_b[:, :], p_bc[:, :]), 1)
                self.s_v.inc(v.memset(vaug.rearrange("p a (h c) -> p a h c", c=65)[:, :, :, 64:65], 1.0), 1)
            self.vc(bias_ev)
            n_bev = self.s_v.n

            for tt in range(NTT):
                def vmm(t):
                    if tt == 0:
                        t.wait_ge(self.s_v.h, n_bev)
                    t.wait_ge(self.s_v.h, self.s_v.n)
                    ins = None
                    for half in range(2):
                        for kt in range(NFT):
                            ins = t.matmul(pz[half][:, :], n1[:, kt, tt * 128:(tt + 1) * 128],
                                           wsb[:, kt, half * 512:(half + 1) * 512],
                                           start=(kt == 0), stop=(kt == NFT - 1))
                    self.s_mm.inc(ins, 1)
                self.tn(vmm)
                n_vmm = self.s_mm.n

                def vev(v):
                    v.wait_ge(self.s_mm.h, n_vmm)
                    dst = vaug[:, tt, :].rearrange("p (h c) -> p h c", c=65)[:, :, 0:64]
                    src0 = pz[0].rearrange("p (h c) -> p h c", c=64)
                    src1 = pz[1].rearrange("p (h c) -> p h c", c=64)
                    bb0 = bias_b[:, 0:512].rearrange("p (h c) -> p h c", c=64)
                    bb1 = bias_b[:, 512:1024].rearrange("p (h c) -> p h c", c=64)
                    v.tensor_add(dst[:, 0:8, :], src0[:, :, :], bb0[:, :, :])
                    self.s_v.inc(v.tensor_add(dst[:, 8:16, :], src1[:, :, :], bb1[:, :, :]), 1)
                self.vc(vev)

            p_o2 = p_bc[0:65, 512:1024]

            def qk_init(v):
                v.wait_ge(self.s_mm.h, self.s_mm.n)
                v.wait_ge(self.s_sc.h, self.s_sc.n)
                self.s_v.inc(v.memset(q3buf[64:66, 0:4096], 1.0), 1)
            self.vc(qk_init)
            n_qinit = self.s_v.n

            e_done = {i: None for i in range(4)}
            av_cons = {i: None for i in range(4)}
            po_cons_v = self.s_v.n
            for h in range(H):
                def ldqk(sy, h=h):
                    sy.wait_ge(self.s_mm.h, self.s_mm.n)
                    if h == 0:
                        sy.wait_ge(self.s_v.h, self.s_v.n)
                        sy.wait_ge(self.st_s.h, self.st_s.n)
                    sy.wait_ge(self.st_o[0].h, self.st_o[0].n)
                    sy.wait_ge(self.st_o[1].h, self.st_o[1].n)
                    self.ld_x.inc(sy.dma_start(out=qha[0:64, :], in_=self.q_sp[h * 64:(h + 1) * 64, :]), 16)
                    self.ld_x.inc(sy.dma_start(out=kha[0:64, :], in_=self.k_sp[h * 64:(h + 1) * 64, :]), 16)
                    self.ld_x.inc(sy.dma_start(out=kha[64:65, :], in_=self.d_sp[0][h:h + 1, :]), 16)
                    self.ld_x.inc(sy.dma_start(out=kha[65:66, :], in_=self.d_sp[1][h:h + 1, :]), 16)
                self.sy(ldqk)
                n_lqk = self.ld_x.n
                for qc in range(8):
                    t_ops, s_ops = [], []
                    t_dot = {}

                    def emit_av1(kc):
                        slot = kc % 4

                        def op_av(t, kc=kc, slot=slot, w=e_done[slot], h=h, pv=po_cons_v):
                            t.wait_ge(self.s_sc.h, w)
                            if kc == 0:
                                t.wait_ge(self.s_v.h, pv)
                            ins = t.matmul(p_o2[:, :], vaug[:, kc, h * 65:(h + 1) * 65],
                                           fsb[:, slot, :], start=(kc == 0), stop=(kc == NTT - 1))
                            ins.then_inc(self.s_mm.h, 1)
                        self.s_mm.n += 1
                        t_ops.append(op_av)
                        av_cons[slot] = self.s_mm.n

                    for kc in range(NTT):
                        slot = kc % 4
                        pd = pz[slot]

                        def op_dot(t, kc=kc, pd=pd, w=e_done[kc % 4], qc=qc, n_lqk=n_lqk):
                            if w is not None:
                                t.wait_ge(self.s_sc.h, w)
                            if kc == 0 and qc == 0:
                                t.wait_ge(self.ld_x.h, n_lqk)
                            ins = t.matmul(pd[:, :], kha[:, kc * 128:(kc + 1) * 128],
                                           qha[:, qc * 512:(qc + 1) * 512], start=True, stop=True)
                            ins.then_inc(self.s_mm.h, 1)
                        self.s_mm.n += 1
                        t_ops.append(op_dot)
                        t_dot[kc] = self.s_mm.n

                        def op_e(s, kc=kc, slot=slot, pd=pd, wmm=t_dot[kc], wav=av_cons[slot]):
                            s.wait_ge(self.s_mm.h, wmm if wav is None else max(wmm, wav))
                            ins = s.activation(fsb[:, slot, :], pd[:, :], AF.Exp)
                            ins.then_inc(self.s_sc.h, 1)
                        self.s_sc.n += 1
                        s_ops.append(op_e)
                        e_done[slot] = self.s_sc.n

                        if kc >= 3:
                            emit_av1(kc - 3)
                    for kc in (NTT - 3, NTT - 2, NTT - 1):
                        emit_av1(kc)
                    n_av = self.s_mm.n

                    self.tn(lambda t, ops=t_ops: [f(t) for f in ops])
                    self.sc(lambda s, ops=s_ops: [f(s) for f in ops])

                    def v_dn(v, n_av=n_av):
                        v.wait_ge(self.s_mm.h, n_av)
                        self.s_v.inc(v.reciprocal(rdb[:, :], p_o2[64:65, :]), 1)
                    self.vc(v_dn)
                    n_dn = self.s_v.n

                    def bcd(t, n_dn=n_dn):
                        t.wait_ge(self.s_v.h, n_dn)
                        self.s_mm.inc(t.matmul(p_bc[:, 0:512], self.k["ones_row_f32"][:, :], rdb[:, :],
                                               start=True, stop=True), 1)
                    self.tn(bcd)
                    n_bcd = self.s_mm.n

                    def s_oev(s, n_av=n_av):
                        s.wait_ge(self.s_mm.h, n_av)
                        s.wait_ge(self.s_v.h, self.s_v.n)
                        self.s_sc.inc(s.activation(osb[:, :], p_o2[0:64, :], AF.Copy), 1)
                    self.sc(s_oev)
                    n_oev = self.s_sc.n

                    def v_onorm(v, n_bcd=n_bcd, n_oev=n_oev, qc=qc):
                        v.wait_ge(self.s_mm.h, n_bcd)
                        v.wait_ge(self.s_sc.h, n_oev)
                        v.wait_ge(self.st_o[qc % 2].h, self.st_o[qc % 2].n)
                        self.s_v.inc(v.tensor_mul(obuf[:, qc % 2, :], osb[:, :], p_bc[0:64, 0:512]), 1)
                    self.vc(v_onorm)
                    n_on = self.s_v.n
                    po_cons_v = self.s_v.n

                    def stob(sy, n_on=n_on, h=h, qc=qc):
                        sy.wait_ge(self.s_v.h, n_on)
                        self.st_o[qc % 2].inc(sy.dma_start(
                            out=self.attn_sp[h * 64:(h + 1) * 64, qc * 512:(qc + 1) * 512],
                            in_=obuf[:, qc % 2, :]), 16)
                    self.sy(stob)

            def ldwo(sy):
                sy.wait_ge(self.s_mm.h, self.s_mm.n)
                self.ld_s.inc(sy.dma_start(out=wsb[:, :, :],
                                           in_=woT.rearrange("(a p) m -> p a m", p=128)), 16)
            self.sy(ldwo)
            n_lwo = self.ld_s.n
            attc = q3v
            for c in range(S // T):
                def lda(sy):
                    sy.wait_ge(self.s_v.h, self.s_v.n)
                    sy.wait_ge(self.s_sc.h, self.s_sc.n)
                    sy.wait_ge(self.s_mm.h, self.s_mm.n)
                    ar = self.attn_sp.rearrange("(a p) t -> a p t", p=128)
                    xr = xin.rearrange("(a p) t -> a p t", p=128)
                    for ft in range(NFT):
                        self.ld_x.inc(sy.dma_start(out=attc[:, ft, :], in_=ar[ft, :, c * T:(c + 1) * T]), 16)
                    for ft in range(NFT):
                        self.ld_x.inc(sy.dma_start(out=xs[:, ft, :], in_=xr[ft, :, c * T:(c + 1) * T]), 16)
                self.sy(lda)
                n_la = self.ld_x.n
                for dt in range(8):
                    for nch in range(2):
                        pzX = pz[2 * (dt % 2) + nch]

                        def omm(t):
                            t.wait_ge(self.ld_x.h, n_la)
                            t.wait_ge(self.ld_s.h, n_lwo)
                            t.wait_ge(self.s_v.h, self.s_v.n)
                            ins = None
                            for kt in range(NFT):
                                ins = t.matmul(pzX[:, :], wsb[:, kt, dt * 128:(dt + 1) * 128],
                                               attc[:, kt, nch * 512:(nch + 1) * 512],
                                               start=(kt == 0), stop=(kt == NFT - 1))
                            self.s_mm.inc(ins, 1)
                        self.tn(omm)
                        n_omm = self.s_mm.n

                        def oadd(v):
                            v.wait_ge(self.s_mm.h, n_omm)
                            v.wait_ge(self.st_o[nch].h, self.st_o[nch].n)
                            self.s_v.inc(v.scalar_tensor_tensor(
                                ostg[:, nch, :], pzX[:, :], cols["ob"][:, dt:dt + 1],
                                xs[:, dt, nch * 512:(nch + 1) * 512], ALU.add, ALU.add), 1)
                        self.vc(oadd)
                        n_oadd = self.s_v.n

                        def ost(sy):
                            sy.wait_ge(self.s_v.h, n_oadd)
                            self.st_o[nch].inc(sy.dma_start(
                                out=xout[dt * 128:(dt + 1) * 128,
                                         c * T + nch * 512:c * T + (nch + 1) * 512],
                                in_=ostg[:, nch, :]), 16)
                        self.sy(ost)

    def _emit_output(self, xcur):
        with ExitStack() as st:
            xt = self.sb(st, "fo_xt", [128, NFT, 128], F32)
            ot = self.sb(st, "fo_ot", [128, D], F32)
            pft = self.psum(st, "fo_p1", [128, 512], F32)
            pft2 = self.psum(st, "fo_p2", [128, 512], F32)
            xr = xcur.rearrange("(a p) t -> p a t", p=128)
            for tt in range(NTT):
                def ld(sy):
                    sy.wait_ge(self.s_mm.h, self.s_mm.n)
                    self.ld_s.inc(sy.dma_start(out=xt[:, :, :], in_=xr[:, :, tt * 128:(tt + 1) * 128]), 16)
                self.sy(ld)
                n0 = self.ld_s.n

                def tr(t):
                    t.wait_ge(self.ld_s.h, n0)
                    t.wait_ge(self.s_v.h, self.s_v.n)
                    ins = None
                    for ft in range(NFT):
                        p = pft if ft < 4 else pft2
                        ins = t.transpose(p[:, (ft % 4) * 128:(ft % 4 + 1) * 128], xt[:, ft, :],
                                          self.k["id_f32"][:])
                    self.s_mm.inc(ins, 1)
                self.tn(tr)
                n1 = self.s_mm.n

                def ev(v):
                    v.wait_ge(self.s_mm.h, n1)
                    v.wait_ge(self.st_s.h, self.st_s.n)
                    v.tensor_copy(ot[:, 0:512], pft[:, :])
                    self.s_v.inc(v.tensor_copy(ot[:, 512:1024], pft2[:, :]), 1)
                self.vc(ev)
                nv = self.s_v.n

                def stv(sy):
                    sy.wait_ge(self.s_v.h, nv)
                    self.st_s.inc(sy.dma_start(out=self.y[tt * 128:(tt + 1) * 128, :], in_=ot[:, :]), 16)
                self.sy(stv)


def kernel(**inputs):
    P = Prog()
    x = np.ascontiguousarray(inputs["L0_attn_x"], dtype=np.float32)
    in_maps = []
    for b in range(8):
        m = {"x0": np.ascontiguousarray(x[b])}
        for name in P.win:
            m[name] = np.ascontiguousarray(inputs[name], dtype=np.float32)
        in_maps.append(m)
    res = run_bass_kernel_spmd(P.nc, in_maps, core_ids=list(range(8)))
    return np.stack([r["y"] for r in res.results], axis=0)


# version 13
# speedup vs baseline: 1.0266x; 1.0266x over previous
import numpy as np
import ml_dtypes
from contextlib import ExitStack

import concourse.bass as bass
import concourse.mybir as mybir
from concourse.bass_utils import run_bass_kernel_spmd

F32 = mybir.dt.float32
BF16 = mybir.dt.bfloat16
AF = mybir.ActivationFunctionType
ALU = mybir.AluOpType
AX = mybir.AxisListType

D = 1024
S = 4096
H = 16
DK = 64
NFT = D // 128
NTT = S // 128

ALL_SUBLAYERS = ["ffn0a", "attn0", "ffn0b", "ffn1a", "attn1", "ffn1b"]

CAST_ONLY = {"L0_attn_Wq", "L0_attn_Wk", "L0_attn_Wv", "L0_attn_Wo"}
TRANSPOSE_2D = {
    "L1_attn_Wq_w": (D, D), "L1_attn_Wk_w": (D, D), "L1_attn_Wv_w": (D, D),
    "L1_attn_out_w": (D, D),
    "L1_attn_qproj_w": (D, 2 * D), "L1_attn_kproj_w": (D, 2 * D),
    "L1_ffn_w1_w": (4 * D, D), "L1_ffn_w2_w": (4 * D, D),
    "L1_ffn_h1_w": (4 * D, 4 * D), "L1_ffn_h2_w": (4 * D, 4 * D),
    "L1_ffn_w3_w": (D, 4 * D),
    "L0_ffn_sg": (512, D), "L0_ffn_su": (512, D), "L0_ffn_sd": (D, 512),
}
TRANSPOSE_3D = {
    "L0_ffn_wg": (8, 512, D), "L0_ffn_wu": (8, 512, D), "L0_ffn_wd": (8, D, 512),
}
SMALL_T = {"L0_ffn_gate_w": (8, D), "L1_attn_tau2_w": (H, 2 * D), "L1_attn_del2_w": (H, 2 * D)}
CONV_W = {"L1_attn_convq_w", "L1_attn_convk_w"}

SUBLAYER_WEIGHTS = {
    "ffn0a": ["L0_ffn_gate_w", "L0_ffn_wg", "L0_ffn_wu", "L0_ffn_wd",
              "L0_ffn_sg", "L0_ffn_su", "L0_ffn_sd"],
    "attn0": ["L0_attn_Wq", "L0_attn_bq", "L0_attn_Wk", "L0_attn_bk",
              "L0_attn_Wv", "L0_attn_bv", "L0_attn_Wo", "L0_attn_bo"],
    "ffn1a": ["L1_ffn_w1_w", "L1_ffn_w1_s", "L1_ffn_w2_w", "L1_ffn_w2_s",
              "L1_ffn_h1_w", "L1_ffn_h1_s", "L1_ffn_h2_w", "L1_ffn_h2_s",
              "L1_ffn_w3_w", "L1_ffn_w3_s"],
    "attn1": ["L1_attn_Wq_w", "L1_attn_Wq_b", "L1_attn_Wk_w", "L1_attn_Wk_b",
              "L1_attn_Wv_w", "L1_attn_Wv_b", "L1_attn_convq_w", "L1_attn_convq_b",
              "L1_attn_convk_w", "L1_attn_convk_b", "L1_attn_qproj_w", "L1_attn_qproj_b",
              "L1_attn_kproj_w", "L1_attn_kproj_b", "L1_attn_tau1_w", "L1_attn_tau1_b",
              "L1_attn_tau2_w", "L1_attn_tau2_b", "L1_attn_del1_w", "L1_attn_del1_b",
              "L1_attn_del2_w", "L1_attn_del2_b", "L1_attn_out_w", "L1_attn_out_b"],
}
SUBLAYER_WEIGHTS["ffn0b"] = SUBLAYER_WEIGHTS["ffn0a"]
SUBLAYER_WEIGHTS["ffn1b"] = SUBLAYER_WEIGHTS["ffn1a"]

TILED = {"L1_ffn_w1_w", "L1_ffn_w2_w", "L1_ffn_h1_w", "L1_ffn_h2_w", "L1_ffn_w3_w",
         "L0_ffn_sg", "L0_ffn_su", "L0_ffn_sd", "L0_ffn_wg", "L0_ffn_wu", "L0_ffn_wd"}

INPUT_SPECS = {
    "L0_attn_Wq": (D, D), "L0_attn_bq": (D,), "L0_attn_Wk": (D, D), "L0_attn_bk": (D,),
    "L0_attn_Wv": (D, D), "L0_attn_bv": (D,), "L0_attn_Wo": (D, D), "L0_attn_bo": (D,),
    "L0_ffn_gate_w": (8, D), "L0_ffn_wg": (8, 512, D), "L0_ffn_wu": (8, 512, D),
    "L0_ffn_wd": (8, D, 512), "L0_ffn_sg": (512, D), "L0_ffn_su": (512, D),
    "L0_ffn_sd": (D, 512),
    "L1_attn_Wq_w": (D, D), "L1_attn_Wq_b": (D,), "L1_attn_Wk_w": (D, D),
    "L1_attn_Wk_b": (D,), "L1_attn_Wv_w": (D, D), "L1_attn_Wv_b": (D,),
    "L1_attn_convq_w": (D, D, 3), "L1_attn_convq_b": (D,),
    "L1_attn_convk_w": (D, D, 3), "L1_attn_convk_b": (D,),
    "L1_attn_qproj_w": (D, 2 * D), "L1_attn_qproj_b": (D,),
    "L1_attn_kproj_w": (D, 2 * D), "L1_attn_kproj_b": (D,),
    "L1_attn_tau1_w": (2 * D, 1, 3), "L1_attn_tau1_b": (2 * D,),
    "L1_attn_tau2_w": (H, 2 * D, 1), "L1_attn_tau2_b": (H,),
    "L1_attn_del1_w": (2 * D, 1, 3), "L1_attn_del1_b": (2 * D,),
    "L1_attn_del2_w": (H, 2 * D, 1), "L1_attn_del2_b": (H,),
    "L1_attn_out_w": (D, D), "L1_attn_out_b": (D,),
    "L1_ffn_w1_w": (4 * D, D), "L1_ffn_w1_s": (4 * D,),
    "L1_ffn_w2_w": (4 * D, D), "L1_ffn_w2_s": (4 * D,),
    "L1_ffn_h1_w": (4 * D, 4 * D), "L1_ffn_h1_s": (4 * D,),
    "L1_ffn_h2_w": (4 * D, 4 * D), "L1_ffn_h2_s": (4 * D,),
    "L1_ffn_w3_w": (D, 4 * D), "L1_ffn_w3_s": (D,),
}

COLVECS = {
    "bq": "L0_attn_bq", "bk": "L0_attn_bk", "bv": "L0_attn_bv", "bo": "L0_attn_bo",
    "w1s": "L1_ffn_w1_s", "w2s": "L1_ffn_w2_s", "h1s": "L1_ffn_h1_s",
    "h2s": "L1_ffn_h2_s", "w3s": "L1_ffn_w3_s",
    "qb": "L1_attn_Wq_b", "kb": "L1_attn_Wk_b", "vb": "L1_attn_Wv_b",
    "cqb": "L1_attn_convq_b", "ckb": "L1_attn_convk_b",
    "qpb": "L1_attn_qproj_b", "kpb": "L1_attn_kproj_b", "ob": "L1_attn_out_b",
}


class Sem:
    __slots__ = ("h", "n")

    def __init__(self, h):
        self.h = h
        self.n = 0

    def inc(self, instr, k=1):
        instr.then_inc(self.h, k)
        self.n += k
        return self.n


class Prog:
    def __init__(self, sublayers=None):
        self.sublayers = list(ALL_SUBLAYERS if sublayers is None else sublayers)
        self.nc = bass.Bass("TRN2", target_bir_lowering=False)
        self.es = ExitStack()
        self._build()

    def sem(self, name):
        return Sem(self.es.enter_context(self.nc.semaphore(name)))

    def sb(self, st, name, shape, dtype):
        return st.enter_context(self.nc.sbuf_tensor(name + getattr(self, "scope_tag", ""), shape, dtype))

    def psum(self, st, name, shape, dtype=F32):
        return st.enter_context(self.nc.psum_tensor(name + getattr(self, "scope_tag", ""), shape, dtype))

    def tn(self, f): self.blk.tensor(f)
    def vc(self, f): self.blk.vector(f)
    def sc(self, f): self.blk.scalar(f)
    def gp(self, f): self.blk.gpsimd(f)
    def sy(self, f): self.blk.sync(f)

    def sem_dma(self, name):
        s = self.sem(name)
        self.dma_sems.append(s)
        return s

    def barrier(self):
        ph = self.ph
        ph.n += 5
        target = ph.n
        dma = [(s, s.n) for s in self.dma_sems if s.n]

        def mk(e):
            e.drain().then_inc(ph.h, 1)
            for s, c in dma:
                e.wait_ge(s.h, c)
            e.wait_ge(ph.h, target)

        def mks(e):
            e.nop().then_inc(ph.h, 1)
            for s, c in dma:
                e.wait_ge(s.h, c)
            e.wait_ge(ph.h, target)
        self.tn(mk); self.vc(mk); self.sc(mk); self.gp(mk); self.sy(mks)

    def _build(self):
        nc = self.nc
        used = set()
        for sl in self.sublayers:
            used.update(SUBLAYER_WEIGHTS[sl])
        self.used = used

        self.x0 = nc.dram_tensor("x0", [S, D], F32, kind="ExternalInput").ap()
        self.y = nc.dram_tensor("y", [S, D], F32, kind="ExternalOutput").ap()
        self.win = {}
        for name, shp in INPUT_SPECS.items():
            if name in used:
                self.win[name] = nc.dram_tensor(name, list(shp), F32, kind="ExternalInput").ap()

        self.xa = nc.dram_tensor("xa", [D, S], F32, kind="Internal").ap()
        self.xb = nc.dram_tensor("xb", [D, S], F32, kind="Internal").ap()
        self.wbf = {}
        self.wT = {}
        for name in sorted(used):
            shp = INPUT_SPECS[name]
            if name in CAST_ONLY:
                self.wbf[name] = nc.dram_tensor(name + "_bf", list(shp), BF16, kind="Internal").ap()
            elif name in TRANSPOSE_2D:
                R, C = TRANSPOSE_2D[name]
                self.wbf[name] = nc.dram_tensor(name + "_bf", [R, C], BF16, kind="Internal").ap()
                if name in TILED:
                    self.wT[name] = nc.dram_tensor(
                        name + "_T", [R // 128, 128, C // 128, 128], BF16, kind="Internal").ap()
                else:
                    self.wT[name] = nc.dram_tensor(name + "_T", [C, R], BF16, kind="Internal").ap()
            elif name in TRANSPOSE_3D:
                E, R, C = TRANSPOSE_3D[name]
                self.wbf[name] = nc.dram_tensor(name + "_bf", [E, R, C], BF16, kind="Internal").ap()
                if name in TILED:
                    self.wT[name] = nc.dram_tensor(
                        name + "_T", [E, R // 128, 128, C // 128, 128], BF16, kind="Internal").ap()
                else:
                    self.wT[name] = nc.dram_tensor(name + "_T", [E, C, R], BF16, kind="Internal").ap()
            elif name in SMALL_T:
                r, c = SMALL_T[name]
                self.wbf[name] = nc.dram_tensor(name + "_bf", [r, c], BF16, kind="Internal").ap()
                self.wT[name] = nc.dram_tensor(name + "_T", [c, r], BF16, kind="Internal").ap()
            elif name in CONV_W:
                self.wbf[name] = nc.dram_tensor(name + "_bf", [D, D * 3], BF16, kind="Internal").ap()
                self.wT[name] = nc.dram_tensor(name + "_T", [3, D, D], BF16, kind="Internal").ap()
        self.attn_sp = nc.dram_tensor("attn_sp", [D, S], BF16, kind="Internal").ap()
        self.q_sp = nc.dram_tensor("q_sp", [D, S], BF16, kind="Internal").ap()
        self.k_sp = nc.dram_tensor("k_sp", [D, S], BF16, kind="Internal").ap()
        self.d_sp = nc.dram_tensor("d_sp", [2, 16, S], BF16, kind="Internal").ap()

        self._inline_consts()

        self.dma_sems = []
        self.ld_g = self.sem_dma("ld_g")
        self.ld_s = self.sem_dma("ld_s")
        self.ld_c = self.sem_dma("ld_c")
        self.st_s = self.sem_dma("st_s")
        self.ld_cast = self.sem_dma("ld_cast")
        self.ld_x = self.sem_dma("ld_x")
        self.ld_w = [self.sem_dma(f"ld_w{i}") for i in range(2)]
        self.ld_t = [self.sem_dma(f"ld_t{i}") for i in range(2)]
        self.st_t = [self.sem_dma(f"st_t{i}") for i in range(2)]
        self.st_o = [self.sem_dma(f"st_o{i}") for i in range(2)]
        self.ld_c2 = [self.sem_dma(f"ld_c2{i}") for i in range(2)]
        self.ld_v2 = [self.sem_dma(f"ld_v2{i}") for i in range(2)]
        self.ld_y2 = [self.sem_dma(f"ld_y2{i}") for i in range(2)]
        self.s_mm = self.sem("s_mm")
        self.s_v = self.sem("s_v")
        self.s_sc = self.sem("s_sc")
        self.s_po = self.sem("s_po")
        self.ph = self.sem("phase")

        self.blk = self.es.enter_context(nc.Block())

        self._persist_alloc()
        self._prepass()
        self.barrier()

        EMIT = {"ffn0a": self.emit_ffn0, "ffn0b": self.emit_ffn0,
                "ffn1a": self.emit_ffn1, "ffn1b": self.emit_ffn1,
                "attn0": self.emit_attn0, "attn1": self.emit_attn1}
        cur, nxt = self.xa, self.xb
        for i, sl in enumerate(self.sublayers):
            self.scope_tag = f"_{i}"
            EMIT[sl](cur, nxt)
            self.scope_tag = ""
            self.barrier()
            cur, nxt = nxt, cur

        self._emit_output(cur)
        stn = self.st_s.n

        def fin(e):
            e.wait_ge(self.st_s.h, stn)
        self.tn(fin); self.vc(fin); self.sc(fin); self.gp(fin); self.sy(fin)

        self.es.close()

    def _inline_consts(self):
        nc = self.nc
        bf = ml_dtypes.bfloat16
        self.c = {}
        self.c["id_bf"] = nc.inline_tensor(np.eye(128, dtype=bf), "c_id_bf").ap()
        self.c["id_f32"] = nc.inline_tensor(np.eye(128, dtype=np.float32), "c_id_f32").ap()
        self.c["ones_col_bf"] = nc.inline_tensor(np.ones((128, 1), dtype=bf), "c_ones_col").ap()
        self.c["ones_row_f32"] = nc.inline_tensor(np.ones((1, 128), dtype=np.float32), "c_ones_row_f").ap()
        b2 = np.zeros((128, 2), dtype=bf); b2[0:64, 0] = 1; b2[64:128, 1] = 1
        self.c["head_ones"] = nc.inline_tensor(b2, "c_head_ones").ap()
        b2t = np.zeros((2, 128), dtype=bf); b2t[0, 0:64] = 1; b2t[1, 64:128] = 1
        self.c["head_bcast"] = nc.inline_tensor(b2t, "c_head_bcast").ap()
        self.c["head_bcast_f32"] = nc.inline_tensor(b2t.astype(np.float32), "c_head_bcast_f").ap()
        self.c["head_bcast_neg"] = nc.inline_tensor(-b2t.astype(np.float32), "c_head_bcast_n").ap()
        sp8 = np.zeros((16, 8 * 128), dtype=np.float32)
        for mt_ in range(8):
            sp8[2 * mt_, mt_ * 128:mt_ * 128 + 64] = 0.125
            sp8[2 * mt_ + 1, mt_ * 128 + 64:(mt_ + 1) * 128] = 0.125
        self.c["sel_pair_8th"] = nc.inline_tensor(sp8.astype(bf), "c_sel_pair_8").ap()
        s16 = np.zeros((16, 16 * 128), dtype=bf)
        for h in range(16):
            s16[h, h * 128:(h + 1) * 128] = 1
        self.c["sel16"] = nc.inline_tensor(s16, "c_sel16").ap()
        self.c["sel16_8th"] = nc.inline_tensor((s16.astype(np.float32) * 0.125).astype(bf),
                                               "c_sel16_8th").ap()
        s8 = np.zeros((8, 8 * 128), dtype=bf)
        for e in range(8):
            s8[e, e * 128:(e + 1) * 128] = 1
        self.c["sel8"] = nc.inline_tensor(s8, "c_sel8").ap()
        self.c["ones_row_8th"] = nc.inline_tensor(
            (np.ones((1, 128)) * 0.125).astype(np.float32), "c_ones_row_8th").ap()
        self.c["ones_64col_bf"] = nc.inline_tensor(
            np.concatenate([np.ones((1, 64), dtype=bf), np.zeros((1, 64), dtype=bf)], axis=1),
            "c_ones64").ap()
        self.FC = [1e-5, 1.0, -1.0, 0.5, -0.5, 1e-6, 1e-8, 1.1, 0.125, 2.0,
                   1e-16, float(np.float32(np.log(4.41)))]
        self.c["fc"] = nc.inline_tensor(
            np.tile(np.array(self.FC, dtype=np.float32), (128, 1)), "c_fc").ap()

    def _persist_alloc(self):
        es = self.es
        self.cols = {}
        for key, name in COLVECS.items():
            if name in self.used:
                ln = INPUT_SPECS[name][0]
                self.cols[key] = self.sb(es, "col_" + key, [128, (ln + 127) // 128], F32)
        if "L1_ffn_w3_s" in self.used:
            self.cols["w3sh"] = self.sb(es, "col_w3sh", [128, 8], F32)
        if "L1_attn_tau2_b" in self.used:
            self.cols["t2b"] = self.sb(es, "col_t2b", [16, 1], F32)
            self.cols["d2b"] = self.sb(es, "col_d2b", [16, 1], F32)
        if "L1_attn_tau1_w" in self.used:
            self.cols["t1w"] = self.sb(es, "col_t1w", [128, 48], F32)
            self.cols["d1w"] = self.sb(es, "col_d1w", [128, 48], F32)
            self.cols["t1b"] = self.sb(es, "col_t1b", [128, 16], F32)
            self.cols["d1b"] = self.sb(es, "col_d1b", [128, 16], F32)
        self.k = {}
        for nm, shape, dt in [("id_bf", [128, 128], BF16), ("id_f32", [128, 128], F32),
                              ("ones_col_bf", [128, 1], BF16), ("ones_row_f32", [1, 128], F32),
                              ("head_ones", [128, 2], BF16), ("head_bcast", [2, 128], BF16),
                              ("head_bcast_f32", [2, 128], F32),
                              ("head_bcast_neg", [2, 128], F32),
                              ("sel_pair_8th", [16, 8 * 128], BF16),
                              ("ones_row_8th", [1, 128], F32), ("ones_64col_bf", [1, 128], BF16),
                              ("fc", [128, len(self.FC)], F32)]:
            self.k[nm] = self.sb(es, "k_" + nm, shape, dt)
        for i, v in enumerate(self.FC):
            self.nc.const_aps.register(F32, v, self.k["fc"][:, i:i + 1])

    def _prepass(self):
        def g_casts(g):
            for name in sorted(self.wbf):
                src = self.win[name]
                if name in CONV_W:
                    src = src.rearrange("o i t -> o (i t)")
                elif name in ("L1_attn_tau2_w", "L1_attn_del2_w"):
                    src = src.rearrange("h c o -> h (c o)")
                dst = self.wbf[name]
                n0 = src.shape[0]
                inner = 1
                for d_ in src.shape[1:]:
                    inner *= d_
                rows_per = max(1, (8 << 20) // (inner * 4))
                r = 0
                while r < n0:
                    r2 = min(n0, r + rows_per)
                    self.ld_cast.inc(g.dma_start(out=dst[r:r2], in_=src[r:r2]), 16)
                    r = r2
        self.gp(g_casts)
        casts_done = self.ld_cast.n

        self._pp_consts()
        self.barrier()
        self._pp_x0()
        self.barrier()

        def wcasts(e):
            e.wait_ge(self.ld_cast.h, casts_done)
        self.sy(wcasts); self.sc(wcasts); self.tn(wcasts)
        self._pp_transposes()
        self.barrier()
        self._pp_pe_transposes()

    def _pp_consts(self):
        with ExitStack() as st:
            def loads(sy):
                for nm in self.k:
                    self.ld_s.inc(sy.dma_start(out=self.k[nm][:], in_=self.c[nm][:]), 16)
            self.sy(loads)
            if "t2b" in self.cols:
                def ld2(sy):
                    self.ld_s.inc(sy.dma_start(
                        out=self.cols["t2b"][:, :],
                        in_=self.win["L1_attn_tau2_b"].rearrange("(p o) -> p o", o=1)), 16)
                    self.ld_s.inc(sy.dma_start(
                        out=self.cols["d2b"][:, :],
                        in_=self.win["L1_attn_del2_b"].rearrange("(p o) -> p o", o=1)), 16)
                self.sy(ld2)
            if "t1w" in self.cols:
                def ldtaps(sy):
                    for key, bkey, wname, bname in (("t1w", "t1b", "L1_attn_tau1_w", "L1_attn_tau1_b"),
                                                    ("d1w", "d1b", "L1_attn_del1_w", "L1_attn_del1_b")):
                        wv = self.win[wname].rearrange("(a p q) o t -> a p (q o t)", p=128, q=2)
                        bv = self.win[bname].rearrange("(a p q) -> a p q", p=128, q=2)
                        for r in range(2):
                            for ct in range(8):
                                for tap in range(3):
                                    col = r * 24 + ct * 3 + tap
                                    self.ld_s.inc(sy.dma_start(
                                        out=self.cols[key][:, col:col + 1],
                                        in_=wv[ct, :, r * 3 + tap:r * 3 + tap + 1],
                                        allow_slow_non_contiguous=True), 16)
                                self.ld_s.inc(sy.dma_start(
                                    out=self.cols[bkey][:, r * 8 + ct:r * 8 + ct + 1],
                                    in_=bv[ct, :, r:r + 1],
                                    allow_slow_non_contiguous=True), 16)
                self.sy(ldtaps)

            vrow = self.sb(st, "pp_vrow", [128, 128], F32)
            pvt = self.psum(st, "pp_vt", [128, 128], F32)

            def vz(v):
                self.s_v.inc(v.memset(vrow[:, :], 0.0), 1)
            self.vc(vz)
            for key, name in COLVECS.items():
                if name not in self.used:
                    continue
                kk = (INPUT_SPECS[name][0] + 127) // 128
                v2 = self.win[name].rearrange("(c p) -> c p", p=128)

                def ld(sy):
                    sy.wait_ge(self.s_mm.h, self.s_mm.n)
                    sy.wait_ge(self.s_v.h, self.s_v.n)
                    self.ld_s.inc(sy.dma_start(out=vrow[0:kk, :], in_=v2), 16)
                self.sy(ld)
                n0 = self.ld_s.n

                def tr(t):
                    t.wait_ge(self.ld_s.h, n0)
                    t.wait_ge(self.s_v.h, self.s_v.n)
                    self.s_mm.inc(t.transpose(pvt[:, :], vrow[:, :], self.k["id_f32"][:]), 1)
                self.tn(tr)
                n1 = self.s_mm.n
                dst = self.cols[key]

                def ev(v):
                    v.wait_ge(self.s_mm.h, n1)
                    self.s_v.inc(v.tensor_copy(dst[:, 0:kk], pvt[:, 0:kk]), 1)
                self.vc(ev)
            if "w3sh" in self.cols:
                nv = self.s_v.n

                def half(v):
                    v.wait_ge(self.s_v.h, nv)
                    self.s_v.inc(v.tensor_scalar(self.cols["w3sh"][:, :], self.cols["w3s"][:, :],
                                                 0.5, None, ALU.mult), 1)
                self.vc(half)

    def _pp_x0(self):
        with ExitStack() as st:
            xt = self.sb(st, "pp_xt", [128, D], F32)
            slab = self.sb(st, "pp_slab", [128, NFT, S], F32)
            pft = self.psum(st, "pp_f1", [128, 512], F32)
            pft2 = self.psum(st, "pp_f2", [128, 512], F32)
            for tt in range(NTT):
                def ldx(sy):
                    sy.wait_ge(self.s_mm.h, self.s_mm.n)
                    self.ld_s.inc(sy.dma_start(out=xt[:, :], in_=self.x0[tt * 128:(tt + 1) * 128, :]), 16)
                self.sy(ldx)
                n0 = self.ld_s.n

                def tr(t):
                    t.wait_ge(self.ld_s.h, n0)
                    t.wait_ge(self.s_v.h, self.s_v.n)
                    ins = None
                    for ft in range(NFT):
                        p = pft if ft < 4 else pft2
                        ins = t.transpose(p[:, (ft % 4) * 128:(ft % 4 + 1) * 128],
                                          xt[:, ft * 128:(ft + 1) * 128], self.k["id_f32"][:])
                    self.s_mm.inc(ins, 1)
                self.tn(tr)
                n1 = self.s_mm.n

                def ev(v):
                    v.wait_ge(self.s_mm.h, n1)
                    for ft in range(NFT):
                        p = pft if ft < 4 else pft2
                        ins = v.tensor_copy(slab[:, ft, tt * 128:(tt + 1) * 128],
                                            p[:, (ft % 4) * 128:(ft % 4 + 1) * 128])
                    self.s_v.inc(ins, 1)
                self.vc(ev)
            nv = self.s_v.n

            def stx(sy):
                sy.wait_ge(self.s_v.h, nv)
                for ft in range(NFT):
                    self.st_s.inc(sy.dma_start(out=self.xa[ft * 128:(ft + 1) * 128, :],
                                               in_=slab[:, ft, :]), 16)
            self.sy(stx)
            stn = self.st_s.n

            def wst(sy):
                sy.wait_ge(self.st_s.h, stn)
            self.sy(wst)

    def _pp_transposes(self):
        jobs = []
        for name in sorted(self.wT):
            if name in TRANSPOSE_2D:
                R, C = TRANSPOSE_2D[name]
                jobs.append((self.wbf[name], self.wT[name], R, C, name in TILED))
            elif name in TRANSPOSE_3D:
                E, R, C = TRANSPOSE_3D[name]
                for e in range(E):
                    jobs.append((self.wbf[name][e], self.wT[name][e], R, C, name in TILED))
            elif name in SMALL_T:
                r, c = SMALL_T[name]
                if r % 16 == 0:
                    jobs.append((self.wbf[name], self.wT[name], r, c, False))
        with ExitStack() as st:
            slabs = [self.sb(st, f"pp_ws{i}", [128, 4096], BF16) for i in range(2)]
            i = 0
            for src, dst, R, C, tiled in jobs:
                for cb in range(C // 128):
                    par = i % 2
                    buf = slabs[par]
                    prev = self.st_t[par].n

                    def tld(sy):
                        if prev:
                            sy.wait_ge(self.st_t[par].h, prev)
                        self.ld_t[par].inc(sy.dma_start(out=buf[:, 0:R],
                                                        in_=src[0:R, cb * 128:(cb + 1) * 128],
                                                        transpose=True), 16)
                    self.sy(tld)
                    n0 = self.ld_t[par].n

                    def tst(sc):
                        sc.wait_ge(self.ld_t[par].h, n0)
                        if tiled:
                            self.st_t[par].inc(sc.dma_start(
                                out=dst[:, :, cb, :].rearrange("mt p m -> p mt m"),
                                in_=buf[:, 0:R].rearrange("p (mt m) -> p mt m", m=128)), 16)
                        else:
                            self.st_t[par].inc(sc.dma_start(out=dst[cb * 128:(cb + 1) * 128, 0:R],
                                                            in_=buf[:, 0:R]), 16)
                    self.sc(tst)
                    i += 1
        if "L0_ffn_gate_w" in self.used:
            with ExitStack() as st:
                buf = self.sb(st, "pp_gw", [128, D], BF16)
                obuf = self.sb(st, "pp_gwo", [128, NFT * 8], BF16)
                pt = self.psum(st, "pp_gpt", [128, 1024], BF16)

                nt0, nt1 = self.st_t[0].n, self.st_t[1].n

                def vz(v):
                    if nt0:
                        v.wait_ge(self.st_t[0].h, nt0)
                    if nt1:
                        v.wait_ge(self.st_t[1].h, nt1)
                    self.s_v.inc(v.memset(buf[:, :], 0.0), 1)
                self.vc(vz)
                nvz = self.s_v.n

                def ld(sy):
                    sy.wait_ge(self.s_v.h, nvz)
                    self.ld_s.inc(sy.dma_start(out=buf[0:8, :], in_=self.wbf["L0_ffn_gate_w"][:, :]), 16)
                self.sy(ld)
                n0 = self.ld_s.n

                def tr(t):
                    t.wait_ge(self.ld_s.h, n0)
                    ins = None
                    for ic in range(8):
                        ins = t.transpose(pt[:, ic * 128:(ic + 1) * 128],
                                          buf[:, ic * 128:(ic + 1) * 128], self.k["id_bf"][:])
                    self.s_mm.inc(ins, 1)
                self.tn(tr)
                n1 = self.s_mm.n

                def ev(v):
                    v.wait_ge(self.s_mm.h, n1)
                    for ic in range(8):
                        ins = v.tensor_copy(obuf[:, ic * 8:(ic + 1) * 8],
                                            pt[:, ic * 128:ic * 128 + 8])
                    self.s_v.inc(ins, 1)
                self.vc(ev)
                nv = self.s_v.n

                def stg(sy):
                    sy.wait_ge(self.s_v.h, nv)
                    gT = self.wT["L0_ffn_gate_w"].rearrange("(a p) e -> a p e", p=128)
                    for ic in range(8):
                        self.st_s.inc(sy.dma_start(out=gT[ic], in_=obuf[:, ic * 8:(ic + 1) * 8]), 16)
                self.sy(stg)
                stn = self.st_s.n

                def wst(sy):
                    sy.wait_ge(self.st_s.h, stn)
                self.sy(wst)

    def _pp_pe_transposes(self):
        names = [n for n in ("L1_attn_convq_w", "L1_attn_convk_w") if n in self.used]
        if not names:
            return
        with ExitStack() as st:
            buf = self.sb(st, "pp_cv", [128, D * 3], BF16)
            obuf = self.sb(st, "pp_co", [128, D], BF16)
            pt = self.psum(st, "pp_cpt", [128, 1024], BF16)
            for name in names:
                for ob in range(8):
                    def ld(sy):
                        sy.wait_ge(self.s_mm.h, self.s_mm.n)
                        self.ld_s.inc(sy.dma_start(out=buf[:, :],
                                                   in_=self.wbf[name][ob * 128:(ob + 1) * 128, :]), 16)
                    self.sy(ld)
                    n0 = self.ld_s.n
                    v3 = buf.rearrange("p (i t) -> p i t", t=3)
                    for tap in range(3):
                        def tr(t):
                            t.wait_ge(self.ld_s.h, n0)
                            t.wait_ge(self.s_v.h, self.s_v.n)
                            ins = None
                            for ic in range(8):
                                ins = t.transpose(pt[:, ic * 128:(ic + 1) * 128],
                                                  v3[:, ic * 128:(ic + 1) * 128, tap], self.k["id_bf"][:])
                            self.s_mm.inc(ins, 1)
                        self.tn(tr)
                        n1 = self.s_mm.n

                        def ev(v):
                            v.wait_ge(self.s_mm.h, n1)
                            v.wait_ge(self.st_s.h, self.st_s.n)
                            self.s_v.inc(v.tensor_copy(obuf[:, :], pt[:, :]), 1)
                        self.vc(ev)
                        nv = self.s_v.n

                        def stc(sy):
                            sy.wait_ge(self.s_v.h, nv)
                            for ic in range(8):
                                self.st_s.inc(sy.dma_start(
                                    out=self.wT[name][tap, ic * 128:(ic + 1) * 128,
                                                      ob * 128:(ob + 1) * 128],
                                    in_=obuf[:, ic * 128:(ic + 1) * 128]), 16)
                        self.sy(stc)
            stn = self.st_s.n

            def wst(sy):
                sy.wait_ge(self.st_s.h, stn)
            self.sy(wst)

    def emit_ln_chunk(self, xs, nbuf, xbf, xsq, rows, tt, p_s1, p_s2, p_bc, T):
        NCH = T // 512
        r_m = rows[:, 0:512]; r_msq = rows[:, 512:1024]; r_var = rows[:, 1024:1536]
        r_a = rows[:, 1536:2048]; r_t = rows[:, 2048:2560]
        r_b = r_msq

        def s_cast(s):
            s.activation(xbf[:, :, :], xs[:, :, :], AF.Copy)
            self.s_sc.inc(s.activation(xsq[:, :, :], xs[:, :, :], AF.Square), 1)
        self.sc(s_cast)
        n_sc = self.s_sc.n

        for nch in range(NCH):
            sl = slice(nch * 512, (nch + 1) * 512)

            def t_stats(t):
                t.wait_ge(self.s_sc.h, n_sc)
                t.wait_ge(self.s_sc.h, self.s_sc.n)
                for kt in range(NFT):
                    t.matmul(p_s1[:, :], self.k["ones_col_bf"][:, :], xbf[:, kt, sl],
                             start=(kt == 0), stop=(kt == NFT - 1))
                ins = None
                for kt in range(NFT):
                    ins = t.matmul(p_s2[:, :], self.k["ones_col_bf"][:, :], xsq[:, kt, sl],
                                   start=(kt == 0), stop=(kt == NFT - 1))
                self.s_mm.inc(ins, 1)
            self.tn(t_stats)
            n_mm = self.s_mm.n

            def s_stats(s):
                s.wait_ge(self.s_mm.h, n_mm)
                s.wait_ge(self.s_v.h, self.s_v.n)
                s.activation(r_t[:, :], p_s1[:, :], AF.Square, scale=1.0 / D)
                s.activation(r_msq[:, :], p_s2[:, :], AF.Copy, scale=1.0 / D)
                self.s_sc.inc(s.activation(r_m[:, :], p_s1[:, :], AF.Copy, scale=1.0 / D), 1)
            self.sc(s_stats)
            n_s1 = self.s_sc.n

            def v_var(v):
                v.wait_ge(self.s_sc.h, n_s1)
                self.s_v.inc(v.tensor_sub(r_var[:, :], r_msq[:, :], r_t[:, :]), 1)
            self.vc(v_var)
            n_v = self.s_v.n

            def s_rstd(s):
                s.wait_ge(self.s_v.h, n_v)
                self.s_sc.inc(s.activation(r_t[:, :], r_var[:, :], AF.Ln, bias=1e-5), 1)
                s.wait_ge(self.s_sc.h, self.s_sc.n)
                self.s_sc.inc(s.activation(r_a[:, :], r_t[:, :], AF.Exp, scale=-0.5), 1)
            self.sc(s_rstd)
            n_s2 = self.s_sc.n

            def v_b(v):
                v.wait_ge(self.s_sc.h, n_s2)
                self.s_v.inc(v.scalar_tensor_tensor(r_b[:, :], r_m[:, :], -1.0, r_a[:, :],
                                                    ALU.mult, ALU.mult), 1)
            self.vc(v_b)
            n_v2 = self.s_v.n

            def t_bc(t):
                t.wait_ge(self.s_v.h, n_v2)
                t.wait_ge(self.s_po.h, self.s_po.n)
                t.wait_ge(self.s_sc.h, self.s_sc.n)
                t.matmul(p_bc[:, 0:512], self.k["ones_row_f32"][:, :], r_a[:, :], start=True, stop=True)
                ins = t.matmul(p_bc[:, 512:1024], self.k["ones_row_f32"][:, :], r_b[:, :], start=True, stop=True)
                self.s_mm.inc(ins, 1)
            self.tn(t_bc)
            n_mm2 = self.s_mm.n

            def s_bevac(s):
                s.wait_ge(self.s_mm.h, n_mm2)
                s.wait_ge(self.s_po.h, self.s_po.n)
                self.s_sc.inc(s.activation(tt[:, 1, :], p_bc[:, 512:1024], AF.Copy), 1)
            self.sc(s_bevac)
            n_s3 = self.s_sc.n

            for ft in range(NFT):
                def v_mul(v):
                    v.wait_ge(self.s_mm.h, n_mm2)
                    if ft >= 1:
                        v.wait_ge(self.s_po.h, self.s_po.n)
                    self.s_v.inc(v.tensor_mul(tt[:, 0, :], xs[:, ft, sl], p_bc[:, 0:512]), 1)
                self.vc(v_mul)
                n_v3 = self.s_v.n

                def p_add(g):
                    g.wait_ge(self.s_v.h, n_v3)
                    g.wait_ge(self.s_sc.h, n_s3)
                    self.s_po.inc(g.tensor_add(nbuf[:, ft, sl], tt[:, 0, :], tt[:, 1, :]), 1)
                self.gp(p_add)

    def emit_ffn1(self, xin, xout):
        T = 1024
        NCH = S // T
        w1T, w2T = self.wT["L1_ffn_w1_w"], self.wT["L1_ffn_w2_w"]
        h1T, h2T = self.wT["L1_ffn_h1_w"], self.wT["L1_ffn_h2_w"]
        w3T = self.wT["L1_ffn_w3_w"]
        cols = self.cols
        with ExitStack() as st:
            xs = self.sb(st, "f1_xs", [128, NFT, T], F32)
            hbuf = self.sb(st, "f1_h", [128, 32, T], BF16)
            gbuf = self.sb(st, "f1_g", [128, 32, T], BF16)
            wcol = [self.sb(st, f"f1_w{i}", [128, 32, 128], BF16) for i in range(2)]
            sstg = self.sb(st, "f1_ss", [128, 2, 512], BF16)
            vstg = self.sb(st, "f1_vs", [128, 2, 512], BF16)
            ostg = self.sb(st, "f1_os", [128, 2, 512], F32)
            rows = self.sb(st, "f1_rows", [1, 5 * 512], F32)
            ttile = self.sb(st, "f1_tt", [128, 2, 512], F32)
            zstg = ttile
            pzp = [self.psum(st, f"f1_pzp{i}", [128, 1024], F32) for i in range(2)]
            pz = [pzp[0][:, 0:512], pzp[0][:, 512:1024], pzp[1][:, 0:512], pzp[1][:, 512:1024]]
            p_s1 = self.psum(st, "f1_ps1", [1, 512], F32)
            p_s2 = self.psum(st, "f1_ps2", [1, 512], F32)
            p_bc = self.psum(st, "f1_pbc", [128, 1024], F32)
            nbuf = gbuf[:, 0:NFT, :]
            xbf = hbuf[:, 0:NFT, :]
            xsq = hbuf[:, NFT:2 * NFT, :]

            wc_cons = {0: None, 1: None}
            z1_cons = {0: 0, 1: 0}
            zfull = [ttile.reshape([128, 1024]), ostg.reshape([128, 1024])]
            zf_cons = {0: 0, 1: 0}

            for c in range(NCH):
                def ldx(sy):
                    sy.wait_ge(self.s_v.h, self.s_v.n)
                    sy.wait_ge(self.s_sc.h, self.s_sc.n)
                    xr = xin.rearrange("(a p) t -> a p t", p=128)
                    for ft in range(NFT):
                        self.ld_x.inc(sy.dma_start(out=xs[:, ft, :], in_=xr[ft, :, c * T:(c + 1) * T]), 16)
                self.sy(ldx)
                nld = self.ld_x.n

                def wx(e):
                    e.wait_ge(self.ld_x.h, nld)
                    e.wait_ge(self.s_mm.h, self.s_mm.n)
                self.sc(wx)
                self.vc(lambda v: v.wait_ge(self.ld_x.h, nld))

                self.emit_ln_chunk(xs, nbuf, xbf, xsq, rows, ttile, p_s1, p_s2, p_bc, T)
                nv_ln = self.s_v.n

                pz_cons = {0: None, 1: None}
                s1_prev = None
                for mt in range(32):
                    wb = wcol[mt % 2]

                    def ldw(g):
                        if wc_cons[mt % 2] is not None:
                            g.wait_ge(self.s_mm.h, wc_cons[mt % 2])
                        self.ld_w[mt % 2].inc(g.dma_start(out=wb[:, 0:4, :], in_=w1T[mt][:, 0:4, :]), 16)
                    self.gp(ldw)

                    def ldwv(v_):
                        if wc_cons[mt % 2] is not None:
                            v_.wait_ge(self.s_mm.h, wc_cons[mt % 2])
                        self.ld_v2[mt % 2].inc(v_.dma_start(out=wb[:, 4:8, :], in_=w1T[mt][:, 4:8, :]), 16)
                    self.sy(ldwv)

                    def ldw2(sc_):
                        if wc_cons[mt % 2] is not None:
                            sc_.wait_ge(self.s_mm.h, wc_cons[mt % 2])
                        self.ld_c2[mt % 2].inc(sc_.dma_start(out=wb[:, 8:12, :], in_=w2T[mt][:, 0:4, :]), 16)
                    self.sc(ldw2)

                    def ldwy(sy):
                        if wc_cons[mt % 2] is not None:
                            sy.wait_ge(self.s_mm.h, wc_cons[mt % 2])
                        self.ld_y2[mt % 2].inc(sy.dma_start(out=wb[:, 12:16, :], in_=w2T[mt][:, 4:8, :]), 16)
                    self.sy(ldwy)
                    nw = self.ld_w[mt % 2].n
                    nw2 = self.ld_c2[mt % 2].n
                    nw3 = self.ld_v2[mt % 2].n
                    nw4 = self.ld_y2[mt % 2].n
                    pzA, pzB = pz[2 * (mt % 2)], pz[2 * (mt % 2) + 1]

                    s1_pend = []
                    for nch in range(2):
                        pcw = pz_cons[mt % 2]

                        def mm(t):
                            t.wait_ge(self.ld_w[mt % 2].h, nw)
                            t.wait_ge(self.ld_c2[mt % 2].h, nw2)
                            t.wait_ge(self.ld_v2[mt % 2].h, nw3)
                            t.wait_ge(self.ld_y2[mt % 2].h, nw4)
                            if mt == 0 and nch == 0:
                                t.wait_ge(self.s_v.h, nv_ln)
                            if pcw is not None:
                                t.wait_ge(self.s_v.h, pcw)
                            for kt in range(NFT):
                                t.matmul(pzA[:, :], wb[:, kt, :], nbuf[:, kt, nch * 512:(nch + 1) * 512],
                                         start=(kt == 0), stop=(kt == NFT - 1))
                            ins = None
                            for kt in range(NFT):
                                ins = t.matmul(pzB[:, :], wb[:, 8 + kt, :], nbuf[:, kt, nch * 512:(nch + 1) * 512],
                                               start=(kt == 0), stop=(kt == NFT - 1))
                            self.s_mm.inc(ins, 1)
                        self.tn(mm)
                        n_mm = self.s_mm.n

                        def ev_z(v, n_mm=n_mm, nch=nch, mt=mt, zw=z1_cons[nch], pog=self.s_po.n):
                            v.wait_ge(self.s_mm.h, n_mm)
                            v.wait_ge(self.s_po.h, pog)
                            if zw:
                                v.wait_ge(self.s_sc.h, zw)
                            self.s_v.inc(v.tensor_scalar(zstg[:, nch, :], pzA[:, :],
                                                         cols["w1s"][:, mt:mt + 1], None, ALU.mult), 1)
                        self.vc(ev_z)
                        n_vz = self.s_v.n

                        self.s_sc.n += 1
                        n_silu = self.s_sc.n
                        z1_cons[nch] = n_silu

                        def ev_s(s, n_vz=n_vz, nch=nch, pog=self.s_po.n):
                            s.wait_ge(self.s_v.h, n_vz)
                            s.wait_ge(self.s_po.h, pog)
                            s.activation(sstg[:, nch, :], zstg[:, nch, :],
                                         AF.Silu).then_inc(self.s_sc.h, 1)
                        s1_pend.append(ev_s)

                        def ev_v(v):
                            v.wait_ge(self.s_mm.h, n_mm)
                            v.wait_ge(self.s_po.h, self.s_po.n)
                            self.s_v.inc(v.tensor_scalar(vstg[:, nch, :], pzB[:, :],
                                                         cols["w2s"][:, mt:mt + 1], None, ALU.mult), 1)
                        self.vc(ev_v)
                        n_sc2, n_v2 = n_silu, self.s_v.n

                        def po(g):
                            g.wait_ge(self.s_sc.h, n_sc2)
                            g.wait_ge(self.s_v.h, n_v2)
                            self.s_po.inc(g.tensor_mul(hbuf[:, mt, nch * 512:(nch + 1) * 512],
                                                       sstg[:, nch, :], vstg[:, nch, :]), 1)
                        self.gp(po)
                        pz_cons[mt % 2] = self.s_v.n
                    if s1_prev is not None:
                        self.sc(lambda s, ops=s1_prev: [f(s) for f in ops])
                    s1_prev = s1_pend
                    wc_cons[mt % 2] = self.s_mm.n
                if s1_prev is not None:
                    self.sc(lambda s, ops=s1_prev: [f(s) for f in ops])

                for stage, (wt_, src_, dst_, scol) in enumerate(
                        [(h1T, hbuf, gbuf, "h1s"), (h2T, gbuf, hbuf, "h2s")]):
                    n_po0, n_sc0 = self.s_po.n, self.s_sc.n

                    def stage_gate(t):
                        t.wait_ge(self.s_po.h, n_po0)
                        t.wait_ge(self.s_sc.h, n_sc0)
                    self.tn(stage_gate)
                    pzp_cons = {0: 0, 1: 0}
                    n_st0 = [self.st_o[0].n, self.st_o[1].n]
                    s2_pend = []
                    s2_prev = None
                    for mt in range(32):
                        par = mt % 2
                        wb = wcol[par]

                        def ldw(g):
                            if wc_cons[par] is not None:
                                g.wait_ge(self.s_mm.h, wc_cons[par])
                            self.ld_w[par].inc(g.dma_start(out=wb[:, 0:8, :],
                                                           in_=wt_[mt][:, 0:8, :]), 16)
                        self.gp(ldw)

                        def ldwv(v_):
                            if wc_cons[par] is not None:
                                v_.wait_ge(self.s_mm.h, wc_cons[par])
                            self.ld_v2[par].inc(v_.dma_start(out=wb[:, 8:16, :],
                                                             in_=wt_[mt][:, 8:16, :]), 16)
                        self.sy(ldwv)

                        def ldwb(sc_):
                            if wc_cons[par] is not None:
                                sc_.wait_ge(self.s_mm.h, wc_cons[par])
                            self.ld_c2[par].inc(sc_.dma_start(out=wb[:, 16:24, :],
                                                              in_=wt_[mt][:, 16:24, :]), 16)
                        self.sc(ldwb)

                        def ldwy(sy):
                            if wc_cons[par] is not None:
                                sy.wait_ge(self.s_mm.h, wc_cons[par])
                            self.ld_y2[par].inc(sy.dma_start(out=wb[:, 24:32, :],
                                                             in_=wt_[mt][:, 24:32, :]), 16)
                        self.sy(ldwy)
                        nw = self.ld_w[par].n
                        nw2 = self.ld_c2[par].n
                        nw3 = self.ld_v2[par].n
                        nw4 = self.ld_y2[par].n
                        pcw = pzp_cons[par]

                        def mm(t, par=par, nw=nw, nw2=nw2, nw3=nw3, nw4=nw4, pcw=pcw):
                            t.wait_ge(self.ld_w[par].h, nw)
                            t.wait_ge(self.ld_c2[par].h, nw2)
                            t.wait_ge(self.ld_v2[par].h, nw3)
                            t.wait_ge(self.ld_y2[par].h, nw4)
                            if pcw:
                                t.wait_ge(self.s_v.h, pcw)
                            ins = None
                            for nch in range(2):
                                for kt in range(32):
                                    ins = t.matmul(pzp[par][:, nch * 512:(nch + 1) * 512],
                                                   wb[:, kt, :], src_[:, kt, nch * 512:(nch + 1) * 512],
                                                   start=(kt == 0), stop=(kt == 31))
                            self.s_mm.inc(ins, 1)
                        self.tn(mm)
                        n_mm = self.s_mm.n

                        def ev_z(v, par=par, n_mm=n_mm, mt=mt, zw=zf_cons[par]):
                            v.wait_ge(self.s_mm.h, n_mm)
                            if zw:
                                v.wait_ge(self.s_sc.h, zw)
                            if par == 1:
                                v.wait_ge(self.st_o[0].h, n_st0[0])
                                v.wait_ge(self.st_o[1].h, n_st0[1])
                            self.s_v.inc(v.tensor_scalar(zfull[par][:, :], pzp[par][:, :],
                                                         cols[scol][:, mt:mt + 1], None, ALU.mult), 1)
                        self.vc(ev_z)
                        n_vz = self.s_v.n
                        pzp_cons[par] = n_vz

                        self.s_sc.n += 1
                        zf_cons[par] = self.s_sc.n

                        def ev(s, par=par, n_vz=n_vz, mt=mt):
                            s.wait_ge(self.s_v.h, n_vz)
                            s.activation(dst_[:, mt, :], zfull[par][:, :],
                                         AF.Gelu).then_inc(self.s_sc.h, 1)
                        s2_pend.append(ev)
                        if mt % 2 == 1:
                            if s2_prev is not None:
                                self.sc(lambda s, ops=s2_prev: [f(s) for f in ops])
                            s2_prev = s2_pend
                            s2_pend = []
                        wc_cons[par] = self.s_mm.n
                    if s2_prev is not None:
                        self.sc(lambda s, ops=s2_prev: [f(s) for f in ops])

                n_sc0 = self.s_sc.n

                def stage_gate4(t):
                    t.wait_ge(self.s_sc.h, n_sc0)
                self.tn(stage_gate4)
                pz_cons4 = {0: None, 1: None, 2: None, 3: None}
                for dt in range(8):
                    wb = wcol[dt % 2]

                    def ldw(g):
                        if wc_cons[dt % 2] is not None:
                            g.wait_ge(self.s_mm.h, wc_cons[dt % 2])
                        self.ld_w[dt % 2].inc(g.dma_start(out=wb[:, 0:8, :],
                                                          in_=w3T[dt][:, 0:8, :]), 16)
                    self.gp(ldw)

                    def ldwv(v_):
                        if wc_cons[dt % 2] is not None:
                            v_.wait_ge(self.s_mm.h, wc_cons[dt % 2])
                        self.ld_v2[dt % 2].inc(v_.dma_start(out=wb[:, 8:16, :],
                                                            in_=w3T[dt][:, 8:16, :]), 16)
                    self.sy(ldwv)

                    def ldwb(sc_):
                        if wc_cons[dt % 2] is not None:
                            sc_.wait_ge(self.s_mm.h, wc_cons[dt % 2])
                        self.ld_c2[dt % 2].inc(sc_.dma_start(out=wb[:, 16:24, :],
                                                             in_=w3T[dt][:, 16:24, :]), 16)
                    self.sc(ldwb)

                    def ldwy(sy):
                        if wc_cons[dt % 2] is not None:
                            sy.wait_ge(self.s_mm.h, wc_cons[dt % 2])
                        self.ld_y2[dt % 2].inc(sy.dma_start(out=wb[:, 24:32, :],
                                                            in_=w3T[dt][:, 24:32, :]), 16)
                    self.sy(ldwy)
                    nw = self.ld_w[dt % 2].n
                    nw2 = self.ld_c2[dt % 2].n
                    nw3 = self.ld_v2[dt % 2].n
                    nw4 = self.ld_y2[dt % 2].n
                    pzA = pz[2 * (dt % 2)]
                    pzB = pz[2 * (dt % 2) + 1]
                    for nch in range(2):
                        pzX = pzA if nch == 0 else pzB
                        bank = 2 * (dt % 2) + nch
                        pcw = pz_cons4[bank]

                        def mm(t):
                            t.wait_ge(self.ld_w[dt % 2].h, nw)
                            t.wait_ge(self.ld_c2[dt % 2].h, nw2)
                            t.wait_ge(self.ld_v2[dt % 2].h, nw3)
                            t.wait_ge(self.ld_y2[dt % 2].h, nw4)
                            if pcw is not None:
                                t.wait_ge(self.s_v.h, pcw)
                            ins = None
                            for kt in range(32):
                                ins = t.matmul(pzX[:, :], wb[:, kt, :], hbuf[:, kt, nch * 512:(nch + 1) * 512],
                                               start=(kt == 0), stop=(kt == 31))
                            self.s_mm.inc(ins, 1)
                        self.tn(mm)
                        n_mm = self.s_mm.n

                        def ev(v):
                            v.wait_ge(self.s_mm.h, n_mm)
                            v.wait_ge(self.st_o[nch].h, self.st_o[nch].n)
                            self.s_v.inc(v.scalar_tensor_tensor(
                                ostg[:, nch, :], pzX[:, :], cols["w3sh"][:, dt:dt + 1],
                                xs[:, dt, nch * 512:(nch + 1) * 512], ALU.mult, ALU.add), 1)
                        self.vc(ev)
                        n_v3 = self.s_v.n

                        def stv(sy):
                            sy.wait_ge(self.s_v.h, n_v3)
                            self.st_o[nch].inc(sy.dma_start(
                                out=xout[dt * 128:(dt + 1) * 128,
                                         c * T + nch * 512:c * T + (nch + 1) * 512],
                                in_=ostg[:, nch, :]), 16)
                        self.sy(stv)
                        pz_cons4[bank] = self.s_v.n
                    wc_cons[dt % 2] = self.s_mm.n

    def emit_ffn0(self, xin, xout):
        T = 1024
        NCH_T = S // T
        wgT = self.wT["L0_ffn_wg"]; wuT = self.wT["L0_ffn_wu"]; wdT = self.wT["L0_ffn_wd"]
        sgT = self.wT["L0_ffn_sg"]; suT = self.wT["L0_ffn_su"]; sdT = self.wT["L0_ffn_sd"]
        gT = self.wT["L0_ffn_gate_w"]
        with ExitStack() as st:
            xs = self.sb(st, "f0_xs", [128, NFT, T], F32)
            hall = self.sb(st, "f0_h", [128, 36, T], BF16)
            nbuf = self.sb(st, "f0_n", [128, NFT, T], BF16)
            wcol = [self.sb(st, f"f0_w{i}", [128, 36, 128], BF16) for i in range(2)]
            gate_sT = self.sb(st, "f0_gt", [128, 8, 8], BF16)
            sel8 = self.sb(st, "f0_sel8", [8, 8 * 128], BF16)
            gsc = self.sb(st, "f0_gs", [128, 16], F32)
            ebuf = self.sb(st, "f0_eb", [128, 8], F32)
            pbuf = self.sb(st, "f0_pb", [128, 8], F32)
            pbuf2 = self.sb(st, "f0_pb2", [128, 8], F32)
            cwf = self.sb(st, "f0_cwf", [128, 8], F32)
            cwT = self.sb(st, "f0_cwT", [8, T], BF16)
            cwbs = self.sb(st, "f0_cwb", [128, T], BF16)
            sstg = self.sb(st, "f0_ss", [128, 2, 512], BF16)
            vstg = self.sb(st, "f0_vs", [128, 2, 512], F32)
            ostg = self.sb(st, "f0_os", [128, 2, 512], F32)
            rows = self.sb(st, "f0_rows", [1, 5 * 512], F32)
            ttile = self.sb(st, "f0_tt", [128, 2, 512], F32)
            pz = [self.psum(st, f"f0_pz{i}", [128, 512], F32) for i in range(4)]
            p_s1 = self.psum(st, "f0_ps1", [1, 512], F32)
            p_s2 = self.psum(st, "f0_ps2", [1, 512], F32)
            p_bc = self.psum(st, "f0_pbc", [128, 1024], F32)
            xbf = hall[:, 0:NFT, :]
            xsq = hall[:, NFT:2 * NFT, :]

            def ldg(sy):
                self.ld_s.inc(sy.dma_start(out=gate_sT[:, :, :],
                                           in_=gT.rearrange("(a p) e -> p a e", p=128)), 16)
                self.ld_s.inc(sy.dma_start(out=sel8[:, :], in_=self.c["sel8"][:]), 16)
            self.sy(ldg)
            nldg = self.ld_s.n

            wc_cons = {0: None, 1: None}

            for c in range(NCH_T):
                def ldx(sy):
                    sy.wait_ge(self.s_v.h, self.s_v.n)
                    sy.wait_ge(self.s_sc.h, self.s_sc.n)
                    xr = xin.rearrange("(a p) t -> a p t", p=128)
                    for ft in range(NFT):
                        self.ld_x.inc(sy.dma_start(out=xs[:, ft, :], in_=xr[ft, :, c * T:(c + 1) * T]), 16)
                self.sy(ldx)
                nld = self.ld_x.n

                def wx(e):
                    e.wait_ge(self.ld_x.h, nld)
                    e.wait_ge(self.s_mm.h, self.s_mm.n)
                self.sc(wx)
                self.vc(lambda v: v.wait_ge(self.ld_x.h, nld))

                self.emit_ln_chunk(xs, nbuf, xbf, xsq, rows, ttile, p_s1, p_s2, p_bc, T)
                nv_ln = self.s_v.n
                npo_ln = self.s_po.n

                for tt in range(T // 128):
                    def g_mm(t):
                        if tt == 0:
                            t.wait_ge(self.s_po.h, npo_ln)
                            t.wait_ge(self.ld_s.h, nldg)
                            t.wait_ge(self.s_sc.h, self.s_sc.n)
                            t.wait_ge(self.s_v.h, self.s_v.n)
                        else:
                            t.wait_ge(self.s_sc.h, self.s_sc.n)
                        ins = None
                        for kt in range(NFT):
                            ins = t.matmul(pz[0][:, 0:8], nbuf[:, kt, tt * 128:(tt + 1) * 128],
                                           gate_sT[:, kt, :], start=(kt == 0), stop=(kt == NFT - 1))
                        self.s_mm.inc(ins, 1)
                    self.tn(g_mm)
                    n_gmm = self.s_mm.n

                    def g_mx(v):
                        v.wait_ge(self.s_mm.h, n_gmm)
                        self.s_v.inc(v.reduce_max(gsc[:, 0:1], pz[0][:, 0:8], axis=AX.X), 1)
                        v.wait_ge(self.s_v.h, self.s_v.n)
                        self.s_v.inc(v.tensor_scalar(gsc[:, 1:2], gsc[:, 0:1], -1.0, None, ALU.mult), 1)
                    self.vc(g_mx)
                    n_vmx = self.s_v.n

                    def g_exp(s):
                        s.wait_ge(self.s_v.h, n_vmx)
                        self.s_sc.inc(s.activation(ebuf[:, :], pz[0][:, 0:8], AF.Exp,
                                                   bias=gsc[:, 1:2]), 1)
                    self.sc(g_exp)
                    n_sexp = self.s_sc.n

                    def g_top2(v):
                        v.wait_ge(self.s_sc.h, n_sexp)
                        self.s_v.inc(v.reduce_sum(gsc[:, 2:3], ebuf[:, :], axis=AX.X), 1)
                        v.wait_ge(self.s_v.h, self.s_v.n)
                        self.s_v.inc(v.reciprocal(gsc[:, 3:4], gsc[:, 2:3]), 1)
                        v.wait_ge(self.s_v.h, self.s_v.n)
                        self.s_v.inc(v.tensor_scalar(pbuf[:, :], ebuf[:, :], gsc[:, 3:4], None, ALU.mult), 1)
                        v.wait_ge(self.s_v.h, self.s_v.n)
                        self.s_v.inc(v.reduce_max(gsc[:, 4:5], pbuf[:, :], axis=AX.X), 1)
                        v.wait_ge(self.s_v.h, self.s_v.n)
                        self.s_v.inc(v.tensor_scalar(pbuf2[:, :], pbuf[:, :], gsc[:, 4:5], None,
                                                     ALU.is_equal), 1)
                        v.wait_ge(self.s_v.h, self.s_v.n)
                        self.s_v.inc(v.tensor_scalar(pbuf2[:, :], pbuf2[:, :], -1.0, 1.0,
                                                     ALU.mult, ALU.add), 1)
                        v.wait_ge(self.s_v.h, self.s_v.n)
                        self.s_v.inc(v.tensor_mul(pbuf2[:, :], pbuf[:, :], pbuf2[:, :]), 1)
                        v.wait_ge(self.s_v.h, self.s_v.n)
                        self.s_v.inc(v.reduce_max(gsc[:, 5:6], pbuf2[:, :], axis=AX.X), 1)
                        v.wait_ge(self.s_v.h, self.s_v.n)
                        self.s_v.inc(v.tensor_scalar(pbuf2[:, :], pbuf[:, :], gsc[:, 5:6], None,
                                                     ALU.is_ge), 1)
                        v.wait_ge(self.s_v.h, self.s_v.n)
                        self.s_v.inc(v.tensor_mul(cwf[:, :], pbuf[:, :], pbuf2[:, :]), 1)
                    self.vc(g_top2)
                    n_vcw = self.s_v.n

                    def g_tr(t):
                        t.wait_ge(self.s_v.h, n_vcw)
                        if tt == 0:
                            t.wait_ge(self.s_v.h, self.s_v.n)
                        p = pz[1] if tt < 4 else pz[2]
                        self.s_mm.inc(t.transpose(p[0:8, (tt % 4) * 128:(tt % 4 + 1) * 128],
                                                  cwf[:, :], self.k["id_f32"][:]), 1)
                    self.tn(g_tr)
                n_trcw = self.s_mm.n

                def cw_ev(v):
                    v.wait_ge(self.s_mm.h, n_trcw)
                    v.tensor_copy(cwT[:, 0:512], pz[1][0:8, :])
                    self.s_v.inc(v.tensor_copy(cwT[:, 512:1024], pz[2][0:8, :]), 1)
                self.vc(cw_ev)
                n_cwev = self.s_v.n

                pzA_cons = {0: None, 1: None}
                sA_prev = None
                for unit in range(9):
                    is_sh = (unit == 8)
                    if not is_sh:
                        def cw_bc(t):
                            t.wait_ge(self.s_v.h, self.s_v.n)
                            t.matmul(p_bc[:, 0:512], sel8[:, unit * 128:(unit + 1) * 128],
                                     cwT[:, 0:512], start=True, stop=True)
                            ins = t.matmul(p_bc[:, 512:1024], sel8[:, unit * 128:(unit + 1) * 128],
                                           cwT[:, 512:1024], start=True, stop=True)
                            self.s_mm.inc(ins, 1)
                        self.tn(cw_bc)
                        n_bcmm = self.s_mm.n

                        def cw_evb(v):
                            v.wait_ge(self.s_mm.h, n_bcmm)
                            v.wait_ge(self.s_po.h, self.s_po.n)
                            self.s_v.inc(v.tensor_copy(cwbs[:, :], p_bc[:, :]), 1)
                        self.vc(cw_evb)

                    for ft in range(4):
                        par = (unit * 4 + ft) % 2
                        wb = wcol[par]

                        if is_sh:
                            s1, s2 = sgT, suT
                        else:
                            s1, s2 = wgT[unit], wuT[unit]
                        wcc = wc_cons[par]

                        def ldw(g, s1=s1, par=par, wcc=wcc, ft=ft):
                            if wcc is not None:
                                g.wait_ge(self.s_mm.h, wcc)
                            self.ld_w[par].inc(g.dma_start(out=wb[:, 0:4, :], in_=s1[ft][:, 0:4, :]), 16)
                        self.gp(ldw)

                        def ldwv(v_, s1=s1, par=par, wcc=wcc, ft=ft):
                            if wcc is not None:
                                v_.wait_ge(self.s_mm.h, wcc)
                            self.ld_v2[par].inc(v_.dma_start(out=wb[:, 4:8, :], in_=s1[ft][:, 4:8, :]), 16)
                        self.sy(ldwv)

                        def ldw2(sc_, s2=s2, par=par, wcc=wcc, ft=ft):
                            if wcc is not None:
                                sc_.wait_ge(self.s_mm.h, wcc)
                            self.ld_c2[par].inc(sc_.dma_start(out=wb[:, 8:12, :], in_=s2[ft][:, 0:4, :]), 16)
                        self.sc(ldw2)

                        def ldwy(sy, s2=s2, par=par, wcc=wcc, ft=ft):
                            if wcc is not None:
                                sy.wait_ge(self.s_mm.h, wcc)
                            self.ld_y2[par].inc(sy.dma_start(out=wb[:, 12:16, :], in_=s2[ft][:, 4:8, :]), 16)
                        self.sy(ldwy)
                        nw = self.ld_w[par].n
                        nw2 = self.ld_c2[par].n
                        nw3 = self.ld_v2[par].n
                        nw4 = self.ld_y2[par].n
                        pzA, pzB = pz[2 * par], pz[2 * par + 1]
                        dst_u = 32 + ft if is_sh else unit * 4 + ft

                        sA_pend = []
                        for nch in range(2):
                            pcw = pzA_cons[par]

                            def mm(t):
                                t.wait_ge(self.ld_w[par].h, nw)
                                t.wait_ge(self.ld_c2[par].h, nw2)
                                t.wait_ge(self.ld_v2[par].h, nw3)
                                t.wait_ge(self.ld_y2[par].h, nw4)
                                if unit == 0 and ft == 0 and nch == 0:
                                    t.wait_ge(self.s_sc.h, self.s_sc.n)
                                if pcw is not None:
                                    t.wait_ge(self.s_sc.h, pcw[0])
                                    t.wait_ge(self.s_v.h, pcw[1])
                                for kt in range(NFT):
                                    t.matmul(pzA[:, :], wb[:, kt, :], nbuf[:, kt, nch * 512:(nch + 1) * 512],
                                             start=(kt == 0), stop=(kt == NFT - 1))
                                ins = None
                                for kt in range(NFT):
                                    ins = t.matmul(pzB[:, :], wb[:, 8 + kt, :],
                                                   nbuf[:, kt, nch * 512:(nch + 1) * 512],
                                                   start=(kt == 0), stop=(kt == NFT - 1))
                                self.s_mm.inc(ins, 1)
                            self.tn(mm)
                            n_mm = self.s_mm.n

                            self.s_sc.n += 1
                            n_silu = self.s_sc.n

                            def ev_s(s, n_mm=n_mm, nch=nch, pog=self.s_po.n, pzA=pzA):
                                s.wait_ge(self.s_mm.h, n_mm)
                                s.wait_ge(self.s_po.h, pog)
                                s.activation(sstg[:, nch, :], pzA[:, :],
                                             AF.Silu).then_inc(self.s_sc.h, 1)
                            sA_pend.append(ev_s)

                            def ev_v(v, n_mm=n_mm, n_silu=n_silu, nch=nch):
                                v.wait_ge(self.s_mm.h, n_mm)
                                v.wait_ge(self.s_sc.h, n_silu)
                                v.wait_ge(self.s_po.h, self.s_po.n)
                                self.s_v.inc(v.tensor_mul(vstg[:, nch, :], sstg[:, nch, :], pzB[:, :]), 1)
                            self.vc(ev_v)
                            n_v2 = self.s_v.n

                            def po(g):
                                g.wait_ge(self.s_v.h, n_v2)
                                if is_sh:
                                    self.s_po.inc(g.tensor_copy(hall[:, dst_u, nch * 512:(nch + 1) * 512],
                                                                vstg[:, nch, :]), 1)
                                else:
                                    self.s_po.inc(g.tensor_mul(hall[:, dst_u, nch * 512:(nch + 1) * 512],
                                                               vstg[:, nch, :],
                                                               cwbs[:, nch * 512:(nch + 1) * 512]), 1)
                            self.gp(po)
                            pzA_cons[par] = (n_silu, self.s_v.n)
                        if sA_prev is not None:
                            self.sc(lambda s, ops=sA_prev: [f(s) for f in ops])
                        sA_prev = sA_pend
                        wc_cons[par] = self.s_mm.n

                if sA_prev is not None:
                    self.sc(lambda s, ops=sA_prev: [f(s) for f in ops])
                    sA_prev = None

                n_po0 = self.s_po.n

                def gate_b(t):
                    t.wait_ge(self.s_po.h, n_po0)
                self.tn(gate_b)
                pzB_cons = {0: None, 1: None, 2: None, 3: None}
                for dt in range(8):
                    par = dt % 2
                    wb = wcol[par]

                    wcc = wc_cons[par]

                    def ldw(g, par=par, wcc=wcc, dt=dt):
                        if wcc is not None:
                            g.wait_ge(self.s_mm.h, wcc)
                        for e in range(3):
                            self.ld_w[par].inc(g.dma_start(out=wb[:, 4 * e:4 * e + 4, :],
                                                           in_=wdT[e][dt]), 16)
                    self.gp(ldw)

                    def ldwv(v_, par=par, wcc=wcc, dt=dt):
                        if wcc is not None:
                            v_.wait_ge(self.s_mm.h, wcc)
                        for e in range(3, 5):
                            self.ld_v2[par].inc(v_.dma_start(out=wb[:, 4 * e:4 * e + 4, :],
                                                             in_=wdT[e][dt]), 16)
                    self.sy(ldwv)

                    def ldwc(sc_, par=par, wcc=wcc, dt=dt):
                        if wcc is not None:
                            sc_.wait_ge(self.s_mm.h, wcc)
                        for e in range(5, 7):
                            self.ld_c2[par].inc(sc_.dma_start(out=wb[:, 4 * e:4 * e + 4, :],
                                                              in_=wdT[e][dt]), 16)
                    self.sc(ldwc)

                    def ldwy(sy, par=par, wcc=wcc, dt=dt):
                        if wcc is not None:
                            sy.wait_ge(self.s_mm.h, wcc)
                        self.ld_y2[par].inc(sy.dma_start(out=wb[:, 28:32, :], in_=wdT[7][dt]), 16)
                        self.ld_y2[par].inc(sy.dma_start(out=wb[:, 32:36, :], in_=sdT[dt]), 16)
                    self.sy(ldwy)
                    nw = self.ld_w[par].n
                    nw2 = self.ld_c2[par].n
                    nw3 = self.ld_v2[par].n
                    nw4 = self.ld_y2[par].n
                    pzA, pzB = pz[2 * par], pz[2 * par + 1]
                    for nch in range(2):
                        pzX = pzA if nch == 0 else pzB
                        bank = 2 * par + nch
                        pcw = pzB_cons[bank]

                        def mm(t):
                            t.wait_ge(self.ld_w[par].h, nw)
                            t.wait_ge(self.ld_c2[par].h, nw2)
                            t.wait_ge(self.ld_v2[par].h, nw3)
                            t.wait_ge(self.ld_y2[par].h, nw4)
                            if pcw is not None:
                                t.wait_ge(self.s_v.h, pcw)
                            ins = None
                            for kt in range(36):
                                ins = t.matmul(pzX[:, :], wb[:, kt, :], hall[:, kt, nch * 512:(nch + 1) * 512],
                                               start=(kt == 0), stop=(kt == 35))
                            self.s_mm.inc(ins, 1)
                        self.tn(mm)
                        n_mm = self.s_mm.n

                        def ev(v):
                            v.wait_ge(self.s_mm.h, n_mm)
                            v.wait_ge(self.st_o[nch].h, self.st_o[nch].n)
                            self.s_v.inc(v.scalar_tensor_tensor(
                                ostg[:, nch, :], pzX[:, :], 0.5,
                                xs[:, dt, nch * 512:(nch + 1) * 512], ALU.mult, ALU.add), 1)
                        self.vc(ev)
                        n_v3 = self.s_v.n

                        def stv(sy):
                            sy.wait_ge(self.s_v.h, n_v3)
                            self.st_o[nch].inc(sy.dma_start(
                                out=xout[dt * 128:(dt + 1) * 128,
                                         c * T + nch * 512:c * T + (nch + 1) * 512],
                                in_=ostg[:, nch, :]), 16)
                        self.sy(stv)
                        pzB_cons[bank] = self.s_v.n
                    wc_cons[par] = self.s_mm.n

    def emit_attn0(self, xin, xout):
        T = 1024
        wq_bf = self.wbf["L0_attn_Wq"]; wk_bf = self.wbf["L0_attn_Wk"]
        wv_bf = self.wbf["L0_attn_Wv"]; wo_bf = self.wbf["L0_attn_Wo"]
        cols = self.cols
        with ExitStack() as st:
            n1 = self.sb(st, "a0_n1", [128, NFT, S], BF16)
            vaug = self.sb(st, "a0_va", [128, NTT, H * 65], BF16)
            vaug_f = vaug.reshape([128, NTT * H * 65]).bitcast(F32)
            xs = vaug_f[:, 8192:16384].rearrange("p (a t) -> p a t", a=NFT)
            wsb = self.sb(st, "a0_wsb", [128, NFT, D], BF16)
            f2f = self.sb(st, "a0_f2f", [128, S], F32)
            fsq = self.sb(st, "a0_fsq", [128, S], BF16)
            knt = fsq[:, 0:2048]
            qn = self.sb(st, "a0_qn", [128, S], BF16)
            kn = self.sb(st, "a0_kn", [128, S], BF16)
            wblk = self.sb(st, "a0_wblk", [128, 2, NFT, 128], BF16)
            nsr = self.sb(st, "a0_nsr", [2, 512], F32)
            bias_b = qn.bitcast(F32)[:, 0:D]
            brow = kn.bitcast(F32)[0:1, 0:D]
            fsb = f2f.bitcast(BF16)[:, 3072:6144].rearrange("p (a t) -> p a t", a=3)
            osb = self.sb(st, "a0_osb", [64, 512], F32)
            obuf = self.sb(st, "a0_ob", [64, 2, 512], BF16)
            rdb = self.sb(st, "a0_rdb", [1, 512], F32)
            rows = f2f[0:1, 0:2560]
            ttile = f2f[:, 2560:3584].rearrange("p (a t) -> p a t", a=2)
            kn_f = kn.bitcast(F32)
            tstg = kn_f[:, 0:512]
            ostg = kn_f[:, 512:1536].rearrange("p (a t) -> p a t", a=2)
            pzC = self.psum(st, "a0_pzC", [128, 1024], F32)
            p_s1 = pzC[:, 0:512]
            p_s2 = pzC[:, 512:1024]
            p_bc = self.psum(st, "a0_pbc", [128, 1024], F32)
            pzA = self.psum(st, "a0_pzA", [128, 1024], F32)
            pzB = self.psum(st, "a0_pzB", [128, 1024], F32)
            pz = [pzA[:, 0:512], pzA[:, 512:1024], pzB[:, 0:512], pzB[:, 512:1024]]
            p_ns = p_s1[0:2, :]
            p_o = p_bc[0:65, 512:1024]
            wneg = self.sb(st, "a0_wng", [128, 65], BF16)
            ccol = self.sb(st, "a0_cc", [65, 1], F32)
            xbf = vaug.reshape([128, NTT * H * 65])[:, 0:NFT * T]
            xsq = vaug.reshape([128, NTT * H * 65])[:, NFT * T:2 * NFT * T]
            xbf = xbf.rearrange("p (a t) -> p a t", a=NFT)
            xsq = xsq.rearrange("p (a t) -> p a t", a=NFT)

            for c in range(S // T):
                def ldx(sy):
                    sy.wait_ge(self.s_v.h, self.s_v.n)
                    sy.wait_ge(self.s_sc.h, self.s_sc.n)
                    xr = xin.rearrange("(a p) t -> a p t", p=128)
                    for ft in range(NFT):
                        self.ld_x.inc(sy.dma_start(out=xs[:, ft, :], in_=xr[ft, :, c * T:(c + 1) * T]), 16)
                self.sy(ldx)
                nld = self.ld_x.n

                def wx(e):
                    e.wait_ge(self.ld_x.h, nld)
                    e.wait_ge(self.s_mm.h, self.s_mm.n)
                self.sc(wx)
                self.vc(lambda v: v.wait_ge(self.ld_x.h, nld))
                nbufc = n1[:, :, c * T:(c + 1) * T]
                self.emit_ln_chunk(xs, nbufc, xbf, xsq, rows, ttile, p_s1[0:1, :], p_s2[0:1, :], p_bc, T)
            n_po_ln = self.s_po.n

            def ldwv(sy):
                self.ld_s.inc(sy.dma_start(out=wsb[:, :, :],
                                           in_=wv_bf.rearrange("(a p) m -> p a m", p=128)), 16)
                self.ld_s.inc(sy.dma_start(out=brow[:, :],
                                           in_=self.win["L0_attn_bv"].rearrange("(o d) -> o d", o=1)), 16)
            self.sy(ldwv)
            n_ldv = self.ld_s.n

            def bias_mm(t):
                t.wait_ge(self.ld_s.h, n_ldv)
                t.wait_ge(self.s_po.h, n_po_ln)
                t.wait_ge(self.s_sc.h, self.s_sc.n)
                t.wait_ge(self.s_v.h, self.s_v.n)
                t.matmul(p_bc[:, 0:512], self.k["ones_row_f32"][:, :], brow[:, 0:512], start=True, stop=True)
                ins = t.matmul(p_bc[:, 512:1024], self.k["ones_row_f32"][:, :], brow[:, 512:1024],
                               start=True, stop=True)
                self.s_mm.inc(ins, 1)
            self.tn(bias_mm)
            n_bmm = self.s_mm.n

            def bias_ev(v):
                v.wait_ge(self.s_mm.h, n_bmm)
                self.s_v.inc(v.tensor_copy(bias_b[:, :], p_bc[:, :]), 1)
                self.s_v.inc(v.memset(vaug.rearrange("p a (h c) -> p a h c", c=65)[:, :, :, 64:65], 1.0), 1)
            self.vc(bias_ev)
            n_bev = self.s_v.n

            for tt in range(NTT):
                def vmm(t):
                    if tt == 0:
                        t.wait_ge(self.s_v.h, n_bev)
                    t.wait_ge(self.s_v.h, self.s_v.n)
                    ins = None
                    for half in range(2):
                        for kt in range(NFT):
                            ins = t.matmul(pz[half][:, :], n1[:, kt, tt * 128:(tt + 1) * 128],
                                           wsb[:, kt, half * 512:(half + 1) * 512],
                                           start=(kt == 0), stop=(kt == NFT - 1))
                    self.s_mm.inc(ins, 1)
                self.tn(vmm)
                n_vmm = self.s_mm.n

                def vev(v):
                    v.wait_ge(self.s_mm.h, n_vmm)
                    dst = vaug[:, tt, :].rearrange("p (h c) -> p h c", c=65)[:, :, 0:64]
                    src0 = pz[0].rearrange("p (h c) -> p h c", c=64)
                    src1 = pz[1].rearrange("p (h c) -> p h c", c=64)
                    bb0 = bias_b[:, 0:512].rearrange("p (h c) -> p h c", c=64)
                    bb1 = bias_b[:, 512:1024].rearrange("p (h c) -> p h c", c=64)
                    v.tensor_add(dst[:, 0:8, :], src0[:, :, :], bb0[:, :, :])
                    self.s_v.inc(v.tensor_add(dst[:, 8:16, :], src1[:, :, :], bb1[:, :, :]), 1)
                self.vc(vev)

            for hp in range(8):
                for which, (w_bf, bcol, dst, dstsq_needed) in enumerate(
                        [(wq_bf, "bq", qn, True), (wk_bf, "bk", kn, True)]):
                    def ldw(sy):
                        sy.wait_ge(self.s_mm.h, self.s_mm.n)
                        self.ld_s.inc(sy.dma_start(
                            out=wblk[:, which, :, :],
                            in_=w_bf.rearrange("(a p) m -> p a m", p=128)[:, :, hp * 128:(hp + 1) * 128]), 16)
                    self.sy(ldw)
                    n_lw = self.ld_s.n
                    for qc in range(8):
                        def qmm(t):
                            t.wait_ge(self.ld_s.h, n_lw)
                            t.wait_ge(self.s_v.h, self.s_v.n)
                            ins = None
                            for kt in range(NFT):
                                ins = t.matmul(pz[2][:, :], wblk[:, which, kt, :],
                                               n1[:, kt, qc * 512:(qc + 1) * 512],
                                               start=(kt == 0), stop=(kt == NFT - 1))
                            self.s_mm.inc(ins, 1)
                        self.tn(qmm)
                        n_qmm = self.s_mm.n

                        def qev(v):
                            v.wait_ge(self.s_mm.h, n_qmm)
                            v.wait_ge(self.s_po.h, self.s_po.n)
                            v.wait_ge(self.s_v.h, self.s_v.n)
                            self.s_v.inc(v.tensor_scalar(f2f[:, qc * 512:(qc + 1) * 512], pz[2][:, :],
                                                         cols[bcol][:, hp:hp + 1], None, ALU.add), 1)
                        self.vc(qev)
                        n_qb = self.s_v.n

                        def qsq_(g):
                            g.wait_ge(self.s_v.h, n_qb)
                            g.wait_ge(self.s_mm.h, self.s_mm.n)
                            self.s_po.inc(g.tensor_mul(fsq[:, qc * 512:(qc + 1) * 512],
                                                       f2f[:, qc * 512:(qc + 1) * 512],
                                                       f2f[:, qc * 512:(qc + 1) * 512]), 1)
                        self.gp(qsq_)
                        n_qsq = self.s_po.n

                        def nsmm(t):
                            t.wait_ge(self.s_po.h, n_qsq)
                            t.wait_ge(self.s_sc.h, self.s_sc.n)
                            t.wait_ge(self.s_v.h, self.s_v.n)
                            self.s_mm.inc(t.matmul(p_ns[:, :], self.k["head_ones"][:, :],
                                                   fsq[:, qc * 512:(qc + 1) * 512],
                                                   start=True, stop=True), 1)
                        self.tn(nsmm)
                        n_ns = self.s_mm.n

                        def s_rq(s):
                            s.wait_ge(self.s_mm.h, n_ns)
                            s.wait_ge(self.s_v.h, self.s_v.n)
                            self.s_sc.inc(s.activation(nsr[:, :], p_ns[:, :], AF.Ln, bias=1e-16), 1)
                            s.wait_ge(self.s_sc.h, self.s_sc.n)
                            self.s_sc.inc(s.activation(nsr[:, :], nsr[:, :], AF.Exp, scale=-0.5), 1)
                        self.sc(s_rq)
                        n_rq = self.s_sc.n

                        def bcq(t):
                            t.wait_ge(self.s_sc.h, n_rq)
                            t.wait_ge(self.s_v.h, self.s_v.n)
                            bc = self.k["head_bcast_neg"] if which == 1 else self.k["head_bcast_f32"]
                            self.s_mm.inc(t.matmul(pz[3][:, :], bc[:, :], nsr[:, :],
                                                   start=True, stop=True), 1)
                        self.tn(bcq)
                        n_bc = self.s_mm.n

                        def v_mulq(v):
                            v.wait_ge(self.s_mm.h, n_bc)
                            self.s_v.inc(v.tensor_mul(dst[:, qc * 512:(qc + 1) * 512],
                                                      f2f[:, qc * 512:(qc + 1) * 512], pz[3][:, :]), 1)
                        self.vc(v_mulq)

                CDEN = float((-3.1 * S + 1e-6) / 4.41)
                for hh in range(2):
                    h = hp * 2 + hh
                    base = 64 * hh

                    for rnd in range(4):
                        def ktr(t, rnd=rnd):
                            t.wait_ge(self.s_v.h, self.s_v.n)
                            t.wait_ge(self.s_sc.h, self.s_sc.n)
                            ins = None
                            for j in range(8):
                                kc = rnd * 8 + j
                                ins = t.matmul(pzA[:, j * 64:(j + 1) * 64],
                                               kn[base:base + 64, kc * 128:(kc + 1) * 128],
                                               self.k["id_bf"][0:64, 0:64] if base == 0 else
                                               self.k["id_bf"][64:128, 64:128],
                                               start=True, stop=True)
                            self.s_mm.inc(ins, 1)
                        self.tn(ktr)
                        n_ktr = self.s_mm.n

                        def kev(v, rnd=rnd, n_ktr=n_ktr):
                            v.wait_ge(self.s_mm.h, n_ktr)
                            self.s_v.inc(v.tensor_copy(knt[:, rnd * 512:(rnd + 1) * 512], pzA[:, 0:512]), 1)
                        self.vc(kev)
                    n_kev = self.s_v.n

                    def prep_mm(t):
                        t.wait_ge(self.s_v.h, n_kev)
                        t.wait_ge(self.s_sc.h, self.s_sc.n)
                        ins = None
                        for kc in range(NTT):
                            ins = t.matmul(p_s1[0:65, 0:1], vaug[:, kc, h * 65:(h + 1) * 65],
                                           self.k["ones_col_bf"][:, :],
                                           start=(kc == 0), stop=(kc == NTT - 1))
                        self.s_mm.inc(ins, 1)
                        ins = None
                        for kc in range(NTT):
                            ins = t.matmul(p_s2[0:64, 0:65], knt[:, kc * 64:(kc + 1) * 64],
                                           vaug[:, kc, h * 65:(h + 1) * 65],
                                           start=(kc == 0), stop=(kc == NTT - 1))
                        self.s_mm.inc(ins, 1)
                    self.tn(prep_mm)
                    n_prep = self.s_mm.n

                    def prep_ev(v):
                        v.wait_ge(self.s_mm.h, n_prep)
                        self.s_v.inc(v.tensor_scalar(ccol[:, :], p_s1[0:65, 0:1],
                                                     float(-3.1 / 4.41), None, ALU.mult), 1)
                    self.vc(prep_ev)

                    def prep_ev2(s):
                        s.wait_ge(self.s_mm.h, n_prep)
                        self.s_sc.inc(s.activation(wneg[base:base + 64, :], p_s2[0:64, 0:65],
                                                   AF.Copy, scale=float(1.0 / 4.41)), 1)
                    self.sc(prep_ev2)
                    n_wneg = self.s_sc.n

                    NS_SC = 16
                    po_cons_v = self.s_v.n
                    r_done = {0: None, 1: None, 2: None}
                    av_done = {0: None, 1: None, 2: None}
                    PD = [pzA, pzB, pzC]
                    for qc in range(8):
                        t_ops, v_ops, s_ops = [], [], []
                        t_dot = {}

                        def emit_av(kcc):
                            par = kcc % 3

                            def op_av(t, kcc=kcc, par=par, w=r_done[par], h=h):
                                sem_h = self.s_sc.h if w[0] == "s" else self.s_v.h
                                t.wait_ge(sem_h, w[1])
                                for half in range(2):
                                    kc = 2 * kcc + half
                                    ins = t.matmul(p_o[:, :], vaug[:, kc, h * 65:(h + 1) * 65],
                                                   fsb[:, par, half * 512:(half + 1) * 512],
                                                   start=(kc == 0), stop=False)
                                ins.then_inc(self.s_mm.h, 1)
                            self.s_mm.n += 1
                            t_ops.append(op_av)
                            av_done[par] = self.s_mm.n

                        for kcc in range(16):
                            par = kcc % 3
                            pd = PD[par]
                            use_sc = (kcc * NS_SC) // 16 != ((kcc + 1) * NS_SC) // 16

                            def op_dot(t, kcc=kcc, par=par, pd=pd, w=r_done[par],
                                       pocv=po_cons_v, qc=qc):
                                if w is not None:
                                    sem_h = self.s_sc.h if w[0] == "s" else self.s_v.h
                                    t.wait_ge(sem_h, w[1])
                                if kcc == 0:
                                    t.wait_ge(self.s_sc.h, n_wneg)
                                    t.wait_ge(self.s_v.h, pocv)
                                for half in range(2):
                                    kc = 2 * kcc + half
                                    ins = t.matmul(pd[:, half * 512:(half + 1) * 512],
                                                   kn[base:base + 64, kc * 128:(kc + 1) * 128],
                                                   qn[base:base + 64, qc * 512:(qc + 1) * 512],
                                                   start=True, stop=True)
                                ins.then_inc(self.s_mm.h, 1)
                            self.s_mm.n += 1
                            t_ops.append(op_dot)
                            t_dot[kcc] = self.s_mm.n

                            if use_sc:
                                def op_l(s, par=par, pd=pd, w=t_dot[kcc]):
                                    s.wait_ge(self.s_mm.h, w)
                                    ins = s.activation(pd[:, :], pd[:, :], AF.Ln, bias=1.1)
                                    ins.then_inc(self.s_sc.h, 1)
                                self.s_sc.n += 1
                                s_ops.append(op_l)

                                def op_x(s, par=par, pd=pd, wav=av_done[par]):
                                    if wav is not None:
                                        s.wait_ge(self.s_mm.h, wav)
                                    ins = s.activation(fsb[:, par, :], pd[:, :], AF.Exp, scale=-1.0)
                                    ins.then_inc(self.s_sc.h, 1)
                                self.s_sc.n += 1
                                s_ops.append(op_x)
                                r_done[par] = ("s", self.s_sc.n)
                            else:
                                def op_u(v, par=par, pd=pd, w=t_dot[kcc]):
                                    v.wait_ge(self.s_mm.h, w)
                                    ins = v.tensor_scalar(pd[:, :], pd[:, :], 1.1, None, ALU.add)
                                    ins.then_inc(self.s_v.h, 1)
                                self.s_v.n += 1
                                v_ops.append(op_u)

                                def op_r(v, par=par, pd=pd, wav=av_done[par]):
                                    if wav is not None:
                                        v.wait_ge(self.s_mm.h, wav)
                                    with self.nc.allow_low_precision(
                                            reason="r=1/u feeds bf16 matmul operands, like prior bf16 scores"):
                                        ins = v.reciprocal(fsb[:, par, :], pd[:, :])
                                    ins.then_inc(self.s_v.h, 1)
                                self.s_v.n += 1
                                v_ops.append(op_r)
                                r_done[par] = ("v", self.s_v.n)

                            if kcc >= 2:
                                emit_av(kcc - 2)
                        emit_av(14)
                        emit_av(15)

                        def op_corr(t, qc=qc):
                            ins = t.matmul(p_o[:, :], wneg[base:base + 64, :],
                                           qn[base:base + 64, qc * 512:(qc + 1) * 512],
                                           start=False, stop=True)
                            ins.then_inc(self.s_mm.h, 1)
                        self.s_mm.n += 1
                        t_ops.append(op_corr)
                        n_corr = self.s_mm.n

                        self.tn(lambda t, ops=t_ops: [f(t) for f in ops])
                        self.vc(lambda v, ops=v_ops: [f(v) for f in ops])
                        self.sc(lambda s, ops=s_ops: [f(s) for f in ops])

                        def v_dn(v, n_corr=n_corr, qc=qc):
                            v.wait_ge(self.s_mm.h, n_corr)
                            self.s_v.inc(v.tensor_scalar(rdb[:, :], p_o[64:65, :], CDEN, None, ALU.add), 1)
                            v.wait_ge(self.s_v.h, self.s_v.n)
                            self.s_v.inc(v.reciprocal(rdb[:, :], rdb[:, :]), 1)
                            self.s_v.inc(v.tensor_scalar(osb[:, :], p_o[0:64, :], ccol[0:64, :],
                                                         None, ALU.add), 1)
                        self.vc(v_dn)
                        n_dn = self.s_v.n

                        def bcd(t, n_dn=n_dn):
                            t.wait_ge(self.s_v.h, n_dn)
                            self.s_mm.inc(t.matmul(p_bc[:, 0:512], self.k["ones_row_f32"][:, :],
                                                   rdb[:, :], start=True, stop=True), 1)
                        self.tn(bcd)
                        n_bcd = self.s_mm.n

                        def v_onorm(v, n_bcd=n_bcd, qc=qc):
                            v.wait_ge(self.s_mm.h, n_bcd)
                            v.wait_ge(self.st_o[qc % 2].h, self.st_o[qc % 2].n)
                            self.s_v.inc(v.tensor_mul(obuf[:, qc % 2, :], osb[:, :], p_bc[0:64, 0:512]), 1)
                        self.vc(v_onorm)
                        n_on = self.s_v.n
                        po_cons_v = self.s_v.n

                        def stob(sy, n_on=n_on, h=h, qc=qc):
                            sy.wait_ge(self.s_v.h, n_on)
                            self.st_o[qc % 2].inc(sy.dma_start(
                                out=self.attn_sp[h * 64:(h + 1) * 64, qc * 512:(qc + 1) * 512],
                                in_=obuf[:, qc % 2, :]), 16)
                        self.sy(stob)

            def ldwo(sy):
                sy.wait_ge(self.s_mm.h, self.s_mm.n)
                self.ld_s.inc(sy.dma_start(out=wsb[:, :, :],
                                           in_=wo_bf.rearrange("(a p) m -> p a m", p=128)), 16)
            self.sy(ldwo)
            n_lwo = self.ld_s.n
            att = f2f.rearrange("p (a t) -> p a t", a=4)
            attc = fsq.rearrange("p (a t) -> p a t", a=4)
            attc2 = qn.rearrange("p (a t) -> p a t", a=4)
            for c in range(S // T):
                def lda(sy):
                    sy.wait_ge(self.s_v.h, self.s_v.n)
                    sy.wait_ge(self.s_sc.h, self.s_sc.n)
                    sy.wait_ge(self.s_mm.h, self.s_mm.n)
                    ar = self.attn_sp.rearrange("(a p) t -> a p t", p=128)
                    xr = xin.rearrange("(a p) t -> a p t", p=128)
                    for ft in range(NFT):
                        dst = attc[:, ft, :] if ft < 4 else attc2[:, ft - 4, :]
                        self.ld_x.inc(sy.dma_start(out=dst, in_=ar[ft, :, c * T:(c + 1) * T]), 16)
                    for ft in range(NFT):
                        self.ld_x.inc(sy.dma_start(out=xs[:, ft, :], in_=xr[ft, :, c * T:(c + 1) * T]), 16)
                self.sy(lda)
                n_la = self.ld_x.n
                for dt in range(8):
                    for nch in range(2):
                        pzX = pz[2 * (dt % 2) + nch]

                        def omm(t):
                            t.wait_ge(self.ld_x.h, n_la)
                            t.wait_ge(self.ld_s.h, n_lwo)
                            t.wait_ge(self.s_v.h, self.s_v.n)
                            ins = None
                            for kt in range(NFT):
                                rhs = attc[:, kt, nch * 512:(nch + 1) * 512] if kt < 4 else \
                                    attc2[:, kt - 4, nch * 512:(nch + 1) * 512]
                                ins = t.matmul(pzX[:, :], wsb[:, kt, dt * 128:(dt + 1) * 128], rhs,
                                               start=(kt == 0), stop=(kt == NFT - 1))
                            self.s_mm.inc(ins, 1)
                        self.tn(omm)
                        n_omm = self.s_mm.n

                        def oadd(v):
                            v.wait_ge(self.s_mm.h, n_omm)
                            v.wait_ge(self.st_o[nch].h, self.st_o[nch].n)
                            self.s_v.inc(v.scalar_tensor_tensor(
                                ostg[:, nch, :], pzX[:, :], cols["bo"][:, dt:dt + 1],
                                xs[:, dt, nch * 512:(nch + 1) * 512], ALU.add, ALU.add), 1)
                        self.vc(oadd)
                        n_oadd = self.s_v.n

                        def ost(sy):
                            sy.wait_ge(self.s_v.h, n_oadd)
                            self.st_o[nch].inc(sy.dma_start(
                                out=xout[dt * 128:(dt + 1) * 128,
                                         c * T + nch * 512:c * T + (nch + 1) * 512],
                                in_=ostg[:, nch, :]), 16)
                        self.sy(ost)

    def emit_attn1(self, xin, xout):
        T = 1024
        wqT = self.wT["L1_attn_Wq_w"]; wkT = self.wT["L1_attn_Wk_w"]
        wvT = self.wT["L1_attn_Wv_w"]; woT = self.wT["L1_attn_out_w"]
        qpT = self.wT["L1_attn_qproj_w"]; kpT = self.wT["L1_attn_kproj_w"]
        cqT = self.wT["L1_attn_convq_w"]; ckT = self.wT["L1_attn_convk_w"]
        t2T = self.wT["L1_attn_tau2_w"]; d2T = self.wT["L1_attn_del2_w"]
        cols = self.cols
        with ExitStack() as st:
            n1 = self.sb(st, "a1_n1", [128, NFT, S], BF16)
            BIG = self.sb(st, "a1_big", [128, NTT * H * 65], BF16)
            vaug = BIG.rearrange("p (a c) -> p a c", a=NTT)
            q_in = BIG[:, 0:NFT * S].rearrange("p (a t) -> p a t", a=NFT)
            dwg = BIG[:, 0:16 * 1024].rearrange("p (a t) -> p a t", a=16)
            BIG_f = BIG.bitcast(F32)
            xs = BIG_f[:, 8192:16384].rearrange("p (a t) -> p a t", a=NFT)
            xbf = BIG[:, 0:NFT * T].rearrange("p (a t) -> p a t", a=NFT)
            xsq = BIG[:, NFT * T:2 * NFT * T].rearrange("p (a t) -> p a t", a=NFT)
            q3buf = self.sb(st, "a1_q3", [128, 8192], BF16)
            q3 = q3buf.rearrange("p (a t) -> p a t", a=8)[:, :, 0:1024]
            q3v = q3buf.rearrange("p (a t) -> p a t", a=8)
            qha = q3buf[0:66, 0:4096]
            kha = q3buf[0:66, 4096:8192]
            wsb = self.sb(st, "a1_wsb", [128, NFT, D], BF16)
            wblk = [self.sb(st, f"a1_wb{i}", [128, 24, 128], BF16) for i in range(2)]
            t2blk = self.sb(st, "a1_t2b", [128, 16, 16], BF16)
            accb = self.sb(st, "a1_acc", [128, T], F32)
            drows = self.sb(st, "a1_dr", [128, T], F32)
            tau_bf = self.sb(st, "a1_tau", [16, S], BF16)
            dstage = drows[0:16, :]
            taubc_sb = drows.bitcast(BF16)[:, 0:1024].rearrange("p (a t) -> p a t", a=2)
            bias_b = self.sb(st, "a1_bb", [128, D], F32)
            brow = drows[0:1, 0:D]
            wblk1_flat = wblk[1].reshape([128, 24 * 128])
            fsb = wblk1_flat[:, 512:2560].rearrange("p (a t) -> p a t", a=4)
            osb = accb[0:64, 512:1024]
            obuf = bias_b.bitcast(BF16)[0:64, 0:1024].rearrange("p (a t) -> p a t", a=2)
            rdb = accb[0:1, 0:512]
            q3_f = q3buf.bitcast(F32)
            rows = q3_f[0:1, 0:2560]
            ttile = q3_f[:, 2560:3584].rearrange("p (a t) -> p a t", a=2)
            ostg = wblk[0].reshape([128, 24 * 128]).bitcast(F32)[:, 0:1024].rearrange(
                "p (a t) -> p a t", a=2)
            qstg = self.sb(st, "a1_qs", [128, 2, 512], BF16)
            p_s1 = self.psum(st, "a1_ps1", [1, 512], F32)
            p_s2 = self.psum(st, "a1_ps2", [1, 512], F32)
            p_bc = self.psum(st, "a1_pbc", [128, 1024], F32)
            pz = [self.psum(st, f"a1_pz{i}", [128, 512], F32) for i in range(4)]
            p_t = p_bc[0:16, 0:512]
            p_tr = p_bc[0:128, 512:1024]
            p_o = self.psum(st, "a1_po", [65, 512], F32) if False else None
            p_o = pz[3][0:65, :]

            for c in range(S // T):
                def ldx(sy):
                    sy.wait_ge(self.s_v.h, self.s_v.n)
                    sy.wait_ge(self.s_sc.h, self.s_sc.n)
                    xr = xin.rearrange("(a p) t -> a p t", p=128)
                    for ft in range(NFT):
                        self.ld_x.inc(sy.dma_start(out=xs[:, ft, :], in_=xr[ft, :, c * T:(c + 1) * T]), 16)
                self.sy(ldx)
                nld = self.ld_x.n

                def wx(e):
                    e.wait_ge(self.ld_x.h, nld)
                    e.wait_ge(self.s_mm.h, self.s_mm.n)
                self.sc(wx)
                self.vc(lambda v: v.wait_ge(self.ld_x.h, nld))
                nbufc = n1[:, :, c * T:(c + 1) * T]
                self.emit_ln_chunk(xs, nbufc, xbf, xsq, rows, ttile, p_s1, p_s2, p_bc, T)
            n_po_ln = self.s_po.n

            hi_sb = self.sb(st, "a1_dhi", [16, T], BF16)
            lo_sb = self.sb(st, "a1_dlo", [16, T], BF16)

            def ln_done(e):
                e.wait_ge(self.s_po.h, n_po_ln)
            self.tn(ln_done)

            for which, (wcolk, bcolk, w2_, b2c) in enumerate(
                    [("t1w", "t1b", t2T, "t2b"), (("d1w"), ("d1b"), d2T, "d2b")]):
                def ldt2(sy):
                    sy.wait_ge(self.s_mm.h, self.s_mm.n)
                    wv_ = w2_.rearrange("(a p q) h -> a p (q h)", p=128, q=2)
                    for r in range(2):
                        for ct in range(8):
                            self.ld_s.inc(sy.dma_start(out=t2blk[:, r * 8 + ct, :],
                                                       in_=wv_[ct, :, r * 16:(r + 1) * 16]), 16)
                self.sy(ldt2)
                n_lt2 = self.ld_s.n

                for c in range(S // T):
                    g0, g1 = c * T, (c + 1) * T
                    for r in range(2):
                        for ct in range(8):
                            wc = cols[wcolk]
                            col0 = r * 24 + ct * 3

                            def dw(v):
                                v.wait_ge(self.s_sc.h, self.s_sc.n)
                                if which == 0 and c == 0 and r == 0 and ct == 0:
                                    v.wait_ge(self.s_po.h, n_po_ln)
                                self.s_v.inc(v.tensor_scalar(accb[:, :], n1[:, ct, g0:g1],
                                                             wc[:, col0 + 1:col0 + 2], None, ALU.mult), 1)
                                v.wait_ge(self.s_v.h, self.s_v.n)
                                lo = 1 if c == 0 else 0
                                self.s_v.inc(v.scalar_tensor_tensor(
                                    accb[:, lo:T], n1[:, ct, g0 + lo - 1:g1 - 1],
                                    wc[:, col0:col0 + 1], accb[:, lo:T], ALU.mult, ALU.add), 1)
                                v.wait_ge(self.s_v.h, self.s_v.n)
                                hi = T - 1 if c == S // T - 1 else T
                                self.s_v.inc(v.scalar_tensor_tensor(
                                    accb[:, 0:hi], n1[:, ct, g0 + 1:g0 + hi + 1],
                                    wc[:, col0 + 2:col0 + 3], accb[:, 0:hi], ALU.mult, ALU.add), 1)
                            self.vc(dw)
                            n_dw = self.s_v.n

                            def gel(s):
                                s.wait_ge(self.s_v.h, n_dw)
                                s.wait_ge(self.s_mm.h, self.s_mm.n)
                                self.s_sc.inc(s.activation(dwg[:, r * 8 + ct, :], accb[:, :],
                                                           AF.Gelu, bias=cols[bcolk][:, r * 8 + ct:r * 8 + ct + 1]), 1)
                            self.sc(gel)
                    n_gel = self.s_sc.n

                    for nch in range(2):
                        def t2mm(t):
                            t.wait_ge(self.s_sc.h, n_gel)
                            t.wait_ge(self.ld_s.h, n_lt2)
                            t.wait_ge(self.s_sc.h, self.s_sc.n)
                            t.wait_ge(self.s_v.h, self.s_v.n)
                            ins = None
                            for u in range(16):
                                ins = t.matmul(p_t[:, :], t2blk[:, u, :], dwg[:, u, nch * 512:(nch + 1) * 512],
                                               start=(u == 0), stop=(u == 15))
                            self.s_mm.inc(ins, 1)
                        self.tn(t2mm)
                        n_t2 = self.s_mm.n

                        def sig(s):
                            s.wait_ge(self.s_mm.h, n_t2)
                            if which == 0:
                                self.s_sc.inc(s.activation(tau_bf[:, g0 + nch * 512:g0 + (nch + 1) * 512],
                                                           p_t[:, :], AF.Sigmoid,
                                                           bias=cols[b2c][:, 0:1]), 1)
                            else:
                                s.wait_ge(self.s_v.h, self.s_v.n)
                                self.s_sc.inc(s.activation(dstage[:, nch * 512:(nch + 1) * 512],
                                                           p_t[:, :], AF.Sigmoid,
                                                           bias=cols[b2c][:, 0:1]), 1)
                        self.sc(sig)
                    if which == 1:
                        n_sig = self.s_sc.n
                        nst0 = self.st_s.n

                        def d_hi(s):
                            s.wait_ge(self.st_s.h, nst0)
                            self.s_sc.inc(s.activation(hi_sb[:, :], dstage[:, :], AF.Copy), 1)
                        self.sc(d_hi)
                        n_hi = self.s_sc.n

                        def d_lo(v):
                            v.wait_ge(self.s_sc.h, n_hi)
                            self.s_v.inc(v.tensor_sub(lo_sb[:, :], dstage[:, :], hi_sb[:, :]), 1)
                        self.vc(d_lo)
                        n_lo = self.s_v.n

                        def d_st(sy):
                            sy.wait_ge(self.s_v.h, n_lo)
                            self.st_s.inc(sy.dma_start(out=self.d_sp[0][:, g0:g1], in_=hi_sb[:, :]), 16)
                            self.st_s.inc(sy.dma_start(out=self.d_sp[1][:, g0:g1], in_=lo_sb[:, :]), 16)
                        self.sy(d_st)

            wb_cons = {0: None, 1: None}
            pzq_cons = {0: None, 1: None, 2: None, 3: None}
            n_p1_sc = self.s_sc.n
            n_p1_v = self.s_v.n
            tbc_ev = {0: n_p1_sc, 1: n_p1_sc}
            tbc_cons = {0: None, 1: None}
            for which, (wT_, bcol, cT_, cbcol, pT_, pbcol, spill) in enumerate(
                    [(wqT, "qb", cqT, "cqb", qpT, "qpb", self.q_sp),
                     (wkT, "kb", ckT, "ckb", kpT, "kpb", self.k_sp)]):
                for mt in range(8):
                    par = mt % 2
                    wcc = wb_cons[par]

                    def ldw(g, wT_=wT_, par=par, wcc=wcc, mt=mt):
                        if wcc is not None:
                            g.wait_ge(self.s_mm.h, wcc)
                        sv = wT_.rearrange("(a p) m -> p a m", p=128)
                        self.ld_w[par].inc(g.dma_start(out=wblk[par][:, 0:8, :],
                                                       in_=sv[:, :, mt * 128:(mt + 1) * 128]), 16)
                    self.gp(ldw)
                    nw = self.ld_w[par].n
                    for qc in range(8):
                        bank = qc % 4
                        pzX = pz[bank]
                        pcw = pzq_cons[bank]

                        def mmq(t, par=par, nw=nw, pcw=pcw, which=which, mt=mt, qc=qc, pzX=pzX):
                            t.wait_ge(self.ld_w[par].h, nw)
                            if pcw is not None:
                                t.wait_ge(self.s_v.h, pcw)
                            if which == 0 and mt == 0 and qc == 0:
                                t.wait_ge(self.s_sc.h, self.s_sc.n)
                                t.wait_ge(self.s_v.h, self.s_v.n)
                            ins = None
                            for kt in range(NFT):
                                ins = t.matmul(pzX[:, :], wblk[par][:, kt, :],
                                               n1[:, kt, qc * 512:(qc + 1) * 512],
                                               start=(kt == 0), stop=(kt == NFT - 1))
                            self.s_mm.inc(ins, 1)
                        self.tn(mmq)
                        n_mmq = self.s_mm.n

                        def evq(v, n_mmq=n_mmq, mt=mt, qc=qc, pzX=pzX, bcol=bcol):
                            v.wait_ge(self.s_mm.h, n_mmq)
                            self.s_v.inc(v.tensor_scalar(q_in[:, mt, qc * 512:(qc + 1) * 512],
                                                         pzX[:, :], cols[bcol][:, mt:mt + 1],
                                                         None, ALU.add), 1)
                        self.vc(evq)
                        pzq_cons[bank] = self.s_v.n
                    wb_cons[par] = self.s_mm.n

                n_qin_done = self.s_v.n
                for c in range(S // T):
                    g0, g1 = c * T, (c + 1) * T
                    for ot in range(8):
                        par = ot % 2
                        wcc = wb_cons[par]

                        def ldc(g, cT_=cT_, par=par, wcc=wcc, ot=ot):
                            if wcc is not None:
                                g.wait_ge(self.s_mm.h, wcc)
                            for tap in range(2):
                                sv = cT_[tap].rearrange("(a p) m -> p a m", p=128)
                                self.ld_w[par].inc(g.dma_start(out=wblk[par][:, 8 * tap:8 * tap + 8, :],
                                                               in_=sv[:, :, ot * 128:(ot + 1) * 128]), 16)
                        self.gp(ldc)

                        def ldc2(sc_, cT_=cT_, par=par, wcc=wcc, ot=ot):
                            if wcc is not None:
                                sc_.wait_ge(self.s_mm.h, wcc)
                            sv = cT_[2].rearrange("(a p) m -> p a m", p=128)
                            self.ld_c2[par].inc(sc_.dma_start(out=wblk[par][:, 16:24, :],
                                                              in_=sv[:, :, ot * 128:(ot + 1) * 128]), 16)
                        self.sc(ldc2)
                        nwc = self.ld_w[par].n
                        nwc2 = self.ld_c2[par].n
                        for nch in range(2):
                            bank = nch
                            pzX = pz[bank]
                            pcw = pzq_cons[bank]
                            o0 = g0 + nch * 512

                            def mmc(t, par=par, nwc=nwc, nwc2=nwc2, pcw=pcw, o0=o0, pzX=pzX,
                                    c=c, nq=n_qin_done):
                                t.wait_ge(self.ld_w[par].h, nwc)
                                t.wait_ge(self.ld_c2[par].h, nwc2)
                                if pcw is not None:
                                    t.wait_ge(self.s_v.h, pcw)
                                else:
                                    t.wait_ge(self.s_v.h, nq)
                                ins = None
                                n_total = 3 * NFT
                                i_mm = 0
                                for tap in (1, 0, 2):
                                    sh = tap - 1
                                    lo = max(0, -(o0 + sh))
                                    hi = min(512, S - (o0 + sh))
                                    for kt in range(NFT):
                                        ins = t.matmul(pzX[:, lo:hi], wblk[par][:, 8 * tap + kt, :],
                                                       q_in[:, kt, o0 + sh + lo:o0 + sh + hi],
                                                       start=(i_mm == 0), stop=(i_mm == n_total - 1))
                                        i_mm += 1
                                self.s_mm.inc(ins, 1)
                            self.tn(mmc)
                            n_mmc = self.s_mm.n

                            def evc(v, n_mmc=n_mmc, ot=ot, nch=nch, pzX=pzX, cbcol=cbcol):
                                v.wait_ge(self.s_mm.h, n_mmc)
                                self.s_v.inc(v.tensor_scalar(q3[:, ot, nch * 512:(nch + 1) * 512],
                                                             pzX[:, :], cols[cbcol][:, ot:ot + 1],
                                                             None, ALU.add), 1)
                            self.vc(evc)
                            pzq_cons[bank] = self.s_v.n
                        wb_cons[par] = self.s_mm.n
                    n_q3_done = self.s_v.n
                    for mt in range(8):
                        par = mt % 2
                        wcc = wb_cons[par]

                        def ldp(g, pT_=pT_, par=par, wcc=wcc, mt=mt):
                            if wcc is not None:
                                g.wait_ge(self.s_mm.h, wcc)
                            sv = pT_.rearrange("(a p) m -> p a m", p=128)
                            self.ld_w[par].inc(g.dma_start(out=wblk[par][:, 0:16, :],
                                                           in_=sv[:, :, mt * 128:(mt + 1) * 128]), 16)
                        self.gp(ldp)
                        nwp = self.ld_w[par].n
                        for nch in range(2):
                            bank = 2 + nch
                            pzX = pz[bank]
                            pcw = pzq_cons[bank]

                            if which == 0:
                                tprev = tbc_ev[nch]

                                def tbc_mm(t, mt=mt, nch=nch, g0=g0, tprev=tprev):
                                    if tprev is not None:
                                        t.wait_ge(self.s_sc.h, tprev)
                                    ins = t.matmul(p_bc[:, nch * 512:(nch + 1) * 512],
                                                   self.k["sel_pair_8th"][:, mt * 128:(mt + 1) * 128],
                                                   tau_bf[:, g0 + nch * 512:g0 + (nch + 1) * 512],
                                                   start=True, stop=True)
                                    ins.then_inc(self.s_mm.h, 1)
                                self.s_mm.n += 1
                                self.tn(tbc_mm)
                                n_tbc = self.s_mm.n
                                tcons = tbc_cons[nch]

                                def tbc_ev_f(s, nch=nch, n_tbc=n_tbc, tcons=tcons):
                                    s.wait_ge(self.s_mm.h, n_tbc)
                                    s.wait_ge(self.s_v.h, n_p1_v if tcons is None else tcons)
                                    self.s_sc.inc(s.activation(taubc_sb[:, nch, :],
                                                               p_bc[:, nch * 512:(nch + 1) * 512],
                                                               AF.Copy), 1)
                                self.sc(tbc_ev_f)
                                tbc_ev[nch] = self.s_sc.n

                            def mmp(t, par=par, nwp=nwp, pcw=pcw, g0=g0, nch=nch, pzX=pzX,
                                    nq3=n_q3_done):
                                t.wait_ge(self.ld_w[par].h, nwp)
                                t.wait_ge(self.s_v.h, nq3 if pcw is None else max(pcw, nq3))
                                ins = None
                                for kt in range(NFT):
                                    ins = t.matmul(pzX[:, :], wblk[par][:, kt, :],
                                                   q_in[:, kt, g0 + nch * 512:g0 + (nch + 1) * 512],
                                                   start=(kt == 0), stop=False)
                                for kt in range(NFT):
                                    ins = t.matmul(pzX[:, :], wblk[par][:, 8 + kt, :],
                                                   q3[:, kt, nch * 512:(nch + 1) * 512],
                                                   start=False, stop=(kt == NFT - 1))
                                self.s_mm.inc(ins, 1)
                            self.tn(mmp)
                            n_mmp = self.s_mm.n

                            if which == 0:
                                n_tev = tbc_ev[nch]

                                def evp(v, n_mmp=n_mmp, nch=nch, pzX=pzX, pbcol=pbcol, mt=mt,
                                        n_tev=n_tev):
                                    v.wait_ge(self.s_mm.h, n_mmp)
                                    v.wait_ge(self.s_sc.h, n_tev)
                                    v.wait_ge(self.st_o[nch].h, self.st_o[nch].n)
                                    self.s_v.inc(v.scalar_tensor_tensor(
                                        qstg[:, nch, :], pzX[:, :], cols[pbcol][:, mt:mt + 1],
                                        taubc_sb[:, nch, :], ALU.add, ALU.mult), 1)
                                self.vc(evp)
                                tbc_cons[nch] = self.s_v.n
                            else:
                                def evp(v, n_mmp=n_mmp, nch=nch, pzX=pzX, pbcol=pbcol, mt=mt):
                                    v.wait_ge(self.s_mm.h, n_mmp)
                                    v.wait_ge(self.st_o[nch].h, self.st_o[nch].n)
                                    self.s_v.inc(v.tensor_scalar(qstg[:, nch, :],
                                                                 pzX[:, :], cols[pbcol][:, mt:mt + 1],
                                                                 None, ALU.add), 1)
                                self.vc(evp)
                            n_evp = self.s_v.n
                            pzq_cons[bank] = self.s_v.n

                            def stp(sy, n_evp=n_evp, mt=mt, g0=g0, nch=nch, spill=spill):
                                sy.wait_ge(self.s_v.h, n_evp)
                                self.st_o[nch].inc(sy.dma_start(
                                    out=spill[mt * 128:(mt + 1) * 128, g0 + nch * 512:g0 + (nch + 1) * 512],
                                    in_=qstg[:, nch, :]), 16)
                            self.sy(stp)
                        wb_cons[par] = self.s_mm.n

            def ldwv(sy):
                sy.wait_ge(self.s_mm.h, self.s_mm.n)
                sy.wait_ge(self.s_v.h, self.s_v.n)
                self.ld_s.inc(sy.dma_start(out=wsb[:, :, :],
                                           in_=wvT.rearrange("(a p) m -> p a m", p=128)), 16)
                self.ld_s.inc(sy.dma_start(out=brow[:, :],
                                           in_=self.win["L1_attn_Wv_b"].rearrange("(o d) -> o d", o=1)), 16)
            self.sy(ldwv)
            n_ldv = self.ld_s.n

            def bias_mm(t):
                t.wait_ge(self.ld_s.h, n_ldv)
                t.wait_ge(self.s_sc.h, self.s_sc.n)
                t.wait_ge(self.s_v.h, self.s_v.n)
                t.matmul(p_bc[:, 0:512], self.k["ones_row_f32"][:, :], brow[:, 0:512], start=True, stop=True)
                ins = t.matmul(p_bc[:, 512:1024], self.k["ones_row_f32"][:, :], brow[:, 512:1024],
                               start=True, stop=True)
                self.s_mm.inc(ins, 1)
            self.tn(bias_mm)
            n_bmm = self.s_mm.n

            def bias_ev(v):
                v.wait_ge(self.s_mm.h, n_bmm)
                self.s_v.inc(v.tensor_copy(bias_b[:, :], p_bc[:, :]), 1)
                self.s_v.inc(v.memset(vaug.rearrange("p a (h c) -> p a h c", c=65)[:, :, :, 64:65], 1.0), 1)
            self.vc(bias_ev)
            n_bev = self.s_v.n

            for tt in range(NTT):
                def vmm(t):
                    if tt == 0:
                        t.wait_ge(self.s_v.h, n_bev)
                    t.wait_ge(self.s_v.h, self.s_v.n)
                    ins = None
                    for half in range(2):
                        for kt in range(NFT):
                            ins = t.matmul(pz[half][:, :], n1[:, kt, tt * 128:(tt + 1) * 128],
                                           wsb[:, kt, half * 512:(half + 1) * 512],
                                           start=(kt == 0), stop=(kt == NFT - 1))
                    self.s_mm.inc(ins, 1)
                self.tn(vmm)
                n_vmm = self.s_mm.n

                def vev(v):
                    v.wait_ge(self.s_mm.h, n_vmm)
                    dst = vaug[:, tt, :].rearrange("p (h c) -> p h c", c=65)[:, :, 0:64]
                    src0 = pz[0].rearrange("p (h c) -> p h c", c=64)
                    src1 = pz[1].rearrange("p (h c) -> p h c", c=64)
                    bb0 = bias_b[:, 0:512].rearrange("p (h c) -> p h c", c=64)
                    bb1 = bias_b[:, 512:1024].rearrange("p (h c) -> p h c", c=64)
                    v.tensor_add(dst[:, 0:8, :], src0[:, :, :], bb0[:, :, :])
                    self.s_v.inc(v.tensor_add(dst[:, 8:16, :], src1[:, :, :], bb1[:, :, :]), 1)
                self.vc(vev)

            p_o2 = p_bc[0:65, 512:1024]

            def qk_init(v):
                v.wait_ge(self.s_mm.h, self.s_mm.n)
                v.wait_ge(self.s_sc.h, self.s_sc.n)
                self.s_v.inc(v.memset(q3buf[64:66, 0:4096], 1.0), 1)
            self.vc(qk_init)
            n_qinit = self.s_v.n

            e_done = {i: None for i in range(4)}
            av_cons = {i: None for i in range(4)}
            po_cons_v = self.s_v.n
            for h in range(H):
                def ldqk(sy, h=h):
                    sy.wait_ge(self.s_mm.h, self.s_mm.n)
                    if h == 0:
                        sy.wait_ge(self.s_v.h, self.s_v.n)
                        sy.wait_ge(self.st_s.h, self.st_s.n)
                    sy.wait_ge(self.st_o[0].h, self.st_o[0].n)
                    sy.wait_ge(self.st_o[1].h, self.st_o[1].n)
                    self.ld_x.inc(sy.dma_start(out=qha[0:64, :], in_=self.q_sp[h * 64:(h + 1) * 64, :]), 16)
                    self.ld_x.inc(sy.dma_start(out=kha[0:64, :], in_=self.k_sp[h * 64:(h + 1) * 64, :]), 16)
                    self.ld_x.inc(sy.dma_start(out=kha[64:65, :], in_=self.d_sp[0][h:h + 1, :]), 16)
                    self.ld_x.inc(sy.dma_start(out=kha[65:66, :], in_=self.d_sp[1][h:h + 1, :]), 16)
                self.sy(ldqk)
                n_lqk = self.ld_x.n
                for qc in range(8):
                    t_ops, s_ops = [], []
                    t_dot = {}

                    def emit_av1(kc):
                        slot = kc % 4

                        def op_av(t, kc=kc, slot=slot, w=e_done[slot], h=h, pv=po_cons_v):
                            t.wait_ge(self.s_sc.h, w)
                            if kc == 0:
                                t.wait_ge(self.s_v.h, pv)
                            ins = t.matmul(p_o2[:, :], vaug[:, kc, h * 65:(h + 1) * 65],
                                           fsb[:, slot, :], start=(kc == 0), stop=(kc == NTT - 1))
                            ins.then_inc(self.s_mm.h, 1)
                        self.s_mm.n += 1
                        t_ops.append(op_av)
                        av_cons[slot] = self.s_mm.n

                    for kc in range(NTT):
                        slot = kc % 4
                        pd = pz[slot]

                        def op_dot(t, kc=kc, pd=pd, w=e_done[kc % 4], qc=qc, n_lqk=n_lqk):
                            if w is not None:
                                t.wait_ge(self.s_sc.h, w)
                            if kc == 0 and qc == 0:
                                t.wait_ge(self.ld_x.h, n_lqk)
                            ins = t.matmul(pd[:, :], kha[:, kc * 128:(kc + 1) * 128],
                                           qha[:, qc * 512:(qc + 1) * 512], start=True, stop=True)
                            ins.then_inc(self.s_mm.h, 1)
                        self.s_mm.n += 1
                        t_ops.append(op_dot)
                        t_dot[kc] = self.s_mm.n

                        def op_e(s, kc=kc, slot=slot, pd=pd, wmm=t_dot[kc], wav=av_cons[slot]):
                            s.wait_ge(self.s_mm.h, wmm if wav is None else max(wmm, wav))
                            ins = s.activation(fsb[:, slot, :], pd[:, :], AF.Exp)
                            ins.then_inc(self.s_sc.h, 1)
                        self.s_sc.n += 1
                        s_ops.append(op_e)
                        e_done[slot] = self.s_sc.n

                        if kc >= 3:
                            emit_av1(kc - 3)
                    for kc in (NTT - 3, NTT - 2, NTT - 1):
                        emit_av1(kc)
                    n_av = self.s_mm.n

                    self.tn(lambda t, ops=t_ops: [f(t) for f in ops])
                    self.sc(lambda s, ops=s_ops: [f(s) for f in ops])

                    def v_dn(v, n_av=n_av):
                        v.wait_ge(self.s_mm.h, n_av)
                        self.s_v.inc(v.reciprocal(rdb[:, :], p_o2[64:65, :]), 1)
                    self.vc(v_dn)
                    n_dn = self.s_v.n

                    def bcd(t, n_dn=n_dn):
                        t.wait_ge(self.s_v.h, n_dn)
                        self.s_mm.inc(t.matmul(p_bc[:, 0:512], self.k["ones_row_f32"][:, :], rdb[:, :],
                                               start=True, stop=True), 1)
                    self.tn(bcd)
                    n_bcd = self.s_mm.n

                    def s_oev(s, n_av=n_av):
                        s.wait_ge(self.s_mm.h, n_av)
                        s.wait_ge(self.s_v.h, self.s_v.n)
                        self.s_sc.inc(s.activation(osb[:, :], p_o2[0:64, :], AF.Copy), 1)
                    self.sc(s_oev)
                    n_oev = self.s_sc.n

                    def v_onorm(v, n_bcd=n_bcd, n_oev=n_oev, qc=qc):
                        v.wait_ge(self.s_mm.h, n_bcd)
                        v.wait_ge(self.s_sc.h, n_oev)
                        v.wait_ge(self.st_o[qc % 2].h, self.st_o[qc % 2].n)
                        self.s_v.inc(v.tensor_mul(obuf[:, qc % 2, :], osb[:, :], p_bc[0:64, 0:512]), 1)
                    self.vc(v_onorm)
                    n_on = self.s_v.n
                    po_cons_v = self.s_v.n

                    def stob(sy, n_on=n_on, h=h, qc=qc):
                        sy.wait_ge(self.s_v.h, n_on)
                        self.st_o[qc % 2].inc(sy.dma_start(
                            out=self.attn_sp[h * 64:(h + 1) * 64, qc * 512:(qc + 1) * 512],
                            in_=obuf[:, qc % 2, :]), 16)
                    self.sy(stob)

            def ldwo(sy):
                sy.wait_ge(self.s_mm.h, self.s_mm.n)
                self.ld_s.inc(sy.dma_start(out=wsb[:, :, :],
                                           in_=woT.rearrange("(a p) m -> p a m", p=128)), 16)
            self.sy(ldwo)
            n_lwo = self.ld_s.n
            attc = q3v
            for c in range(S // T):
                def lda(sy):
                    sy.wait_ge(self.s_v.h, self.s_v.n)
                    sy.wait_ge(self.s_sc.h, self.s_sc.n)
                    sy.wait_ge(self.s_mm.h, self.s_mm.n)
                    ar = self.attn_sp.rearrange("(a p) t -> a p t", p=128)
                    xr = xin.rearrange("(a p) t -> a p t", p=128)
                    for ft in range(NFT):
                        self.ld_x.inc(sy.dma_start(out=attc[:, ft, :], in_=ar[ft, :, c * T:(c + 1) * T]), 16)
                    for ft in range(NFT):
                        self.ld_x.inc(sy.dma_start(out=xs[:, ft, :], in_=xr[ft, :, c * T:(c + 1) * T]), 16)
                self.sy(lda)
                n_la = self.ld_x.n
                for dt in range(8):
                    for nch in range(2):
                        pzX = pz[2 * (dt % 2) + nch]

                        def omm(t):
                            t.wait_ge(self.ld_x.h, n_la)
                            t.wait_ge(self.ld_s.h, n_lwo)
                            t.wait_ge(self.s_v.h, self.s_v.n)
                            ins = None
                            for kt in range(NFT):
                                ins = t.matmul(pzX[:, :], wsb[:, kt, dt * 128:(dt + 1) * 128],
                                               attc[:, kt, nch * 512:(nch + 1) * 512],
                                               start=(kt == 0), stop=(kt == NFT - 1))
                            self.s_mm.inc(ins, 1)
                        self.tn(omm)
                        n_omm = self.s_mm.n

                        def oadd(v):
                            v.wait_ge(self.s_mm.h, n_omm)
                            v.wait_ge(self.st_o[nch].h, self.st_o[nch].n)
                            self.s_v.inc(v.scalar_tensor_tensor(
                                ostg[:, nch, :], pzX[:, :], cols["ob"][:, dt:dt + 1],
                                xs[:, dt, nch * 512:(nch + 1) * 512], ALU.add, ALU.add), 1)
                        self.vc(oadd)
                        n_oadd = self.s_v.n

                        def ost(sy):
                            sy.wait_ge(self.s_v.h, n_oadd)
                            self.st_o[nch].inc(sy.dma_start(
                                out=xout[dt * 128:(dt + 1) * 128,
                                         c * T + nch * 512:c * T + (nch + 1) * 512],
                                in_=ostg[:, nch, :]), 16)
                        self.sy(ost)

    def _emit_output(self, xcur):
        with ExitStack() as st:
            xt = self.sb(st, "fo_xt", [128, NFT, 128], F32)
            ot = self.sb(st, "fo_ot", [128, D], F32)
            pft = self.psum(st, "fo_p1", [128, 512], F32)
            pft2 = self.psum(st, "fo_p2", [128, 512], F32)
            xr = xcur.rearrange("(a p) t -> p a t", p=128)
            for tt in range(NTT):
                def ld(sy):
                    sy.wait_ge(self.s_mm.h, self.s_mm.n)
                    self.ld_s.inc(sy.dma_start(out=xt[:, :, :], in_=xr[:, :, tt * 128:(tt + 1) * 128]), 16)
                self.sy(ld)
                n0 = self.ld_s.n

                def tr(t):
                    t.wait_ge(self.ld_s.h, n0)
                    t.wait_ge(self.s_v.h, self.s_v.n)
                    ins = None
                    for ft in range(NFT):
                        p = pft if ft < 4 else pft2
                        ins = t.transpose(p[:, (ft % 4) * 128:(ft % 4 + 1) * 128], xt[:, ft, :],
                                          self.k["id_f32"][:])
                    self.s_mm.inc(ins, 1)
                self.tn(tr)
                n1 = self.s_mm.n

                def ev(v):
                    v.wait_ge(self.s_mm.h, n1)
                    v.wait_ge(self.st_s.h, self.st_s.n)
                    v.tensor_copy(ot[:, 0:512], pft[:, :])
                    self.s_v.inc(v.tensor_copy(ot[:, 512:1024], pft2[:, :]), 1)
                self.vc(ev)
                nv = self.s_v.n

                def stv(sy):
                    sy.wait_ge(self.s_v.h, nv)
                    self.st_s.inc(sy.dma_start(out=self.y[tt * 128:(tt + 1) * 128, :], in_=ot[:, :]), 16)
                self.sy(stv)


def kernel(**inputs):
    P = Prog()
    x = np.ascontiguousarray(inputs["L0_attn_x"], dtype=np.float32)
    in_maps = []
    for b in range(8):
        m = {"x0": np.ascontiguousarray(x[b])}
        for name in P.win:
            m[name] = np.ascontiguousarray(inputs[name], dtype=np.float32)
        in_maps.append(m)
    res = run_bass_kernel_spmd(P.nc, in_maps, core_ids=list(range(8)))
    return np.stack([r["y"] for r in res.results], axis=0)
